# version 02
# baseline (speedup 1.0000x reference)
import contextlib
import numpy as np
import ml_dtypes
import concourse.bass as bass
import concourse.mybir as mybir
from concourse.bass_utils import run_bass_kernel_spmd

F32 = mybir.dt.float32
BF16 = mybir.dt.bfloat16
I32 = mybir.dt.int32
AF = mybir.ActivationFunctionType
ALU = mybir.AluOpType
AX = mybir.AxisListType
BF = ml_dtypes.bfloat16

T = 2048
D = 1024
KC = 8
TS = 512
NSL = 4
NCORES = 8
CH = 64
NCH = T // CH

DEBUG_STAGE = 0
DBG_TAGS = set()
CC_RANKS = None

ESEM = {"tensor": "t", "vector": "v", "scalar": "s", "gpsimd": "g"}


class KB:
    ENGINES = ("sync", "gpsimd", "scalar", "vector", "tensor")

    def __init__(self, nc, sems):
        self.nc = nc
        self.sems = sems
        self.cnt = {n: 0 for n in sems}
        self.ops = {e: [] for e in self.ENGINES}
        self.waited = {e: {} for e in self.ENGINES}

    def op(self, eng, fn, inc=None):
        if inc is not None:
            sem, n = inc
            self.cnt[sem] += n
            val = self.cnt[sem]
            h = self.sems[sem]

            def wrapped(e, fn=fn, h=h, n=n):
                fn(e).then_inc(h, n)
            self.ops[eng].append(wrapped)
            return (sem, val)
        self.ops[eng].append(fn)
        return None

    def sig(self, eng, fn):
        return self.op(eng, fn, inc=(ESEM[eng], 1))

    def wait(self, eng, tok):
        if tok is None:
            return
        sem, val = tok
        if self.waited[eng].get(sem, 0) >= val:
            return
        self.waited[eng][sem] = val
        h = self.sems[sem]
        self.ops[eng].append(lambda e, h=h, val=val: e.wait_ge(h, val))

    def waits(self, eng, toks):
        for t in toks:
            self.wait(eng, t)

    def dma(self, queue, out, in_, waits=(), slow=False):
        sem = {"sync": "dq", "gpsimd": "dg", "scalar": "da"}[queue]
        self.waits(queue, waits)
        return self.op(queue,
                       lambda e, o=out, i=in_: e.dma_start(
                           out=o, in_=i, allow_slow_non_contiguous=slow),
                       inc=(sem, 16))

    def tcnt(self):
        return ("t", self.cnt["t"])

    def vcnt(self):
        return ("v", self.cnt["v"])

    def scnt(self):
        return ("s", self.cnt["s"])


class Bank:
    __slots__ = ("ap", "free")

    def __init__(self, ap):
        self.ap = ap
        self.free = None


class Arena:
    def __init__(self, bf_ap, nbytes):
        self.bf = bf_ap
        self.f32 = bf_ap.bitcast(F32)
        self.nbytes = nbytes

    def b(self, off, rows, cols):
        assert off % 2 == 0 and off + 2 * cols <= self.nbytes, (off, cols)
        return self.bf[0:rows, off // 2: off // 2 + cols]

    def b_at(self, off, r0, rows, cols):
        assert off % 2 == 0 and off + 2 * cols <= self.nbytes, (off, cols)
        return self.bf[r0:r0 + rows, off // 2: off // 2 + cols]

    def f(self, off, rows, cols):
        assert off % 4 == 0 and off + 4 * cols <= self.nbytes, (off, cols)
        return self.f32[0:rows, off // 4: off // 4 + cols]


def _host_constants():
    c = {}
    c["c_ident_bf"] = np.eye(128, dtype=np.float32).astype(BF)
    c["c_onesrow_bf"] = np.ones((1, 128), dtype=np.float32).astype(BF)
    c["c_onescol_bf"] = np.ones((128, 1), dtype=np.float32).astype(BF)
    sel = np.zeros((8, 4 * 128), dtype=np.float32)
    for cc in range(4):
        for p in range(128):
            sel[cc * 2 + (p // 64), cc * 128 + p] = 1.0
    c["c_headsel_bf"] = np.ascontiguousarray(sel.astype(BF))
    bo8 = np.zeros((128, 32), dtype=np.float32)
    for cc in range(4):
        for p in range(128):
            bo8[p, cc * 8 + 2 * cc + (p // 64)] = 1.0
    c["c_bo8_bf"] = np.ascontiguousarray(bo8.astype(BF))
    lt = np.tril(np.ones((CH, CH), dtype=np.float32), -1)
    ut = np.triu(np.ones((CH, CH), dtype=np.float32), 1)
    ui = np.triu(np.ones((CH, CH), dtype=np.float32), 0)
    c["c_ltmask"] = np.ascontiguousarray(np.concatenate([lt, lt], axis=0))
    c["c_utmask"] = np.ascontiguousarray(np.concatenate([ut, ut], axis=0))
    c["c_uimask"] = np.ascontiguousarray(np.concatenate([ui, ui], axis=0))
    c["c_istack_bf"] = np.ascontiguousarray(
        np.concatenate([np.eye(CH, dtype=np.float32)] * 2, axis=0)).astype(BF)
    DL1 = 64
    inv_freq = 1.0 / (10000.0 ** (np.arange(0, DL1, 2, dtype=np.float32) / DL1))
    bd = np.zeros((8, 128, 128), dtype=np.float32)
    for ck in range(8):
        for hh in range(2):
            ang = (ck * 2 + hh) * inv_freq
            co, si = np.cos(ang), np.sin(ang)
            for j in range(32):
                b = hh * 64 + 2 * j
                bd[ck, b, b] = co[j]
                bd[ck, b + 1, b] = -si[j]
                bd[ck, b, b + 1] = si[j]
                bd[ck, b + 1, b + 1] = co[j]
    c["c_rope_bd_bf"] = np.ascontiguousarray(bd.reshape(8 * 128, 128)).astype(BF)
    c["c_onesrow_f"] = np.ones((1, 128), dtype=np.float32)
    c["c_ident_f"] = np.eye(128, dtype=np.float32)
    return c


class St:
    pass


def dbg_dump(st, tag, ap, toks=()):
    if tag not in DBG_TAGS:
        return
    k = st.k
    name = "dbg_" + tag
    dt = ap.dtype
    sh = list(ap.shape)
    dram = st.nc.dram_tensor(name, sh, dt, kind="ExternalOutput").ap()
    k.waits("sync", list(toks))
    for eng in ("tensor", "vector", "scalar", "gpsimd"):
        sem = ESEM[eng]
        k.wait("sync", (sem, k.cnt[sem]))
    tok = k.dma("sync", dram, ap)
    for eng in ("tensor", "vector", "scalar", "gpsimd"):
        k.wait(eng, tok)
    st.dbg_toks.append(tok)


def sv(st, name, col=None, rows=None, ncols=1):
    ap = st.small[name]
    if col is not None:
        ap = ap[:, col:col + ncols]
    if rows is not None:
        ap = ap[0:rows, :]
    return ap


def crow(st, r, ncols=T, rows=1):
    return st.scr[r:r + rows, 0:ncols]


def mm_raw(st, bank, mms, pre_waits=()):
    k = st.k
    k.wait("tensor", bank.free)
    k.waits("tensor", pre_waits)
    n = len(mms)
    tok = None
    for i, fn in enumerate(mms):
        last = i == n - 1
        tok = k.op("tensor",
                   lambda e, fn=fn, s=(i == 0), p=last: fn(e, s, p),
                   inc=(("t", 1) if last else None))
    return tok


def mm_group(st, bank, mms, consumer_eng, consumer_fn, pre_waits=(),
             consumer_waits=()):
    k = st.k
    tok = mm_raw(st, bank, mms, pre_waits)
    k.wait(consumer_eng, tok)
    k.waits(consumer_eng, consumer_waits)
    ctok = k.sig(consumer_eng, consumer_fn)
    bank.free = ctok
    return ctok


def stack_evac(st, bank, dst, mmtok):
    k = st.k
    k.wait("scalar", mmtok)
    k.sig("scalar", lambda e: e.copy(dst[:, 0:64], bank.ap[0:64, 0:64]))
    t2 = k.sig("scalar", lambda e: e.copy(dst[:, 64:128], bank.ap[64:128, 64:128]))
    bank.free = t2
    return t2


def wload_bf(st, dram, col0, ncols, arena_off, waits=(), kc_rows=None):
    nrows = dram.shape[0]
    if nrows < 128:
        dst = st.ar.b_at(arena_off, 0, nrows, ncols)
        return st.k.dma("gpsimd", dst, dram[:, col0:col0 + ncols], waits=waits)
    if kc_rows is None:
        kc_rows = nrows // 128
    dst = st.ar.b(arena_off, 128, kc_rows * ncols)
    src = dram.rearrange("(c p) o -> p c o", p=128)[:, :, col0:col0 + ncols]
    return st.k.dma("gpsimd", dst.rearrange("p (c o) -> p c o", o=ncols), src,
                    waits=waits)


def emit_proj(st, wdram, in_at, n_in, out_cols, evac, wslot, wslot_sz,
              in_toks=(), mcg=4, bank_ids=(4, 5), w_pre=None, nsl=NSL,
              tslice=TS, tok0=0, w_waits=()):
    k = st.k
    if not hasattr(st, "wslot_readers"):
        st.wslot_readers = {}
    ngroups = (out_cols + mcg * 128 - 1) // (mcg * 128)
    gsz = n_in * mcg * 128
    assert 2 * 2 * gsz <= wslot_sz, (gsz, wslot_sz)
    wread = [st.wslot_readers.get((wslot, 0)), st.wslot_readers.get((wslot, 1))]
    for g in range(ngroups):
        col0 = g * mcg * 128
        ncols = min(mcg * 128, out_cols - col0)
        slot = g % 2
        aoff = wslot + slot * 2 * gsz
        wt = wload_bf(st, wdram, col0, ncols, aoff,
                      waits=([wread[slot]] if wread[slot] else []) +
                      list(w_waits))
        wap = st.ar.b(aoff, 128, n_in * ncols)
        if w_pre is not None:
            wt = w_pre(col0, ncols, wap, wt)
        nmc = (ncols + 127) // 128
        for mc in range(nmc):
            mco = col0 // 128 + mc
            for s in range(nsl):
                bank = st.pb[bank_ids[(mc * nsl + s) % len(bank_ids)]]
                mms = []
                for kc in range(n_in):
                    def mk(kc=kc, wap=wap, bank=bank, ncols=ncols, mc=mc, s=s):
                        def f(e, st_, sp):
                            return e.matmul(
                                bank.ap[:, 0:tslice],
                                wap[:, kc * ncols + mc * 128:
                                    kc * ncols + mc * 128 + 128],
                                in_at(kc)[:, tok0 + s * tslice:
                                          tok0 + (s + 1) * tslice],
                                start=st_, stop=sp)
                        return f
                    mms.append(mk())
                mmtok = mm_raw(st, bank, mms,
                               pre_waits=[wt] + list(in_toks))
                evac(mco, s, bank, mmtok)
        wread[slot] = k.tcnt()
        st.wslot_readers[(wslot, slot)] = wread[slot]
    return k.tcnt()


def emit_norm(st, src_at, out_bf_at, kind, eps, tr_off, bc_off,
              gain_at=None, bias_at=None, src_toks=(), also_f32_at=None):
    k = st.k
    ar = st.ar
    eps_ap = st.EPS5 if eps == 1e-5 else st.EPS6
    tr = ar.b(tr_off, 128, TS)
    rowb = ar.b(tr_off + 2 * TS, 1, 2 * TS)
    a_bc = ar.f(bc_off, 128, T)
    need_mean = kind == "ln"
    if need_mean:
        b_bc = ar.f(bc_off + 4 * T, 128, T)
        tmp = ar.f(bc_off + 8 * T, 128, T)
    else:
        b_bc = None
        tmp = ar.f(bc_off + 4 * T, 128, T)
    sum_r = st.nr_sum
    ssq_r = st.nr_ssq
    a_r = st.nr_a
    t_r = st.nr_t
    k.waits("scalar", list(src_toks))
    bc_evs = []
    for s in range(NSL):
        cs = slice(s * TS, (s + 1) * TS)
        passes = ((("sum", AF.Copy, st.pb[0], sum_r),) if need_mean else ()) + \
                 (("ssq", AF.Square, st.pb[1], ssq_r),)
        for nm, fn, bank, row in passes:
            for c in range(KC):
                k.wait("scalar", k.tcnt())
                ct = k.sig("scalar", lambda e, c=c, fn=fn, cs=cs: e.activation(
                    tr, src_at(c)[:, cs], fn))
                if c == 0:
                    k.wait("tensor", bank.free)
                k.wait("tensor", ct)
                k.op("tensor", lambda e, b=bank, c=c: e.matmul(
                    b.ap[0:1, 0:TS], st.CONEC, tr,
                    start=(c == 0), stop=(c == KC - 1)),
                    inc=("t", 1))
            mmtok = k.tcnt()
            k.wait("scalar", mmtok)
            ev = k.sig("scalar", lambda e, b=bank, row=row: e.copy(
                row, b.ap[0:1, 0:TS]))
            bank.free = ev
            k.wait("vector", ev)
        if need_mean:
            k.sig("vector", lambda e: e.tensor_scalar(
                out=sum_r, in0=sum_r, scalar1=1.0 / D, scalar2=None,
                op0=ALU.mult))
            k.sig("vector", lambda e: e.tensor_tensor(
                out=t_r, in0=sum_r, in1=sum_r, op=ALU.mult))
            vt = k.sig("vector", lambda e: e.scalar_tensor_tensor(
                out=t_r, in0=ssq_r, scalar=1.0 / D, in1=t_r,
                op0=ALU.mult, op1=ALU.subtract))
        else:
            vt = k.sig("vector", lambda e: e.tensor_scalar(
                out=t_r, in0=ssq_r, scalar1=1.0 / D, scalar2=None,
                op0=ALU.mult))
        k.wait("scalar", vt)
        k.sig("scalar", lambda e: e.activation(
            t_r, t_r, AF.Ln, bias=eps_ap[0:1, :], scale=1.0))
        sq = k.sig("scalar", lambda e: e.activation(
            a_r, t_r, AF.Exp, scale=-0.5))
        k.wait("vector", sq)
        k.wait("vector", k.tcnt())
        rva = k.sig("vector", lambda e: e.tensor_copy(rowb[0:1, 0:TS], a_r))
        rvb = None
        if need_mean:
            k.sig("vector", lambda e: e.scalar_tensor_tensor(
                out=t_r, in0=sum_r, scalar=-1.0, in1=a_r,
                op0=ALU.mult, op1=ALU.mult))
            rvb = k.sig("vector", lambda e: e.tensor_copy(
                rowb[0:1, TS:2 * TS], t_r))
        srcs = [(0, a_bc, rva)] + ([(TS, b_bc, rvb)] if need_mean else [])
        for (coff, dstbc, rtok) in srcs:
            bank = st.pb[2]
            k.wait("tensor", bank.free)
            k.wait("tensor", rtok)
            mt = k.op("tensor", lambda e, b=bank, coff=coff: e.matmul(
                b.ap[:, 0:TS], st.CONER, rowb[0:1, coff:coff + TS],
                start=True, stop=True), inc=("t", 1))
            k.wait("scalar", mt)
            ev = k.sig("scalar", lambda e, b=bank, cs=cs, dstbc=dstbc: e.copy(
                dstbc[:, cs], b.ap[:, 0:TS]))
            bank.free = ev
            bc_evs.append(ev)
    out_toks = []
    k.waits("vector", bc_evs)
    for c in range(KC):
        if need_mean:
            k.sig("vector", lambda e, c=c: e.tensor_tensor(
                out=tmp, in0=src_at(c), in1=a_bc, op=ALU.mult))
            if gain_at is None:
                tk = k.sig("vector", lambda e, c=c: e.tensor_tensor(
                    out=out_bf_at(c), in0=tmp, in1=b_bc, op=ALU.add))
            else:
                k.sig("vector", lambda e, c=c: e.tensor_tensor(
                    out=tmp, in0=tmp, in1=b_bc, op=ALU.add))
                tk = k.sig("vector", lambda e, c=c: e.tensor_scalar(
                    out=out_bf_at(c), in0=tmp, scalar1=gain_at(c),
                    scalar2=(bias_at(c) if bias_at else None),
                    op0=ALU.mult,
                    op1=(ALU.add if bias_at else Ellipsis)))
        else:
            if gain_at is None:
                tk = k.sig("vector", lambda e, c=c: e.tensor_tensor(
                    out=out_bf_at(c), in0=src_at(c), in1=a_bc, op=ALU.mult))
            else:
                k.sig("vector", lambda e, c=c: e.tensor_tensor(
                    out=tmp, in0=src_at(c), in1=a_bc, op=ALU.mult))
                tk = k.sig("vector", lambda e, c=c: e.tensor_scalar(
                    out=out_bf_at(c), in0=tmp, scalar1=gain_at(c),
                    scalar2=None, op0=ALU.mult))
        out_toks.append(tk)
    return out_toks


O_A = 0
O_B = 32768
O_C = 65536
O_CS = 98304
AR_BYTES = 112 * 1024

O_LN0 = O_A
O_H = O_B
O_GATE = O_C
OQ = O_A
OKK = O_A + 16384
OAB = O_B
OSPK = O_B + 16384
OV = O_C
ODR = O_C + 16384


def qp(st, name, cc):
    base = {"q": OQ, "k": OKK, "v": OV, "dr": ODR, "ab": OAB, "spk": OSPK}[name]
    return st.ar.b(base + cc * 4096, 128, T)


def a_ln0(st, c):
    return st.ar.b(O_LN0 + c * 4096, 128, T)


def a_h(st, c):
    return st.ar.b(O_H + c * 4096, 128, T)


def a_gate(st, c):
    return st.ar.b(O_GATE + c * 4096, 128, T)


def emit_l0_attn(st):
    k = st.k

    ln_toks = emit_norm(
        st, lambda c: st.R[:, c * T:(c + 1) * T], lambda c: a_ln0(st, c),
        "ln", 1e-5, tr_off=O_B, bc_off=O_C,
        src_toks=[t for t in st.r_tok if t])
    for c in range(KC):
        dbg_dump(st, f"ln0_{c}", a_ln0(st, c), [ln_toks[c]])
    emit_rms_conv_h(st, ln_toks)
    for c in range(KC):
        dbg_dump(st, f"h_{c}", a_h(st, c), [st.h_tok[c]])
    gtoks = emit_gate(st)
    for c in range(KC):
        dbg_dump(st, f"gate_{c}", a_gate(st, c), [gtoks[c]])
    dbg_dump(st, "gate_ps", a_gate(st, 0), [gtoks[0]])
    sp = []
    for c in range(KC):
        sp.append(k.dma("sync", st.spill_ln0[c * 128:(c + 1) * 128, :],
                        a_ln0(st, c), waits=[ln_toks[c]]))
        sp.append(k.dma("sync", st.spill_gate[c * 128:(c + 1) * 128, :],
                        a_gate(st, c), waits=[gtoks[c]]))
    st.l0_spill_toks = sp
    emit_l0_projections(st)
    for cc in range(4):
        dbg_dump(st, f"q_{cc}", qp(st, "q", cc), [st.qk_tok])
        dbg_dump(st, f"k_{cc}", qp(st, "k", cc), [st.qk_tok])
        dbg_dump(st, f"v_{cc}", qp(st, "v", cc), [st.v_tok])
        dbg_dump(st, f"drive_{cc}", qp(st, "dr", cc), [st.drive_tok])
        dbg_dump(st, f"ab_{cc}", qp(st, "ab", cc), [st.ab_tok])
    emit_l0_scan(st)
    for cc in range(4):
        dbg_dump(st, f"spk_{cc}", qp(st, "spk", cc), [st.scan_tok])
    emit_l0_decay(st)
    dbg_dump(st, "beta", st.ar.b_at(O_CS + 12288, 0, 8, T), [st.decay_tok])
    dbg_dump(st, "pc", st.misc[:, 8:136], [st.decay_tok])
    for cc in range(4):
        dbg_dump(st, f"P_{cc}", qp(st, "dr", cc), [st.decay_tok])
        dbg_dump(st, f"Pinv_{cc}", qp(st, "spk", cc), [st.decay_tok])
        dbg_dump(st, f"kap_{cc}", qp(st, "ab", cc), [st.decay_tok])
        dbg_dump(st, f"kt_{cc}", qp(st, "k", cc), [st.decay_tok])
        dbg_dump(st, f"qt_{cc}", qp(st, "q", cc), [st.decay_tok])
    emit_l0_deltanet(st)
    for cc in range(4):
        dbg_dump(st, f"yfm_{cc}", st.ar.b(OQ + cc * 4096, 128, T),
                 [st.y_tok])
    emit_l0_tail(st)


def emit_rms_conv_h(st, ln_toks):
    k = st.k
    ar = st.ar
    ssq_r, a_r, t_r = st.nr_ssq, st.nr_a, st.nr_t
    tr = ar.b(O_CS, 128, TS)
    rowb = ar.b(O_CS + 1024, 1, TS)
    a_bc = ar.f(O_CS + 4096, 128, T)
    k.waits("scalar", ln_toks)
    bc_evs = []
    for s in range(NSL):
        cs = slice(s * TS, (s + 1) * TS)
        bank = st.pb[1]
        for c in range(KC):
            k.wait("scalar", k.tcnt())
            ct = k.sig("scalar", lambda e, c=c, cs=cs: e.activation(
                tr, a_ln0(st, c)[:, cs], AF.Square))
            if c == 0:
                k.wait("tensor", bank.free)
            k.wait("tensor", ct)
            k.op("tensor", lambda e, b=bank, c=c: e.matmul(
                b.ap[0:1, 0:TS], st.CONEC, tr,
                start=(c == 0), stop=(c == KC - 1)),
                inc=("t", 1))
        mmtok = k.tcnt()
        k.wait("scalar", mmtok)
        ev = k.sig("scalar", lambda e, b=bank: e.copy(ssq_r, b.ap[0:1, 0:TS]))
        bank.free = ev
        k.wait("vector", ev)
        vt = k.sig("vector", lambda e: e.tensor_scalar(
            out=t_r, in0=ssq_r, scalar1=1.0 / D, scalar2=None, op0=ALU.mult))
        k.wait("scalar", vt)
        k.sig("scalar", lambda e: e.activation(
            t_r, t_r, AF.Ln, bias=st.EPS6[0:1, :], scale=1.0))
        sq = k.sig("scalar", lambda e: e.activation(
            a_r, t_r, AF.Exp, scale=-0.5))
        k.wait("vector", sq)
        k.wait("vector", k.tcnt())
        rva = k.sig("vector", lambda e: e.tensor_copy(rowb, a_r))
        bank = st.pb[2]
        k.wait("tensor", bank.free)
        k.wait("tensor", rva)
        mt = k.op("tensor", lambda e, b=bank: e.matmul(
            b.ap[:, 0:TS], st.CONER, rowb, start=True, stop=True),
            inc=("t", 1))
        k.wait("scalar", mt)
        ev = k.sig("scalar", lambda e, b=bank, cs=cs: e.copy(
            a_bc[:, cs], b.ap[:, 0:TS]))
        bank.free = ev
        bc_evs.append(ev)
    cs1 = ar.f(O_C, 128, T + 3)
    cs2 = ar.f(O_C + 4 * (T + 4), 128, T)
    k.sig("vector", lambda e: e.memset(cs1[:, 0:3], 0.0))
    k.waits("vector", bc_evs)
    dbg_dump(st, "rmsabc", a_bc, bc_evs)
    st.h_tok = []
    for c in range(KC):
        if st.h_tok:
            k.wait("vector", st.h_tok[-1])
        k.sig("vector", lambda e, c=c: e.tensor_tensor(
            out=cs2, in0=a_ln0(st, c), in1=a_bc, op=ALU.mult))
        k.sig("vector", lambda e, c=c: e.tensor_scalar(
            out=cs1[:, 3:3 + T], in0=cs2,
            scalar1=sv(st, "L0_attn_norm_in_w", col=c), scalar2=None,
            op0=ALU.mult))
        k.sig("vector", lambda e, c=c: e.tensor_scalar(
            out=cs2, in0=cs1[:, 0:T],
            scalar1=sv(st, "L0_attn_conv_w", col=4 * c), scalar2=None,
            op0=ALU.mult))
        for j in range(1, 4):
            k.sig("vector", lambda e, c=c, j=j: e.scalar_tensor_tensor(
                out=cs2, in0=cs1[:, j:j + T],
                scalar=sv(st, "L0_attn_conv_w", col=4 * c + j), in1=cs2,
                op0=ALU.mult, op1=ALU.add))
        vt2 = k.vcnt()
        if c == 0:
            dbg_dump(st, "cs1_0", cs1, [vt2])
            dbg_dump(st, "cs2_0", cs2, [vt2])
        k.wait("scalar", vt2)
        ht = k.sig("scalar", lambda e, c=c: e.activation(
            a_h(st, c), cs2, AF.Silu,
            bias=sv(st, "L0_attn_conv_b", col=c), scale=1.0))
        st.h_tok.append(ht)


def emit_gate(st):
    k = st.k
    ar = st.ar
    dr = st.dr
    wu1 = wload_bf(st, dr["L0_attn_Wu1"], 0, 64, O_CS + 2048,
                   waits=[st.h_tok[-1]])
    wu2 = wload_bf(st, dr["L0_attn_Wu2"], 0, D, O_CS + 4096, kc_rows=1,
                   waits=[st.h_tok[-1]])
    u = ar.b(O_CS + 8192, 64, T)
    for s in range(NSL):
        bank = st.pb[4]
        mms = []
        for kc in range(KC):
            def mk(kc=kc, s=s, bank=bank):
                def f(e, st_, sp):
                    return e.matmul(bank.ap[0:64, 0:TS],
                                    ar.b(O_CS + 2048 + kc * 128, 128, 64),
                                    a_ln0(st, kc)[:, s * TS:(s + 1) * TS],
                                    start=st_, stop=sp)
                return f
            mms.append(mk())
        mm_group(st, bank, mms, "scalar",
                 lambda e, s=s, bank=bank: e.activation(
                     u[:, s * TS:(s + 1) * TS], bank.ap[0:64, 0:TS], AF.Silu,
                     bias=sv(st, "L0_attn_bu1", rows=64), scale=1.0),
                 pre_waits=[wu1, st.h_tok[-1]])
    utok = k.scnt()
    gtoks = []
    for c in range(KC):
        tok = None
        for s in range(NSL):
            bank = st.pb[5]
            def mk(c=c, s=s, bank=bank):
                def f(e, st_, sp):
                    return e.matmul(
                        bank.ap[:, 0:TS],
                        ar.b(O_CS + 4096, 64, D)[:, c * 128:(c + 1) * 128],
                        u[:, s * TS:(s + 1) * TS], start=st_, stop=sp)
                return f
            tok = mm_group(st, bank, [mk()], "scalar",
                           lambda e, c=c, s=s, bank=bank: e.activation(
                               a_gate(st, c)[:, s * TS:(s + 1) * TS],
                               bank.ap[:, 0:TS], AF.Sigmoid,
                               bias=sv(st, "L0_attn_bu2", col=c), scale=1.0),
                           pre_waits=[wu2, utok])
        gtoks.append(tok)
    return gtoks


def emit_l0_projections(st):
    k = st.k
    ar = st.ar
    dr = st.dr
    h_at = lambda kc: a_h(st, kc)
    h_toks = list(st.h_tok)
    k.waits("gpsimd", st.l0_spill_toks)

    def evac_to(name):
        def evac(mco, s, bank, mmtok):
            k.wait("scalar", mmtok)
            ev = k.sig("scalar", lambda e, mco=mco, s=s, bank=bank: e.copy(
                qp(st, name, mco)[:, s * TS:(s + 1) * TS], bank.ap[:, 0:TS]))
            bank.free = ev
        return evac

    WSL = O_CS
    emit_proj(st, dr["L0_attn_Wspike"], h_at, KC, 512, evac_to("dr"),
              WSL, 16384, in_toks=h_toks)
    st.drive_tok = k.scnt()
    emit_proj(st, dr["L0_attn_Wq"], h_at, KC, 512, evac_to("q"),
              WSL, 16384, in_toks=h_toks)
    st.q_tok = k.scnt()
    emit_proj(st, dr["L0_attn_Wk"], h_at, KC, 512, evac_to("k"),
              WSL, 16384, in_toks=h_toks)
    st.k_tok = k.scnt()
    emit_proj(st, dr["L0_attn_Wv"], h_at, KC, 512, evac_to("v"),
              WSL, 16384, in_toks=h_toks)
    st.v_tok = k.scnt()

    wau = wload_bf(st, dr["L0_attn_Wau"], 0, 64, WSL, waits=[k.tcnt()])
    wad = wload_bf(st, dr["L0_attn_Wad"], 0, 512, WSL + 1024, kc_rows=1,
                   waits=[k.tcnt()])
    wbeta = wload_bf(st, dr["L0_attn_Wbeta"], 0, 8, WSL + 2048,
                     waits=[k.tcnt()])
    u2 = ar.b(WSL + 8192, 64, T)
    for s in range(NSL):
        bank = st.pb[4]
        mms = []
        for kc in range(KC):
            def mk(kc=kc, s=s, bank=bank):
                def f(e, st_, sp):
                    return e.matmul(bank.ap[0:64, 0:TS],
                                    ar.b(WSL + kc * 128, 128, 64),
                                    h_at(kc)[:, s * TS:(s + 1) * TS],
                                    start=st_, stop=sp)
                return f
            mms.append(mk())
        mm_group(st, bank, mms, "scalar",
                 lambda e, s=s, bank=bank: e.activation(
                     u2[:, s * TS:(s + 1) * TS], bank.ap[0:64, 0:TS], AF.Silu,
                     bias=sv(st, "L0_attn_bau", rows=64), scale=1.0),
                 pre_waits=[wau] + h_toks)
    u2tok = k.scnt()
    beta_rows = ar.b_at(O_CS + 12288, 0, 8, T)
    for s in range(NSL):
        bank = st.pb[5]
        mms = []
        for kc in range(KC):
            def mk(kc=kc, s=s, bank=bank):
                def f(e, st_, sp):
                    return e.matmul(bank.ap[0:8, 0:TS],
                                    ar.b(WSL + 2048 + kc * 16, 128, 8),
                                    h_at(kc)[:, s * TS:(s + 1) * TS],
                                    start=st_, stop=sp)
                return f
            mms.append(mk())
        mm_group(st, bank, mms, "scalar",
                 lambda e, s=s, bank=bank: e.activation(
                     beta_rows[:, s * TS:(s + 1) * TS], bank.ap[0:8, 0:TS],
                     AF.Identity, bias=sv(st, "L0_attn_bbeta", rows=8),
                     scale=1.0),
                 pre_waits=[wbeta] + h_toks)
    st.betab_tok = k.scnt()
    for cc in range(4):
        for s in range(NSL):
            bank = st.pb[4]
            def mk(cc=cc, s=s, bank=bank):
                def f(e, st_, sp):
                    return e.matmul(
                        bank.ap[:, 0:TS],
                        ar.b(WSL + 1024, 64, 512)[:, cc * 128:(cc + 1) * 128],
                        u2[:, s * TS:(s + 1) * TS], start=st_, stop=sp)
                return f
            def ev(e, cc=cc, s=s, bank=bank):
                return e.copy(qp(st, "ab", cc)[:, s * TS:(s + 1) * TS],
                              bank.ap[:, 0:TS],
                              ) if False else e.activation(
                    qp(st, "ab", cc)[:, s * TS:(s + 1) * TS],
                    bank.ap[:, 0:TS], AF.Identity,
                    bias=sv(st, "L0_attn_bad", col=cc), scale=1.0)
            mm_group(st, bank, [mk()], "scalar", ev,
                     pre_waits=[wad, u2tok])
    st.ab_tok = k.scnt()
    emit_l0_qknorm(st)


def emit_l0_qknorm(st):
    k = st.k
    ar = st.ar
    sqb = ar.b(O_CS + 8192, 128, TS)
    rowsA = ar.f(O_CS, 8, T)[0:8, :]
    rab = ar.b_at(O_CS + 9216, 0, 8, TS)
    for name, tok in (("q", st.q_tok), ("k", st.k_tok)):
        for s in range(NSL):
            cs = slice(s * TS, (s + 1) * TS)
            bank = st.pb[0]
            for cc in range(4):
                k.waits("scalar", [tok, st.ab_tok])
                k.wait("scalar", k.tcnt())
                sqt = k.sig("scalar", lambda e, cc=cc, name=name, cs=cs:
                            e.activation(sqb, qp(st, name, cc)[:, cs],
                                         AF.Square))
                if cc == 0:
                    k.wait("tensor", bank.free)
                k.wait("tensor", sqt)
                k.op("tensor", lambda e, b=bank, cc=cc: e.matmul(
                    b.ap[0:8, 0:TS], st.CONBO8[:, cc * 8:(cc + 1) * 8],
                    sqb, start=(cc == 0), stop=(cc == 3)),
                    inc=("t", 1))
            mmt = k.tcnt()
            k.wait("scalar", mmt)
            k.sig("scalar", lambda e, b=bank, cs=cs: e.activation(
                rowsA[:, cs], b.ap[0:8, 0:TS], AF.Ln))
            ev = k.sig("scalar", lambda e, cs=cs: e.activation(
                rowsA[:, cs], rowsA[:, cs], AF.Exp, scale=0.5))
            bank.free = ev
        k.wait("vector", k.scnt())
        k.sig("vector", lambda e: e.tensor_scalar(
            out=rowsA, in0=rowsA, scalar1=1e-6, scalar2=None, op0=ALU.add))
        rv = k.sig("vector", lambda e: e.reciprocal(rowsA, rowsA))
        for s in range(NSL):
            cs = slice(s * TS, (s + 1) * TS)
            k.wait("vector", rv)
            k.wait("vector", k.tcnt())
            rbt = k.sig("vector", lambda e, cs=cs: e.tensor_copy(
                rab, rowsA[:, cs]))
            for cc in range(4):
                bank = st.pb[4 + (cc % 2)]
                k.wait("tensor", bank.free)
                k.wait("tensor", rbt)
                mt = k.op("tensor", lambda e, b=bank, cc=cc: e.matmul(
                    b.ap[:, 0:TS], st.CONSEL[:, cc * 128:(cc + 1) * 128],
                    rab, start=True, stop=True), inc=("t", 1))
                k.wait("vector", mt)
                ct = k.sig("vector", lambda e, b=bank, cc=cc, cs=cs, name=name:
                           e.tensor_tensor(
                               out=qp(st, name, cc)[:, cs],
                               in0=qp(st, name, cc)[:, cs],
                               in1=b.ap[:, 0:TS], op=ALU.mult))
                bank.free = ct
    st.qk_tok = k.vcnt()


def emit_l0_scan(st):
    k = st.k
    ar = st.ar
    mem = st.misc[:, 0:4]
    k.wait("vector", st.drive_tok)
    k.sig("vector", lambda e: e.memset(mem, 0.0))
    dr4 = ar.bf[0:128, ODR // 2: ODR // 2 + 4 * T].rearrange(
        "p (c t) -> p c t", t=T)
    sp4 = ar.bf[0:128, OSPK // 2: OSPK // 2 + 4 * T].rearrange(
        "p (c t) -> p c t", t=T)
    for t in range(T):
        t1 = k.sig("vector", lambda e, t=t: e.scalar_tensor_tensor(
            out=mem, in0=mem, scalar=0.9, in1=dr4[:, :, t:t + 1],
            op0=ALU.mult, op1=ALU.add))
        k.wait("vector", t1)
        t2 = k.sig("vector", lambda e, t=t: e.tensor_scalar(
            out=sp4[:, :, t:t + 1], in0=mem, scalar1=0.5, scalar2=None,
            op0=ALU.is_gt))
        k.wait("vector", t2)
        t3 = k.sig("vector", lambda e, t=t: e.scalar_tensor_tensor(
            out=mem, in0=sp4[:, :, t:t + 1], scalar=-0.5, in1=mem,
            op0=ALU.mult, op1=ALU.add))
        k.wait("vector", t3)
    st.scan_tok = t3


def emit_l0_decay(st):
    k = st.k
    ar = st.ar
    beta_rows = ar.b_at(O_CS + 12288, 0, 8, T)
    actbf = ar.b_at(O_CS, 0, 8, T)
    tmp = ar.b(O_CS + 4096, 128, T)
    k.wait("vector", st.scan_tok)
    for s in range(NSL):
        cs = slice(s * TS, (s + 1) * TS)
        bank = st.pb[0]
        for cc in range(4):
            if cc == 0:
                k.wait("tensor", bank.free)
            k.wait("tensor", st.scan_tok)
            k.op("tensor", lambda e, b=bank, cc=cc, cs=cs: e.matmul(
                b.ap[0:8, 0:TS], st.CONBO8[:, cc * 8:(cc + 1) * 8],
                qp(st, "spk", cc)[:, cs], start=(cc == 0), stop=(cc == 3)),
                inc=("t", 1))
        mmt = k.tcnt()
        k.wait("vector", mmt)
        ev = k.sig("vector", lambda e, b=bank, cs=cs: e.tensor_scalar(
            out=actbf[:, cs], in0=b.ap[0:8, 0:TS],
            scalar1=0.0, scalar2=None, op0=ALU.is_gt))
        bank.free = ev
    abf = k.vcnt()
    k.wait("vector", st.betab_tok)
    for s in range(NSL):
        cs = slice(s * TS, (s + 1) * TS)
        bank = st.pb[1]
        for cc in range(4):
            k.wait("vector", k.tcnt())
            bspt = k.sig("vector", lambda e, cc=cc, cs=cs: e.tensor_scalar(
                out=tmp[:, 0:TS], in0=qp(st, "spk", cc)[:, cs],
                scalar1=sv(st, "L0_attn_beta_spike", col=cc), scalar2=None,
                op0=ALU.mult))
            if cc == 0:
                k.wait("tensor", bank.free)
            k.wait("tensor", bspt)
            k.op("tensor", lambda e, b=bank, cc=cc: e.matmul(
                b.ap[0:8, 0:TS], st.CONBO8[:, cc * 8:(cc + 1) * 8],
                tmp[:, 0:TS], start=(cc == 0), stop=(cc == 3)),
                inc=("t", 1))
        mmt = k.tcnt()
        k.wait("vector", mmt)
        bv = k.sig("vector", lambda e, b=bank, cs=cs: e.tensor_tensor(
            out=beta_rows[:, cs], in0=beta_rows[:, cs],
            in1=b.ap[0:8, 0:TS], op=ALU.add))
        bank.free = bv
    k.wait("scalar", k.vcnt())
    bt = k.sig("scalar", lambda e: e.activation(beta_rows, beta_rows,
                                                AF.Sigmoid))
    k.wait("vector", bt)
    k.wait("vector", abf)
    betabf = k.sig("vector", lambda e: e.tensor_tensor(
        out=beta_rows, in0=beta_rows, in1=actbf, op=ALU.mult))
    dbg_dump(st, "actbf", actbf, [abf])

    la = ar.f(O_CS + 4096, 128, 1024)
    cum = ar.f(O_CS + 8192, 128, 1024)
    pc = st.misc[:, 8:136]
    for cc in range(4):
        for hh in range(2):
            hsl = slice(hh * 1024, (hh + 1) * 1024)
            k.waits("vector", [st.ab_tok, betabf])
            k.wait("vector", k.scnt())
            stt_t = k.sig("vector", lambda e, cc=cc, hsl=hsl:
                          e.scalar_tensor_tensor(
                out=la, in0=qp(st, "spk", cc)[:, hsl],
                scalar=sv(st, "L0_attn_alpha_spike", col=cc),
                in1=qp(st, "ab", cc)[:, hsl], op0=ALU.mult, op1=ALU.add))
            k.wait("scalar", stt_t)
            at = k.sig("scalar", lambda e: e.activation(la, la, AF.Sigmoid))
            for s2 in range(2):
                bank = st.pb[s2]
                k.wait("tensor", bank.free)
                k.wait("tensor", abf)
                mt = k.op("tensor", lambda e, b=bank, cc=cc, hh=hh, s2=s2:
                          e.matmul(
                    b.ap[:, 0:TS], st.CONSEL[:, cc * 128:(cc + 1) * 128],
                    actbf[:, hh * 1024 + s2 * TS: hh * 1024 + (s2 + 1) * TS],
                    start=True, stop=True), inc=("t", 1))
                k.wait("vector", mt)
                k.wait("vector", at)
                ssl = slice(s2 * TS, (s2 + 1) * TS)
                k.sig("vector", lambda e, ssl=ssl: e.tensor_scalar(
                    out=la[:, ssl], in0=la[:, ssl], scalar1=-1.0,
                    scalar2=None, op0=ALU.add))
                k.sig("vector", lambda e, b=bank, ssl=ssl: e.tensor_tensor(
                    out=la[:, ssl], in0=la[:, ssl], in1=b.ap[:, 0:TS],
                    op=ALU.mult))
                ct = k.sig("vector", lambda e, ssl=ssl: e.tensor_scalar(
                    out=la[:, ssl], in0=la[:, ssl], scalar1=1.0,
                    scalar2=None, op0=ALU.add))
                bank.free = ct
            if cc == 0 and hh == 0:
                dbg_dump(st, "laAF", la, [k.vcnt()])
            k.wait("scalar", k.vcnt())
            lt = k.sig("scalar", lambda e: e.activation(la, la, AF.Ln))
            k.wait("vector", lt)
            for ch in range(16):
                k.op("vector", lambda e, ch=ch: e.tensor_tensor_scan(
                    out=cum[:, ch * CH:(ch + 1) * CH],
                    data0=la[:, ch * CH:(ch + 1) * CH],
                    data1=st.ZERO64, initial=0.0, op0=ALU.add, op1=ALU.add))
            cvt = k.sig("vector", lambda e: e.tensor_copy(
                cum[:, 0:1], cum[:, 0:1]))
            if cc == 0 and hh == 0:
                dbg_dump(st, "cum0", cum, [cvt])
            k.wait("scalar", cvt)
            k.sig("scalar", lambda e, cc=cc, hsl=hsl: e.activation(
                qp(st, "dr", cc)[:, hsl], cum, AF.Exp))
            k.sig("scalar", lambda e, cc=cc, hsl=hsl: e.activation(
                qp(st, "spk", cc)[:, hsl], cum, AF.Exp, scale=-1.0))
            k.sig("scalar", lambda e, cc=cc, hh=hh: e.activation(
                pc[:, cc * 32 + hh * 16: cc * 32 + (hh + 1) * 16],
                cum.rearrange("p (ch c) -> p ch c", c=CH)[:, :, CH - 1:CH],
                AF.Exp))
    st.p_tok = k.scnt()
    for cc in range(4):
        k.waits("vector", [st.p_tok, st.qk_tok])
        for s in range(NSL):
            bank = st.pb[s]
            k.wait("tensor", bank.free)
            k.wait("tensor", betabf)
            mt = k.op("tensor", lambda e, b=bank, cc=cc, s=s: e.matmul(
                b.ap[:, 0:TS], st.CONSEL[:, cc * 128:(cc + 1) * 128],
                beta_rows[:, s * TS:(s + 1) * TS], start=True, stop=True),
                inc=("t", 1))
            k.wait("vector", mt)
            ssl = slice(s * TS, (s + 1) * TS)
            k.sig("vector", lambda e, cc=cc, ssl=ssl: e.tensor_tensor(
                out=qp(st, "ab", cc)[:, ssl], in0=qp(st, "k", cc)[:, ssl],
                in1=qp(st, "spk", cc)[:, ssl], op=ALU.mult))
            ct = k.sig("vector", lambda e, b=bank, cc=cc, ssl=ssl:
                       e.tensor_tensor(
                           out=qp(st, "ab", cc)[:, ssl],
                           in0=qp(st, "ab", cc)[:, ssl],
                           in1=b.ap[:, 0:TS], op=ALU.mult))
            bank.free = ct
        k.sig("vector", lambda e, cc=cc: e.tensor_tensor(
            out=qp(st, "k", cc), in0=qp(st, "k", cc), in1=qp(st, "dr", cc),
            op=ALU.mult))
        k.sig("vector", lambda e, cc=cc: e.tensor_tensor(
            out=qp(st, "q", cc), in0=qp(st, "q", cc), in1=qp(st, "dr", cc),
            op=ALU.mult))
    st.decay_tok = k.vcnt()


DN_TILES = ["X0", "B0", "AT", "N", "X2", "B2", "X4", "B4", "X8", "B8",
            "X16", "B16", "X32", "RHS", "U", "KTM", "VTM", "YTM", "KPC"]


def emit_l0_deltanet(st):
    k = st.k
    ar = st.ar
    OY = OQ
    S = st.misc[:, 136:200]
    pc = st.misc[:, 8:136]
    toff = {n: O_CS + i * 128 for i, n in enumerate(DN_TILES)}

    def tb(n):
        return ar.b(toff[n], 128, 64)

    kpc = tb("KPC")
    Sbf = ar.b(O_CS + len(DN_TILES) * 128, 128, 256)
    pbT, pbB, pbA, pbP, pbR, pbU = st.pb
    pf0, pf1 = st.pbf
    k.wait("vector", st.decay_tok)
    for pp in range(4):
        k.sig("vector", lambda e, pp=pp: e.memset(
            st.misc[:, 136 + pp * 64:200 + pp * 64], 0.0))
    zt = k.sig("vector", lambda e: e.memset(Sbf, 0.0))
    s_tok = [zt] * 4

    def two(bank, mkA, mkB, out_col=64):
        return mm_raw(st, bank, [
            lambda e, s, p, mkA=mkA: mkA(e),
            lambda e, s, p, mkB=mkB: mkB(e),
        ], pre_waits=[])

    for c in range(NCH):
        cols = slice(c * CH, (c + 1) * CH)
        for pp in range(4):
            kA = qp(st, "k", pp)[0:64, cols]
            kB = qp(st, "k", pp)[64:128, cols]
            qA = qp(st, "q", pp)[0:64, cols]
            qB = qp(st, "q", pp)[64:128, cols]
            xA = qp(st, "ab", pp)[0:64, cols]
            xB = qp(st, "ab", pp)[64:128, cols]

            def pair(bank, lA, rA, lB, rB, pre):
                k.wait("tensor", bank.free)
                k.waits("tensor", pre)
                k.op("tensor", lambda e, bank=bank, lA=lA, rA=rA: e.matmul(
                    bank.ap[0:64, 0:64], lA, rA, start=True, stop=True))
                return k.op("tensor",
                            lambda e, bank=bank, lB=lB, rB=rB: e.matmul(
                                bank.ap[64:128, 0:64], lB, rB,
                                start=True, stop=True), inc=("t", 1))

            mt = pair(pbT, kA, xA, kB, xB, [st.decay_tok])
            k.wait("vector", mt)
            x0t = k.sig("vector", lambda e: e.tensor_tensor(
                out=tb("X0"), in0=pbT.ap[:, 0:64], in1=st.CONLT,
                op=ALU.mult))
            pbT.free = x0t
            mt = pair(pbB, xA, kA, xB, kB, [])
            k.wait("vector", mt)
            b0t = k.sig("vector", lambda e: e.tensor_tensor(
                out=tb("B0"), in0=pbB.ap[:, 0:64], in1=st.CONUT,
                op=ALU.mult))
            pbB.free = b0t
            mt = pair(pbA, xA, qA, xB, qB, [])
            k.wait("vector", mt)
            att = k.sig("vector", lambda e: e.tensor_tensor(
                out=tb("AT"), in0=pbA.ap[:, 0:64], in1=st.CONUI,
                op=ALU.mult))
            pbA.free = att
            k.wait("vector", b0t)
            ntok = k.sig("vector", lambda e: e.tensor_tensor(
                out=tb("N"), in0=st.CONIS, in1=tb("B0"), op=ALU.subtract))
            xtoks = {"X0": x0t, "B0": b0t}
            prevs = ("X0", "B0")
            for xn, bn in (("X2", "B2"), ("X4", "B4"), ("X8", "B8"),
                           ("X16", "B16"), ("X32", None)):
                px, pbn = prevs
                mt = pair(pbP, tb(pbn)[0:64, :], tb(px)[0:64, :],
                          tb(pbn)[64:128, :], tb(px)[64:128, :],
                          [xtoks[px], xtoks[pbn]])
                k.wait("scalar", mt)
                xt = k.sig("scalar", lambda e, xn=xn: e.copy(
                    tb(xn), pbP.ap[:, 0:64]))
                pbP.free = xt
                xtoks[xn] = xt
                if bn is not None:
                    mt2 = pair(pbR, tb(px)[0:64, :], tb(pbn)[0:64, :],
                               tb(px)[64:128, :], tb(pbn)[64:128, :],
                               [xtoks[px], xtoks[pbn]])
                    k.wait("scalar", mt2)
                    bt2 = k.sig("scalar", lambda e, bn=bn: e.copy(
                        tb(bn), pbR.ap[:, 0:64]))
                    pbR.free = bt2
                    xtoks[bn] = bt2
                    prevs = (xn, bn)
            ctok = ntok
            for xn in ("X2", "X4", "X8", "X16", "X32"):
                k.wait("tensor", pbU.free)
                k.waits("tensor", [ctok, xtoks[xn]])
                k.op("tensor", lambda e, xn=xn: e.matmul(
                    pbU.ap[0:64, 0:64], tb(xn)[0:64, :], tb("N")[0:64, :],
                    start=True, stop=False))
                k.op("tensor", lambda e: e.matmul(
                    pbU.ap[0:64, 0:64], st.CONIS[0:64, :], tb("N")[0:64, :],
                    start=False, stop=True))
                k.op("tensor", lambda e, xn=xn: e.matmul(
                    pbU.ap[64:128, 0:64], tb(xn)[64:128, :],
                    tb("N")[64:128, :], start=True, stop=False))
                mt = k.op("tensor", lambda e: e.matmul(
                    pbU.ap[64:128, 0:64], st.CONIS[64:128, :],
                    tb("N")[64:128, :], start=False, stop=True),
                    inc=("t", 1))
                k.wait("scalar", mt)
                ctok = k.sig("scalar", lambda e: e.copy(tb("N"),
                                                        pbU.ap[:, 0:64]))
                pbU.free = ctok
            ntok = ctok
            if c == 0 and pp == 0:
                dbg_dump(st, "dn_X0", tb("X0"), [x0t])
                dbg_dump(st, "dn_B0", tb("B0"), [b0t])
                dbg_dump(st, "dn_AT", tb("AT"), [att])
                dbg_dump(st, "dn_N", tb("N"), [ntok])
            k.wait("vector", k.tcnt())
            kpt = k.sig("vector", lambda e, pp=pp, c=c, cols=cols:
                        e.tensor_scalar(
                out=kpc, in0=qp(st, "ab", pp)[:, cols],
                scalar1=pc[:, pp * 32 + c: pp * 32 + c + 1],
                scalar2=None, op0=ALU.mult))
            k.wait("tensor", pf0.free)
            k.wait("tensor", kpt)
            k.op("tensor", lambda e: e.transpose(
                pf0.ap[0:64, 0:64], kpc[0:64, :], st.CONID[0:64, 0:64]))
            tt = k.op("tensor", lambda e: e.transpose(
                pf0.ap[64:128, 0:64], kpc[64:128, :],
                st.CONID[64:128, 64:128]), inc=("t", 1))
            k.wait("scalar", tt)
            ktm = k.sig("scalar", lambda e: e.copy(tb("KTM"),
                                                   pf0.ap[:, 0:64]))
            pf0.free = ktm
            k.wait("tensor", pf1.free)
            k.wait("tensor", st.v_tok)
            k.op("tensor", lambda e, pp=pp, cols=cols: e.transpose(
                pf1.ap[0:64, 0:64], qp(st, "v", pp)[0:64, cols],
                st.CONID[0:64, 0:64]))
            tt2 = k.op("tensor", lambda e, pp=pp, cols=cols: e.transpose(
                pf1.ap[64:128, 0:64], qp(st, "v", pp)[64:128, cols],
                st.CONID[64:128, 64:128]), inc=("t", 1))
            k.wait("scalar", tt2)
            vtm = k.sig("scalar", lambda e: e.copy(tb("VTM"),
                                                   pf1.ap[:, 0:64]))
            pf1.free = vtm

            if c == 0 and pp == 0:
                dbg_dump(st, "dn_KPC", kpc, [kpt])
                dbg_dump(st, "dn_KTM", tb("KTM"), [ktm])
                dbg_dump(st, "dn_VTM", tb("VTM"), [vtm])
            Sp = st.misc[:, 136 + pp * 64:200 + pp * 64]
            Sbp = Sbf[:, pp * 64:(pp + 1) * 64]
            mt = pair(pbR, kA, Sbp[0:64, :], kB, Sbp[64:128, :],
                      [s_tok[pp], x0t])
            k.wait("vector", mt)
            k.wait("vector", vtm)
            rhst = k.sig("vector", lambda e: e.tensor_tensor(
                out=tb("RHS"), in0=tb("VTM"), in1=pbR.ap[:, 0:64],
                op=ALU.subtract))
            pbR.free = rhst
            if c == 0 and pp == 0:
                dbg_dump(st, "dn_RHS", tb("RHS"), [rhst])
            mt = pair(pbU, tb("N")[0:64, :], tb("RHS")[0:64, :],
                      tb("N")[64:128, :], tb("RHS")[64:128, :],
                      [ntok, rhst])
            k.wait("scalar", mt)
            utok = k.sig("scalar", lambda e: e.copy(tb("U"),
                                                    pbU.ap[:, 0:64]))
            pbU.free = utok
            k.wait("tensor", pbT.free)
            k.waits("tensor", [att, utok, s_tok[pp]])
            k.op("tensor", lambda e, qA=qA, Sbp=Sbp: e.matmul(
                pbT.ap[0:64, 0:64], qA, Sbp[0:64, :], start=True, stop=False))
            k.op("tensor", lambda e: e.matmul(
                pbT.ap[0:64, 0:64], tb("AT")[0:64, :], tb("U")[0:64, :],
                start=False, stop=True))
            k.op("tensor", lambda e, qB=qB, Sbp=Sbp: e.matmul(
                pbT.ap[64:128, 0:64], qB, Sbp[64:128, :],
                start=True, stop=False))
            ot = k.op("tensor", lambda e: e.matmul(
                pbT.ap[64:128, 0:64], tb("AT")[64:128, :],
                tb("U")[64:128, :], start=False, stop=True), inc=("t", 1))
            mt = pair(pbB, tb("KTM")[0:64, :], tb("U")[0:64, :],
                      tb("KTM")[64:128, :], tb("U")[64:128, :],
                      [ktm, utok])
            k.wait("vector", mt)
            k.sig("vector", lambda e, Sp=Sp, pp=pp, c=c:
                  e.scalar_tensor_tensor(
                      out=Sp, in0=Sp,
                      scalar=pc[:, pp * 32 + c: pp * 32 + c + 1],
                      in1=pbB.ap[:, 0:64], op0=ALU.mult, op1=ALU.add))
            k.wait("vector", k.vcnt())
            sup = k.sig("vector", lambda e, Sp=Sp, Sbp=Sbp: e.tensor_copy(
                Sbp, Sp))
            pbB.free = sup
            s_tok[pp] = sup
            sso = st.misc[:, 392:400]
            k.wait("scalar", ot)
            k.wait("scalar", k.tcnt())
            k.sig("scalar", lambda e: e.activation(
                tb("YTM"), pbT.ap[:, 0:64], AF.Square,
                accum_out=sso[:, 0:1]))
            lnt = k.sig("scalar", lambda e: e.activation(
                sso[:, 2:3], sso[:, 0:1], AF.Ln, bias=st.EPS6,
                scale=1.0 / 64))
            k.wait("scalar", lnt)
            sq3 = k.sig("scalar", lambda e: e.activation(
                sso[:, 2:3], sso[:, 2:3], AF.Exp, scale=-0.5))
            k.wait("vector", sq3)
            rc2 = k.sig("vector", lambda e: e.reciprocal(
                sso[:, 4:5], sso[:, 2:3])) if False else None
            yv = k.sig("vector", lambda e: e.tensor_scalar(
                out=tb("YTM"), in0=pbT.ap[:, 0:64],
                scalar1=sso[:, 2:3], scalar2=None, op0=ALU.mult))
            pbT.free = yv
            k.wait("tensor", pf0.free)
            k.wait("tensor", yv)
            k.op("tensor", lambda e: e.transpose(
                pf0.ap[0:64, 0:64], tb("YTM")[0:64, :], st.CONID[0:64, 0:64]))
            ty = k.op("tensor", lambda e: e.transpose(
                pf0.ap[64:128, 0:64], tb("YTM")[64:128, :],
                st.CONID[64:128, 64:128]), inc=("t", 1))
            k.wait("scalar", ty)
            yev = k.sig("scalar", lambda e, pp=pp, cols=cols: e.copy(
                ar.b(OY + pp * 4096, 128, T)[:, cols], pf0.ap[:, 0:64]))
            pf0.free = yev
            if pp == 0 and c in (0, 1):
                dbg_dump(st, f"dn_U_{c}", tb("U"), [utok])
                dbg_dump(st, f"dn_Y_{c}", tb("YTM"), [yv])
                dbg_dump(st, f"dn_S_{c}", Sp, [s_tok[pp]])
    st.y_tok = k.scnt()


def emit_l0_tail(st):
    k = st.k
    ar = st.ar
    dr = st.dr
    OY = OQ
    k.waits("sync", [st.y_tok] + st.l0_spill_toks)
    ln_toks = []
    g_toks = []
    for c in range(KC):
        ln_toks.append(k.dma("sync", ar.b(O_B + c * 4096, 128, T),
                             st.spill_ln0[c * 128:(c + 1) * 128, :]))
        g_toks.append(k.dma("sync", ar.b(O_C + c * 4096, 128, T),
                            st.spill_gate[c * 128:(c + 1) * 128, :]))
    lnr = lambda c: ar.b(O_B + c * 4096, 128, T)
    gtr = lambda c: ar.b(O_C + c * 4096, 128, T)
    tmpf = ar.f(O_CS, 128, TS)
    y1f = ar.f(O_CS + 2048, 128, TS)

    def w_pre(col0, ncols, wap, tok):
        k.wait("vector", tok)
        vt = None
        for kc in range(4):
            vt = k.sig("vector", lambda e, kc=kc, wap=wap, ncols=ncols:
                       e.tensor_scalar(
                           out=wap[:, kc * ncols:(kc + 1) * ncols],
                           in0=wap[:, kc * ncols:(kc + 1) * ncols],
                           scalar1=sv(st, "L0_attn_head_norm_w", col=kc),
                           scalar2=None, op0=ALU.mult))
        return vt

    y1sp = []

    def evac(mco, s, bank, mmtok):
        ssl = slice(s * TS, (s + 1) * TS)
        k.wait("vector", mmtok)
        k.waits("vector", [ln_toks[mco], g_toks[mco]])
        if y1sp:
            k.wait("vector", y1sp[-1])
        k.sig("vector", lambda e, bank=bank, mco=mco, ssl=ssl: e.tensor_tensor(
            out=tmpf, in0=bank.ap[:, 0:TS], in1=gtr(mco)[:, ssl],
            op=ALU.mult))
        vt = k.sig("vector", lambda e, mco=mco, ssl=ssl: e.tensor_tensor(
            out=y1f, in0=tmpf, in1=lnr(mco)[:, ssl], op=ALU.add))
        bank.free = vt
        k.waits("vector", [st.r_tok[mco]] + st.r_readers[mco])
        st.r_readers[mco] = []
        st.r_tok[mco] = k.sig("vector", lambda e, mco=mco, ssl=ssl:
                              e.tensor_tensor(
            out=st.R[:, mco * T:(mco + 1) * T][:, ssl],
            in0=st.R[:, mco * T:(mco + 1) * T][:, ssl],
            in1=y1f, op=ALU.add))
        y1sp.append(k.dma("gpsimd",
                          st.spill_y1[mco * 128:(mco + 1) * 128, ssl],
                          y1f, waits=[st.r_tok[mco]]))

    emit_proj(st, dr["L0_attn_Wo"], lambda cc: ar.b(OY + cc * 4096, 128, T),
              4, D, evac, O_CS + 4096, 16384, in_toks=[st.y_tok],
              w_pre=w_pre, w_waits=[st.decay_tok, st.y_tok])
    y1r = []
    k.waits("sync", y1sp)
    for c in range(KC):
        y1r.append(k.dma("sync", ar.b(O_A + c * 4096, 128, T),
                         st.spill_y1[c * 128:(c + 1) * 128, :]))
    y1_at = lambda c: ar.b(O_A + c * 4096, 128, T)
    z_at = lambda c: ar.b(O_C + c * 4096, 128, T)
    z_toks = emit_norm(st, y1_at, z_at, "rms", 1e-6,
                       tr_off=O_CS + 4096, bc_off=O_B,
                       gain_at=lambda c: sv(st, "L0_attn_ff_norm_w", col=c),
                       src_toks=y1r)
    for c in range(KC):
        dbg_dump(st, f"z_{c}", z_at(c), [z_toks[c]])
    WF1 = O_CS + 4096
    WF3 = O_CS + 8192
    WF2 = O_CS + 12288
    trsl = ar.b(O_CS, 128, TS)
    pbf_f32 = [b.ap.bitcast(F32) for b in st.pbf]
    zlast = z_toks[-1]
    f1read = [zlast, zlast]
    f3read = [zlast, zlast]
    f2read = [zlast, zlast]
    last_hs_tt = [None]
    hs_at = lambda kcp: ar.b(O_A + kcp * 2048, 128, 1024)
    for half in range(2):
        tok0 = half * 1024
        htoks = []
        for kcp in range(32):
            slot = kcp % 2
            w1 = wload_bf(st, dr["L0_attn_Wff1"], kcp * 128, 128,
                          WF1 + slot * 2048, waits=[f1read[slot]])
            w3 = wload_bf(st, dr["L0_attn_Wff3"], kcp * 128, 128,
                          WF3 + slot * 2048, waits=[f3read[slot]])
            w1ap = ar.b(WF1 + slot * 2048, 128, 1024)
            w3ap = ar.b(WF3 + slot * 2048, 128, 1024)
            for sl in range(2):
                tsl = slice(tok0 + sl * TS, tok0 + (sl + 1) * TS)
                mms1 = []
                mms3 = []
                for kc in range(KC):
                    def mk1(kc=kc, tsl=tsl, w1ap=w1ap):
                        def f(e, s_, p_):
                            return e.matmul(st.pb[4].ap[:, 0:TS],
                                            w1ap[:, kc * 128:(kc + 1) * 128],
                                            z_at(kc)[:, tsl], start=s_, stop=p_)
                        return f
                    def mk3(kc=kc, tsl=tsl, w3ap=w3ap):
                        def f(e, s_, p_):
                            return e.matmul(st.pb[5].ap[:, 0:TS],
                                            w3ap[:, kc * 128:(kc + 1) * 128],
                                            z_at(kc)[:, tsl], start=s_, stop=p_)
                        return f
                    mms1.append(mk1())
                    mms3.append(mk3())
                m1 = mm_raw(st, st.pb[4], mms1, pre_waits=[w1] + z_toks)
                k.wait("scalar", m1)
                if last_hs_tt[0] is not None:
                    k.wait("scalar", last_hs_tt[0])
                e1 = k.sig("scalar", lambda e: e.activation(
                    trsl, st.pb[4].ap[:, 0:TS], AF.Silu))
                st.pb[4].free = e1
                m3 = mm_raw(st, st.pb[5], mms3, pre_waits=[w3] + z_toks)
                k.wait("vector", m3)
                k.wait("vector", e1)
                e3 = k.sig("vector", lambda e, kcp=kcp, sl=sl: e.tensor_tensor(
                    out=hs_at(kcp)[:, sl * TS:(sl + 1) * TS],
                    in0=trsl, in1=st.pb[5].ap[:, 0:TS], op=ALU.mult))
                st.pb[5].free = e3
                last_hs_tt[0] = e3
                htoks.append(e3)
            f1read[slot] = k.tcnt()
            f3read[slot] = k.tcnt()
        for mcog in range(2):
            accb = st.pb[0:6] + st.pbf[0:2]
            accs = [st.pb[i].ap[:, 0:TS] for i in range(6)] + \
                   [pbf_f32[0][:, 0:TS], pbf_f32[1][:, 0:TS]]
            for b in accb:
                k.wait("tensor", b.free)
            k.waits("tensor", htoks)
            for kcp in range(32):
                slot = kcp % 2
                w2ap = ar.b(WF2 + slot * 2048, 128, 1024)
                w2 = k.dma("gpsimd", w2ap,
                           dr["L0_attn_Wff2"][kcp * 128:(kcp + 1) * 128, :],
                           waits=[f2read[slot]])
                k.wait("tensor", w2)
                for mc4 in range(4):
                    mco = mcog * 4 + mc4
                    for sl in range(2):
                        idx = mc4 * 2 + sl
                        k.op("tensor",
                             lambda e, w2ap=w2ap, mco=mco, sl=sl, idx=idx,
                             kcp=kcp, accs=accs: e.matmul(
                                 accs[idx],
                                 w2ap[:, mco * 128:(mco + 1) * 128],
                                 hs_at(kcp)[:, sl * TS:(sl + 1) * TS],
                                 start=(kcp == 0), stop=(kcp == 31)),
                             inc=("t", 1))
                f2read[slot] = k.tcnt()
            dntok = k.tcnt()
            k.wait("vector", dntok)
            for mc4 in range(4):
                mco = mcog * 4 + mc4
                for sl in range(2):
                    idx = mc4 * 2 + sl
                    tsl = slice(tok0 + sl * TS, tok0 + (sl + 1) * TS)
                    k.waits("vector", [st.r_tok[mco]] + st.r_readers[mco])
                    st.r_readers[mco] = []
                    st.r_tok[mco] = k.sig(
                        "vector", lambda e, mco=mco, tsl=tsl, idx=idx,
                        accs=accs: e.tensor_tensor(
                            out=st.R[:, mco * T:(mco + 1) * T][:, tsl],
                            in0=st.R[:, mco * T:(mco + 1) * T][:, tsl],
                            in1=accs[idx], op=ALU.add))
                    accb[idx].free = st.r_tok[mco]
        if half == 0:
            k.wait("vector", k.tcnt())
    st.l0ffn_done = k.vcnt()


def emit_l0_ffn(st):
    k = st.k
    ar = st.ar
    dr = st.dr
    NI = 22
    ln1_at = lambda c: ar.b(O_A + c * 4096, 128, T)
    ln_toks = emit_norm(st, lambda c: st.R[:, c * T:(c + 1) * T], ln1_at,
                        "ln", 1e-5, tr_off=O_CS + 4096, bc_off=O_B,
                        src_toks=[t for t in st.r_tok if t])
    for c in range(KC):
        st.r_readers[c].append(ln_toks[c])
    mx = st.misc[:, 400:404]
    srow = st.misc[0:1, 404:412]
    sbc = st.misc[:, 412:416]
    wmx = st.misc[:, 416:420]
    H2B = [O_B, O_B + 4096]
    WST = O_B + 8192
    WQB = O_B + 12288
    WQT = [O_B + 14336, O_B + 16384]
    GSIL = O_B + 18432
    VQF = O_B + 22528
    vqf = ar.f(VQF, 128, T)
    gsil = ar.b(GSIL, 128, T)

    def absmax_acc(src_ap, first, pre=()):
        k.waits("vector", list(pre))
        if first:
            t = k.sig("vector", lambda e: e.tensor_reduce(
                out=mx[:, 0:1], in_=src_ap, axis=AX.X, op=ALU.max,
                apply_absolute_value=True))
            k.wait("vector", t)
            return t
        t = k.sig("vector", lambda e: e.tensor_reduce(
            out=mx[:, 1:2], in_=src_ap, axis=AX.X, op=ALU.max,
            apply_absolute_value=True))
        k.wait("vector", t)
        t = k.sig("vector", lambda e: e.tensor_tensor(
            out=mx[:, 0:1], in0=mx[:, 0:1], in1=mx[:, 1:2], op=ALU.max))
        k.wait("vector", t)
        return t

    def absmax_finish(t):
        d1 = k.dma("gpsimd", st.cc_in.ap()[0:1, :], mx[:, 0:1], waits=[t])
        d2 = k.dma("gpsimd", st.ccrow, st.cc_in.ap()[0:1, :], waits=[d1])
        k.wait("vector", d2)
        gv = k.sig("vector", lambda e: e.tensor_reduce(
            out=srow[:, 0:1], in_=st.ccrow, axis=AX.X, op=ALU.max))
        d3 = k.dma("gpsimd", st.cc_in.ap()[0:1, 0:1], srow[:, 0:1],
                   waits=[gv])
        ranks = CC_RANKS if CC_RANKS is not None else list(range(NCORES))
        if len(ranks) == 1:
            return k.dma("gpsimd", srow[:, 0:1], st.cc_in.ap()[0:1, 0:1],
                         waits=[d3])
        k.wait("gpsimd", d3)
        cct = k.op("gpsimd", lambda e, ranks=ranks: e.collective_compute(
            "AllReduce", ALU.max,
            replica_groups=[ranks],
            ins=[st.cc_in.ap().opt()],
            outs=[st.cc_out.ap().opt()],
        ), inc=("cc", 1))
        k.wait("gpsimd", cct)
        return k.dma("gpsimd", srow[:, 0:1], st.cc_out.ap()[0:1, 0:1])

    def mk_scales(rt, clip_name):
        cv = k.sig("vector", lambda e: e.tensor_copy(
            srow[:, 1:2], sv(st, clip_name, rows=1)))
        d1 = k.dma("gpsimd", st.bscr[0:1, :], srow[:, 0:2], waits=[rt, cv])
        d2 = k.dma("gpsimd", sbc[:, 0:2],
                   st.bscr[0:1, :].partition_broadcast(128), waits=[d1])
        k.wait("scalar", d2)
        sg = k.sig("scalar", lambda e: e.activation(
            sbc[:, 1:2], sbc[:, 1:2], AF.Sigmoid))
        k.wait("vector", sg)
        m1 = k.sig("vector", lambda e: e.tensor_tensor(
            out=sbc[:, 2:3], in0=sbc[:, 0:1], in1=sbc[:, 1:2], op=ALU.mult))
        k.wait("vector", m1)
        vt = k.sig("vector", lambda e: e.tensor_scalar(
            out=sbc[:, 0:1], in0=sbc[:, 2:3], scalar1=1.0 / 127.0,
            scalar2=None, op0=ALU.mult))
        k.wait("vector", vt)
        bt = k.sig("vector", lambda e: e.reciprocal(sbc[:, 1:2],
                                                    sbc[:, 0:1]))
        k.wait("vector", bt)
        return bt

    RM = 12582912.0

    def quant_magic(src_ap, dst_ap, s_ap, rs_ap, vq_ap, pre=()):
        k.waits("vector", list(pre))
        t = k.sig("vector", lambda e: e.tensor_scalar(
            out=vq_ap, in0=src_ap, scalar1=rs_ap, scalar2=None,
            op0=ALU.mult))
        k.wait("vector", t)
        t = k.sig("vector", lambda e: e.tensor_scalar(
            out=vq_ap, in0=vq_ap, scalar1=RM, scalar2=None, op0=ALU.add))
        k.wait("vector", t)
        t = k.sig("vector", lambda e: e.tensor_scalar(
            out=vq_ap, in0=vq_ap, scalar1=RM, scalar2=-128.0,
            op0=ALU.subtract, op1=ALU.max))
        k.wait("vector", t)
        t = k.sig("vector", lambda e: e.tensor_scalar(
            out=vq_ap, in0=vq_ap, scalar1=127.0, scalar2=None, op0=ALU.min))
        k.wait("vector", t)
        t = k.sig("vector", lambda e: e.tensor_scalar(
            out=dst_ap, in0=vq_ap, scalar1=s_ap, scalar2=None,
            op0=ALU.mult))
        k.wait("vector", t)
        return t

    dbg_dump(st, "ln1_0", ln1_at(0), [ln_toks[0]])
    am = None
    for c in range(KC):
        am = absmax_acc(ln1_at(c), c == 0, pre=[ln_toks[c]])
    rt = absmax_finish(am)
    bt = mk_scales(rt, "L0_ffn_clip_x")
    dbg_dump(st, "scales1", sbc[:, 0:3], [bt])
    xq_toks = []
    for c in range(KC):
        xq_toks.append(quant_magic(ln1_at(c), ln1_at(c), sbc[:, 0:1],
                                   sbc[:, 1:2], vqf,
                                   pre=([bt] if c == 0 else ())))
    dbg_dump(st, "xq_0", ln1_at(0), [xq_toks[0]])

    wqt_free = [None, None]
    st.wst_free = ln_toks[-1]
    st.wqb_free = None

    def wquant_T(wname, i, slot):
        wl = k.dma("sync", ar.f(WST, 128, 1024),
                   dr[wname][i * 128:(i + 1) * 128, :],
                   waits=([st.wst_free] if st.wst_free else []))
        k.wait("vector", wl)
        rm = k.sig("vector", lambda e: e.tensor_reduce(
            out=wmx[:, 0:1], in_=ar.f(WST, 128, 1024), axis=AX.X,
            op=ALU.max, apply_absolute_value=True))
        k.wait("vector", rm)
        rm2 = k.sig("vector", lambda e: e.tensor_scalar(
            out=wmx[:, 1:2], in0=wmx[:, 0:1], scalar1=1.0 / 127.0,
            scalar2=None, op0=ALU.mult))
        k.wait("vector", rm2)
        rm3 = k.sig("vector", lambda e: e.reciprocal(wmx[:, 2:3],
                                                     wmx[:, 1:2]))
        k.wait("vector", rm3)
        if st.wqb_free is not None:
            k.wait("vector", st.wqb_free)
        qt = quant_magic(ar.f(WST, 128, 1024), ar.b(WQB, 128, 1024),
                         wmx[:, 1:2], wmx[:, 2:3],
                         ar.f(WST, 128, 1024))
        st.wst_free = qt
        pfb = st.pbf[0]
        k.wait("tensor", pfb.free)
        k.waits("tensor", [qt] + ([wqt_free[slot]] if wqt_free[slot] else []))
        for kc in range(KC):
            k.op("tensor", lambda e, kc=kc: e.transpose(
                pfb.ap[:, kc * 128:(kc + 1) * 128],
                ar.b(WQB, 128, 1024)[:, kc * 128:(kc + 1) * 128],
                st.CONID), inc=("t", 1))
        tt = k.tcnt()
        st.wqb_free = tt
        k.wait("scalar", tt)
        ev = k.sig("scalar", lambda e, slot=slot: e.copy(
            ar.b(WQT[slot], 128, 1024), pfb.ap[:, 0:1024]))
        pfb.free = ev
        return ev

    h2_free = [[], []]
    sp_toks = []
    am2 = None
    for i in range(NI):
        evg = wquant_T("L0_ffn_w_gate", i, 0)
        evu = wquant_T("L0_ffn_w_up", i, 1)
        p = i % 2
        h2b = ar.b(H2B[p], 128, T)
        e3s = []
        pg, pu = st.pb[4], st.pb[5]
        for s in range(NSL):
            mmsg = []
            mmsu = []
            for kc in range(KC):
                def mkg(kc=kc, s=s):
                    def f(e, s_, p_):
                        return e.matmul(
                            pg.ap[:, 0:TS],
                            ar.b(WQT[0], 128, 1024)[:, kc * 128:(kc + 1) * 128],
                            ln1_at(kc)[:, s * TS:(s + 1) * TS],
                            start=s_, stop=p_)
                    return f
                def mku(kc=kc, s=s):
                    def f(e, s_, p_):
                        return e.matmul(
                            pu.ap[:, 0:TS],
                            ar.b(WQT[1], 128, 1024)[:, kc * 128:(kc + 1) * 128],
                            ln1_at(kc)[:, s * TS:(s + 1) * TS],
                            start=s_, stop=p_)
                    return f
                mmsg.append(mkg())
                mmsu.append(mku())
            mg = mm_raw(st, pg, mmsg, pre_waits=[evg] + xq_toks)
            k.wait("scalar", mg)
            k.wait("scalar", k.vcnt())
            e1 = k.sig("scalar", lambda e, s=s: e.activation(
                gsil[:, s * TS:(s + 1) * TS], pg.ap[:, 0:TS], AF.Silu))
            pg.free = e1
            mu = mm_raw(st, pu, mmsu, pre_waits=[evu] + xq_toks)
            k.wait("vector", mu)
            k.wait("vector", e1)
            k.waits("vector", h2_free[p])
            h2_free[p] = []
            e3 = k.sig("vector", lambda e, p=p, s=s: e.tensor_tensor(
                out=ar.b(H2B[p], 128, T)[:, s * TS:(s + 1) * TS],
                in0=gsil[:, s * TS:(s + 1) * TS],
                in1=pu.ap[:, 0:TS], op=ALU.mult))
            pu.free = e3
            e3s.append(e3)
            sp = k.dma("gpsimd",
                       st.spill_h2[i * 128:(i + 1) * 128,
                                   s * TS:(s + 1) * TS],
                       ar.b(H2B[p], 128, T)[:, s * TS:(s + 1) * TS],
                       waits=[e3])
            sp_toks.append(sp)
            h2_free[p].append(sp)
        am2 = absmax_acc(h2b, i == 0, pre=e3s)
        h2_free[p].append(am2)
        wqt_free[0] = k.tcnt()
        wqt_free[1] = k.tcnt()

    for i in (0, 5, 10, 16, 21):
        dbg_dump(st, f"h2_{i}", st.spill_h2[i * 128:(i + 1) * 128, :],
                 sp_toks)
    rt2 = absmax_finish(am2)
    bt2 = mk_scales(rt2, "L0_ffn_clip_down")
    dbg_dump(st, "scales2", sbc[:, 0:3], [bt2])

    WDS = O_A
    WDQ = O_A + 11264
    WDT = lambda m, ii: ar.b(O_B + (m * NI + ii) * 256, 128, 128)
    WDTg = lambda m, base, n: ar.b(O_B + (m * NI + base) * 256, 128, n * 128)
    gateup_tt = k.tcnt()
    wd_free = None
    wdt_toks = []
    for m in range(KC):
        wl = k.dma("sync", ar.f(WDS, 128, 2816),
                   dr["L0_ffn_w_down"][m * 128:(m + 1) * 128, :],
                   waits=([wd_free] if wd_free else [gateup_tt] + sp_toks))
        k.wait("vector", wl)
        rm = k.sig("vector", lambda e: e.tensor_reduce(
            out=wmx[:, 0:1], in_=ar.f(WDS, 128, 2816), axis=AX.X,
            op=ALU.max, apply_absolute_value=True))
        k.wait("vector", rm)
        rm2 = k.sig("vector", lambda e: e.tensor_scalar(
            out=wmx[:, 1:2], in0=wmx[:, 0:1], scalar1=1.0 / 127.0,
            scalar2=None, op0=ALU.mult))
        k.wait("vector", rm2)
        rm3 = k.sig("vector", lambda e: e.reciprocal(wmx[:, 2:3],
                                                     wmx[:, 1:2]))
        k.wait("vector", rm3)
        if m > 0:
            k.wait("vector", wdt_prev_tt)
        q = quant_magic(ar.f(WDS, 128, 2816), ar.b(WDQ, 128, 2816),
                        wmx[:, 1:2], wmx[:, 2:3], ar.f(WDS, 128, 2816))
        wd_free = q
        pfb = st.pbf[0]
        k.wait("tensor", pfb.free)
        k.wait("tensor", q)
        base = 0
        for ii in range(NI):
            grp = ii % 8
            if grp == 0 and ii > 0:
                tt = k.tcnt()
                k.wait("scalar", tt)
                ev = k.sig("scalar", lambda e, m=m, base=base: e.copy(
                    WDTg(m, base, 8), pfb.ap[:, 0:1024]))
                wdt_toks.append(ev)
                k.wait("tensor", ev)
                base = ii
            k.op("tensor", lambda e, m=m, ii=ii, grp=grp: e.transpose(
                pfb.ap[:, grp * 128:(grp + 1) * 128],
                ar.b(WDQ, 128, 2816)[:, ii * 128:(ii + 1) * 128],
                st.CONID), inc=("t", 1))
        wdt_prev_tt = k.tcnt()
        k.wait("scalar", wdt_prev_tt)
        nrem = NI % 8
        ev = k.sig("scalar", lambda e, m=m, base=base, nrem=nrem: e.copy(
            WDTg(m, base, nrem), pfb.ap[:, 0:nrem * 128]))
        wdt_toks.append(ev)
        pfb.free = ev

    HSB = [O_CS, O_CS + 1024]
    HQF = [O_CS + 2048, O_CS + 4096]
    HQB = [O_CS + 6144, O_CS + 7168]
    pbf_f32 = [b.ap.bitcast(F32) for b in st.pbf]
    accb = list(st.pb[0:6]) + list(st.pbf[0:2])
    accs = [st.pb[j].ap[:, 0:TS] for j in range(6)] +            [pbf_f32[0][:, 0:TS], pbf_f32[1][:, 0:TS]]
    k.waits("tensor", wdt_toks)
    dbg_dump(st, "wdt_m0", ar.b(O_B, 128, NI * 128), wdt_toks)
    dbg_dump(st, "wdt_m4", ar.b(O_B + 4 * NI * 256, 128, NI * 128), wdt_toks)
    hq_free = [None, None]
    for s in range(NSL):
        for b in accb:
            k.wait("tensor", b.free)
        stop_tok = None
        for i in range(NI):
            p = i % 2
            ld = k.dma("gpsimd", ar.b(HSB[p], 128, TS),
                       st.spill_h2[i * 128:(i + 1) * 128,
                                   s * TS:(s + 1) * TS],
                       waits=([hq_free[p]] if hq_free[p] else sp_toks))
            qd = quant_magic(ar.b(HSB[p], 128, TS), ar.b(HQB[p], 128, TS),
                             sbc[:, 0:1], sbc[:, 1:2],
                             ar.f(HQF[p], 128, TS),
                             pre=[ld, bt2] +
                                 ([hq_free[p]] if hq_free[p] else []))
            if s == 0 and i == 0:
                dbg_dump(st, "hsb00", ar.b(HSB[p], 128, TS), [ld])
                dbg_dump(st, "hqb00", ar.b(HQB[p], 128, TS), [qd])
            k.wait("tensor", qd)
            for m in range(KC):
                tok = k.op("tensor", lambda e, m=m, i=i, p=p: e.matmul(
                    accs[m], WDT(m, i), ar.b(HQB[p], 128, TS),
                    start=(i == 0), stop=(i == NI - 1)),
                    inc=("t", 1))
                if i == NI - 1:
                    stop_tok = tok
            hq_free[p] = k.tcnt()
        k.wait("vector", stop_tok)
        if s == 0:
            dbg_dump(st, "rpre0", st.R[:, 0:TS], [st.r_tok[0]])
        for m in range(KC):
            k.waits("vector", [st.r_tok[m]] + st.r_readers[m])
            st.r_readers[m] = []
            tsl = slice(s * TS, (s + 1) * TS)
            st.r_tok[m] = k.sig("vector", lambda e, m=m, tsl=tsl, idx=m:
                                e.tensor_tensor(
                out=st.R[:, m * T:(m + 1) * T][:, tsl],
                in0=st.R[:, m * T:(m + 1) * T][:, tsl],
                in1=accs[idx], op=ALU.add))
            accb[m].free = st.r_tok[m]
        if s == 0:
            dbg_dump(st, "rpost0", st.R[:, 0:TS], [st.r_tok[0]])
    st.l0ffn_out = k.vcnt()



def emit_l1_attn(st):
    k = st.k
    ar = st.ar
    dr = st.dr
    ln2_at = lambda c: ar.b(O_A + c * 4096, 128, T)
    ln_toks = emit_norm(st, lambda c: st.R[:, c * T:(c + 1) * T], ln2_at,
                        "ln", 1e-5, tr_off=O_CS + 4096, bc_off=O_B,
                        src_toks=[t for t in st.r_tok if t])
    for c in range(KC):
        st.r_readers[c].append(ln_toks[c])
    dbg_dump(st, "ln2_0", ln2_at(0), [ln_toks[0]])
    STG = [O_C + 16384, O_C + 17408]
    if not hasattr(st, "stg_free"):
        st.stg_free = [None, None]
    sp_toks = []

    def evac_qkv(mco, s, bank, mmtok):
        slot = (mco * NSL + s) % 2
        stg = ar.b(STG[slot], 128, TS)
        k.wait("vector", mmtok)
        if st.stg_free[slot]:
            k.wait("vector", st.stg_free[slot])
        vt = k.sig("vector", lambda e, stg=stg, bank=bank, mco=mco:
                   e.tensor_scalar(
                       out=stg, in0=bank.ap[:, 0:TS],
                       scalar1=sv(st, "L1_attn_b_qkv", col=mco),
                       scalar2=None, op0=ALU.add))
        bank.free = vt
        d_ = k.dma("sync", st.spill_qkv[mco * 128:(mco + 1) * 128,
                                        s * TS:(s + 1) * TS], stg, waits=[vt])
        sp_toks.append(d_)
        st.stg_free[slot] = d_
    emit_proj(st, dr["L1_attn_w_qkv"], ln2_at, KC, 3 * D, evac_qkv,
              O_C, 16384, in_toks=list(ln_toks),
              w_waits=[t for t in st.r_tok if t])
    dbg_dump(st, "qkvs0", st.spill_qkv[0:128, :], sp_toks)
    OCTX = O_A
    OPT = O_C
    OQT = O_C + 16384
    OKT = O_C + 20480
    OVT = O_C + 24576
    OVN = O_C + 28672
    RDq = O_CS
    RDB = O_CS + 2048
    ctx_at = lambda h: ar.b_at(OCTX + h * 4096, 0, 64, T)
    ctx_toks = []
    qkv_free = None
    vn_read = None
    rd_read = None
    for h in range(16):
        lw = ([qkv_free] if qkv_free else sp_toks)
        tq = k.dma("gpsimd", ar.b_at(OQT, 0, 64, T),
                   st.spill_qkv[h * 64:(h + 1) * 64, :], waits=lw)
        tk = k.dma("gpsimd", ar.b_at(OKT, 0, 64, T),
                   st.spill_qkv[1024 + h * 64:1024 + (h + 1) * 64, :],
                   waits=lw)
        tv = k.dma("gpsimd", ar.b_at(OVT, 0, 64, T),
                   st.spill_qkv[2048 + h * 64:2048 + (h + 1) * 64, :],
                   waits=lw)
        if vn_read:
            k.wait("vector", vn_read)
        mz = k.sig("vector", lambda e: e.memset(ar.b(OVN, 128, 16 * 65), 1.0))
        pfb = st.pbf[0]
        ev_toks = []
        for rnd in range(2):
            k.wait("tensor", pfb.free)
            k.wait("tensor", tv)
            if qkv_free:
                k.wait("tensor", qkv_free)
            for j in range(8):
                kt = rnd * 8 + j
                k.op("tensor", lambda e, kt=kt, j=j: e.transpose(
                    pfb.ap[:, j * 64:(j + 1) * 64],
                    ar.b_at(OVT, 0, 64, T)[:, kt * 128:(kt + 1) * 128],
                    st.CONID[0:64, 0:64]), inc=("t", 1))
            tt = k.tcnt()
            k.wait("scalar", tt)
            k.wait("scalar", mz)
            for j in range(8):
                kt = rnd * 8 + j
                et = k.sig("scalar", lambda e, kt=kt, j=j: e.copy(
                    ar.b(OVN + kt * 130, 128, 64),
                    pfb.ap[:, j * 64:(j + 1) * 64]))
                ev_toks.append(et)
            pfb.free = ev_toks[-1]
        pt_read = None
        for qb in range(4):
            exp_toks = []
            for kt in range(16):
                bank = st.pb[kt % 2]
                k.wait("tensor", bank.free)
                k.waits("tensor", [tq, tk])
                mt = k.op("tensor", lambda e, kt=kt, qb=qb, bank=bank:
                          e.matmul(
                    bank.ap[:, 0:TS],
                    ar.b_at(OKT, 0, 64, T)[:, kt * 128:(kt + 1) * 128],
                    ar.b_at(OQT, 0, 64, T)[:, qb * TS:(qb + 1) * TS],
                    start=True, stop=True), inc=("t", 1))
                k.wait("scalar", mt)
                if pt_read:
                    k.wait("scalar", pt_read)
                et = k.sig("scalar", lambda e, kt=kt, bank=bank: e.activation(
                    ar.b(OPT + kt * 1024, 128, TS), bank.ap[:, 0:TS],
                    AF.Exp, scale=0.125))
                bank.free = et
                exp_toks.append(et)
            pv = st.pb[2]
            k.wait("tensor", pv.free)
            k.waits("tensor", [ev_toks[-1], mz])
            for kt in range(16):
                k.wait("tensor", exp_toks[kt])
                k.op("tensor", lambda e, kt=kt, pv=pv: e.matmul(
                    pv.ap[0:65, 0:TS],
                    ar.b(OVN + kt * 130, 128, 65),
                    ar.b(OPT + kt * 1024, 128, TS),
                    start=(kt == 0), stop=(kt == 15)), inc=("t", 1))
            pvt = k.tcnt()
            pt_read = pvt
            k.wait("vector", pvt)
            if rd_read:
                k.wait("vector", rd_read)
            rd_f = ar.f32[64:65, RDq // 4:RDq // 4 + TS]
            rd_b = ar.bf[64:65, RDB // 2:RDB // 2 + TS]
            r1 = k.sig("vector", lambda e, pv=pv: e.reciprocal(
                rd_f, pv.ap[64:65, 0:TS]))
            k.wait("vector", r1)
            r2 = k.sig("vector", lambda e: e.tensor_copy(rd_b, rd_f))
            bcb = st.pb[3]
            k.wait("tensor", bcb.free)
            k.wait("tensor", r2)
            bt_ = k.op("tensor", lambda e, bcb=bcb: e.matmul(
                bcb.ap[:, 0:TS], st.CONER64, rd_b, start=True, stop=True),
                inc=("t", 1))
            rd_read = bt_
            bcba = ar.b(O_CS + 8192, 128, TS)
            k.wait("scalar", bt_)
            if ctx_toks:
                k.wait("scalar", ctx_toks[-1])
            bev = k.sig("scalar", lambda e, bcb=bcb: e.copy(
                bcba, bcb.ap[:, 0:TS]))
            bcb.free = bev
            k.wait("vector", bev)
            ct = k.sig("vector", lambda e, h=h, qb=qb, pv=pv:
                       e.tensor_tensor(
                out=ctx_at(h)[:, qb * TS:(qb + 1) * TS],
                in0=pv.ap[0:64, 0:TS], in1=bcba[0:64, 0:TS],
                op=ALU.mult))
            pv.free = ct
            ctx_toks.append(ct)
        qkv_free = k.tcnt()
        vn_read = ("t", k.cnt["t"])
    dbg_dump(st, "ctxh0", ctx_at(0), [ctx_toks[3]])
    OWF = O_C
    wf_toks = []
    for hh in range(16):
        wf_toks.append(k.dma("gpsimd",
                             ar.b_at(OWF + hh * 2048, 0, 64, 1024),
                             dr["L1_attn_w_fc"][hh * 64:(hh + 1) * 64, :],
                             waits=[qkv_free]))
    for mco in range(KC):
        for s in range(NSL):
            bank = st.pb[4 + (mco * NSL + s) % 2]
            k.wait("tensor", bank.free)
            k.waits("tensor", wf_toks + [ctx_toks[-1]])
            tok = None
            for hh in range(16):
                tok = k.op("tensor", lambda e, hh=hh, mco=mco, s=s, bank=bank:
                           e.matmul(
                    bank.ap[:, 0:TS],
                    ar.b_at(OWF + hh * 2048, 0, 64, 1024)[
                        :, mco * 128:(mco + 1) * 128],
                    ctx_at(hh)[:, s * TS:(s + 1) * TS],
                    start=(hh == 0), stop=(hh == 15)),
                    inc=(("t", 1) if hh == 15 else None))
            k.wait("vector", tok)
            k.waits("vector", [st.r_tok[mco]] + st.r_readers[mco])
            st.r_readers[mco] = []
            tsl = slice(s * TS, (s + 1) * TS)
            st.r_tok[mco] = k.sig("vector", lambda e, mco=mco, tsl=tsl,
                                  bank=bank: e.scalar_tensor_tensor(
                out=st.R[:, mco * T:(mco + 1) * T][:, tsl],
                in0=bank.ap[:, 0:TS],
                scalar=sv(st, "L1_attn_b_fc", col=mco),
                in1=st.R[:, mco * T:(mco + 1) * T][:, tsl],
                op0=ALU.add, op1=ALU.add))
            bank.free = st.r_tok[mco]
    st.l1attn_done = k.vcnt()
    dbg_dump(st, "r3_0", st.R[:, 0:T], [st.r_tok[0]])


def emit_l1_ffn(st):
    k = st.k
    ar = st.ar
    dr = st.dr
    SP_EQ, SP_EK, SP_EV = st.spill_e[0], st.spill_e[1], st.spill_e[2]
    SP_QT, SP_KT, SP_VT = st.spill_ln0, st.spill_gate, st.spill_y1
    SP_N = st.spill_node
    SP_LN3 = st.spill_ln3
    in_at = lambda c: ar.b(O_A + c * 4096, 128, T)
    STG = [O_C + 16384, O_C + 17408]
    if not hasattr(st, "stg_free"):
        st.stg_free = [None, None]
    ewb = st.misc[:, 420:438]
    actb = st.misc[:, 438:444]
    sA = lambda c: st.misc[:, 448 + c:449 + c]
    sB = lambda c: st.misc[:, 456 + c:457 + c]
    sC = lambda c: st.misc[:, 420 + 3 * c:421 + 3 * c]
    d1 = k.dma("gpsimd", ewb, dr["L1_ffn_ew_row"][0:1, :]
               .partition_broadcast(128))
    d2 = k.dma("gpsimd", actb, dr["L1_ffn_act_w_row"][0:1, :]
               .partition_broadcast(128))
    k.waits("vector", [d1, d2])
    for c in range(6):
        k.sig("vector", lambda e, c=c: e.tensor_scalar(
            out=sA(c), in0=ewb[:, 3 * c:3 * c + 1],
            scalar1=ewb[:, 3 * c + 1:3 * c + 2], op0=ALU.mult,
            scalar2=float(1.0 / np.sqrt(128.0)), op1=ALU.mult))
        k.sig("vector", lambda e, c=c: e.tensor_scalar(
            out=sB(c), in0=ewb[:, 3 * c + 2:3 * c + 3],
            scalar1=actb[:, c:c + 1], scalar2=None, op0=ALU.mult))
    scal_tok = k.vcnt()
    ln_toks = emit_norm(st, lambda c: st.R[:, c * T:(c + 1) * T], in_at,
                        "ln", 1e-5, tr_off=O_CS + 4096, bc_off=O_B,
                        src_toks=[t for t in st.r_tok if t])
    for c in range(KC):
        st.r_readers[c].append(ln_toks[c])
    ln3_sp = []
    for c in range(KC):
        ln3_sp.append(k.dma("sync", SP_LN3[c * 128:(c + 1) * 128, :],
                            in_at(c), waits=[ln_toks[c]]))
    dbg_dump(st, "ln3_0", in_at(0), [ln_toks[0]])
    st.in_toks = list(ln_toks)
    st.in_read = None
    st.comb_read = None
    node_sp = {}

    def get(i):
        if i == -1:
            return SP_LN3, ln3_sp
        return SP_N[i], node_sp[i]

    def load_input(src_sp, src_toks):
        toks = []
        for c in range(KC):
            toks.append(k.dma("gpsimd", in_at(c),
                              src_sp[c * 128:(c + 1) * 128, :],
                              waits=([st.in_read] if st.in_read else []) +
                              ([st.comb_read] if st.comb_read else []) +
                              list(src_toks)))
        st.in_toks = toks

    def proj_to_spill(wname, dst_sp, bias_e=None):
        sp_toks2 = []

        def evac(mco, s, bank, mmtok):
            slot = (mco * NSL + s) % 2
            stg = ar.b(STG[slot], 128, TS)
            k.wait("vector", mmtok)
            if st.stg_free[slot]:
                k.wait("vector", st.stg_free[slot])
            if bias_e is not None:
                vt = k.sig("vector", lambda e, stg=stg, bank=bank, mco=mco:
                           e.tensor_scalar(
                    out=stg, in0=bank.ap[:, 0:TS],
                    scalar1=sv(st, "L1_ffn_be", col=bias_e * 8 + mco),
                    scalar2=None, op0=ALU.add))
            else:
                vt = k.sig("vector", lambda e, stg=stg, bank=bank:
                           e.tensor_copy(stg, bank.ap[:, 0:TS]))
            bank.free = vt
            d_ = k.dma("sync", dst_sp[mco * 128:(mco + 1) * 128,
                                      s * TS:(s + 1) * TS], stg, waits=[vt])
            sp_toks2.append(d_)
            st.stg_free[slot] = d_
        emit_proj(st, dr[wname], in_at, KC, D, evac, O_C, 16384,
                  in_toks=list(st.in_toks),
                  w_waits=([st.in_read] if st.in_read else []) +
                  [t for t in st.r_tok if t])
        st.in_read = k.tcnt()
        return sp_toks2

    OPT = O_C
    OQT = O_C + 16384
    OKT = O_C + 20480
    OVT = O_C + 24576
    OVN = O_C + 28672
    RDq = O_CS
    RDB = O_CS + 2048

    def node_attn(c, qt_sp, kt_sp, vt_sp):
        ctx_toks = []
        qkv_free = None
        rd_read = None
        for h in range(8):
            lw = ([qkv_free] if qkv_free else
                  ([st.in_read] if st.in_read else [])) + \
                 ([st.comb_read] if st.comb_read else [])
            tq = k.dma("gpsimd", ar.b(OQT, 128, T),
                       SP_QT[h * 128:(h + 1) * 128, :], waits=lw + qt_sp)
            tk = k.dma("gpsimd", ar.b(OKT, 128, T),
                       SP_KT[h * 128:(h + 1) * 128, :], waits=lw + kt_sp)
            tv = k.dma("gpsimd", ar.b(OVT, 128, T),
                       SP_VT[h * 128:(h + 1) * 128, :], waits=lw + vt_sp)
            pfb = st.pbf[0]
            ev_toks = []
            for rnd in range(2):
                k.wait("tensor", pfb.free)
                k.wait("tensor", tv)
                if qkv_free:
                    k.wait("tensor", qkv_free)
                for j in range(8):
                    kt = rnd * 8 + j
                    k.op("tensor", lambda e, kt=kt, j=j: e.transpose(
                        pfb.ap[:, j * 128:(j + 1) * 128],
                        ar.b(OVT, 128, T)[:, kt * 128:(kt + 1) * 128],
                        st.CONID), inc=("t", 1))
                tt = k.tcnt()
                k.wait("scalar", tt)
                et = k.sig("scalar", lambda e, rnd=rnd: e.copy(
                    ar.b(OVN + rnd * 2048, 128, 1024), pfb.ap[:, 0:1024]))
                ev_toks.append(et)
                pfb.free = et
            pt_read = None
            for qb in range(4):
                exp_toks = []
                for kt in range(16):
                    bank = st.pb[kt % 2]
                    k.wait("tensor", bank.free)
                    k.waits("tensor", [tq, tk])
                    mt = k.op("tensor", lambda e, kt=kt, qb=qb, bank=bank:
                              e.matmul(
                        bank.ap[:, 0:TS],
                        ar.b(OKT, 128, T)[:, kt * 128:(kt + 1) * 128],
                        ar.b(OQT, 128, T)[:, qb * TS:(qb + 1) * TS],
                        start=True, stop=True), inc=("t", 1))
                    k.wait("scalar", mt)
                    k.wait("scalar", scal_tok)
                    if pt_read:
                        k.wait("scalar", pt_read)
                    et = k.sig("scalar", lambda e, kt=kt, bank=bank, c=c:
                               e.activation(
                        ar.b(OPT + kt * 1024, 128, TS), bank.ap[:, 0:TS],
                        AF.Exp, scale=sA(c)))
                    bank.free = et
                    exp_toks.append(et)
                pv = st.pb[2]
                dnb = st.pb[3]
                k.wait("tensor", pv.free)
                k.wait("tensor", dnb.free)
                k.wait("tensor", ev_toks[-1])
                for kt in range(16):
                    k.wait("tensor", exp_toks[kt])
                    k.op("tensor", lambda e, kt=kt, pv=pv: e.matmul(
                        pv.ap[:, 0:TS],
                        ar.b(OVN + kt * 256, 128, 128),
                        ar.b(OPT + kt * 1024, 128, TS),
                        start=(kt == 0), stop=(kt == 15)), inc=("t", 1))
                    k.op("tensor", lambda e, kt=kt, dnb=dnb: e.matmul(
                        dnb.ap[0:1, 0:TS], st.CONEC,
                        ar.b(OPT + kt * 1024, 128, TS),
                        start=(kt == 0), stop=(kt == 15)), inc=("t", 1))
                pvt = k.tcnt()
                pt_read = pvt
                k.wait("vector", pvt)
                if rd_read:
                    k.wait("vector", rd_read)
                rd_f = ar.f32[0:1, RDq // 4:RDq // 4 + TS]
                rd_b = ar.bf[0:1, RDB // 2:RDB // 2 + TS]
                r1 = k.sig("vector", lambda e, dnb=dnb: e.reciprocal(
                    rd_f, dnb.ap[0:1, 0:TS]))
                k.wait("vector", r1)
                r2 = k.sig("vector", lambda e: e.tensor_copy(rd_b, rd_f))
                bcb = st.pb[4]
                k.wait("tensor", bcb.free)
                k.wait("tensor", r2)
                bt_ = k.op("tensor", lambda e, bcb=bcb: e.matmul(
                    bcb.ap[:, 0:TS], st.CONER, rd_b, start=True, stop=True),
                    inc=("t", 1))
                rd_read = bt_
                bcba = ar.b(O_CS + 8192, 128, TS)
                k.wait("scalar", bt_)
                if ctx_toks:
                    k.wait("scalar", ctx_toks[-1])
                bev = k.sig("scalar", lambda e, bcb=bcb: e.copy(
                    bcba, bcb.ap[:, 0:TS]))
                bcb.free = bev
                k.wait("vector", bev)
                k.wait("vector", st.in_read)
                ct = k.sig("vector", lambda e, h=h, qb=qb, pv=pv:
                           e.tensor_tensor(
                    out=in_at(h)[:, qb * TS:(qb + 1) * TS],
                    in0=pv.ap[:, 0:TS], in1=bcba[:, 0:TS], op=ALU.mult))
                pv.free = ct
                dnb.free = ct
                ctx_toks.append(ct)
            qkv_free = k.tcnt()
        st.in_toks = ctx_toks
        st.in_read = qkv_free
        return ctx_toks

    def wo_combine(c, eq_sp_toks):
        EQR = [O_C + 18432, O_C + 22528]
        eq_free = [None, None]
        eq_toks = {}
        tmpf = ar.f(O_CS + 4096, 128, TS)
        sp_toks2 = []

        def evac(mco, s, bank, mmtok):
            slot = mco % 2
            if s == 0:
                eq_toks[mco] = k.dma(
                    "gpsimd", ar.b(EQR[slot], 128, T),
                    SP_EQ[mco * 128:(mco + 1) * 128, :],
                    waits=([eq_free[slot]] if eq_free[slot] else
                           [st.in_read]) + eq_sp_toks)
            k.wait("vector", mmtok)
            k.wait("vector", eq_toks[mco])
            t1 = k.sig("vector", lambda e, bank=bank, c=c: e.tensor_scalar(
                out=tmpf, in0=bank.ap[:, 0:TS], scalar1=sB(c),
                scalar2=None, op0=ALU.mult))
            bank.free = t1
            k.wait("vector", t1)
            if c == 5:
                dst = ar.b(O_B + mco * 4096, 128, T)[:, s * TS:(s + 1) * TS]
                t2 = k.sig("vector", lambda e, dst=dst, slot=slot, s=s, c=c:
                           e.scalar_tensor_tensor(
                    out=dst, in0=ar.b(EQR[slot], 128, T)[:, s * TS:(s + 1) * TS],
                    scalar=sC(c), in1=tmpf, op0=ALU.mult, op1=ALU.add))
                sp_toks2.append(t2)
            else:
                slot2 = (mco * NSL + s) % 2
                stg = ar.b(STG[slot2], 128, TS)
                if st.stg_free[slot2]:
                    k.wait("vector", st.stg_free[slot2])
                t2 = k.sig("vector", lambda e, stg=stg, slot=slot, s=s, c=c:
                           e.scalar_tensor_tensor(
                    out=stg, in0=ar.b(EQR[slot], 128, T)[:, s * TS:(s + 1) * TS],
                    scalar=sC(c), in1=tmpf, op0=ALU.mult, op1=ALU.add))
                d_ = k.dma("sync", SP_N[c][mco * 128:(mco + 1) * 128,
                                           s * TS:(s + 1) * TS], stg,
                           waits=[t2])
                sp_toks2.append(d_)
                st.stg_free[slot2] = d_
            if s == NSL - 1:
                eq_free[slot] = t2
        emit_proj(st, dr[f"L1_ffn_Wo{c}"], in_at, KC, D, evac, O_C,
                  16384, in_toks=list(st.in_toks),
                  w_waits=([st.in_read] if st.in_read else []))
        st.in_read = k.tcnt()
        st.comb_read = k.vcnt()
        return sp_toks2

    SCHED = [
        (0, [("q", -1, 0)], "q", "q"),
        (1, [("q", 0, 2), ("k", -1, 1)], None, "q"),
        (2, [("q", 1, 5), ("k", -1, 3), ("v", 0, 4)], None, None),
        (3, [("q", 2, 9), ("k", -1, 6), ("v", 1, 8)], None, None),
        (4, [("q", 3, 14), ("k", -1, 10), ("v", 1, 12)], None, None),
        (5, [("q", 4, 19), ("k", 0, 15), ("v", 2, 17)], None, None),
    ]
    ESP = {"q": SP_EQ, "k": SP_EK, "v": SP_EV}
    for (c, projs, k_reuse, v_reuse) in SCHED:
        esp_toks = {}
        for (slot_name, src, e_idx) in projs:
            src_sp, src_toks = get(src)
            if not (c == 0 and slot_name == "q"):
                load_input(src_sp, src_toks)
            esp_toks[slot_name] = proj_to_spill(
                f"L1_ffn_We{e_idx}", ESP[slot_name], bias_e=e_idx)
        if k_reuse:
            esp_toks["k"] = esp_toks[k_reuse]
            ESPk = ESP[k_reuse]
        else:
            ESPk = ESP["k"]
        if v_reuse:
            esp_toks["v"] = esp_toks[v_reuse]
            ESPv = ESP[v_reuse]
        else:
            ESPv = ESP["v"]
        load_input(SP_EQ, esp_toks["q"])
        qt_sp = proj_to_spill(f"L1_ffn_Wq{c}", SP_QT)
        load_input(ESPk, esp_toks["k"])
        kt_sp = proj_to_spill(f"L1_ffn_Wk{c}", SP_KT)
        load_input(ESPv, esp_toks["v"])
        vt_sp = proj_to_spill(f"L1_ffn_Wv{c}", SP_VT)
        node_attn(c, qt_sp, kt_sp, vt_sp)
        node_sp[c] = wo_combine(c, esp_toks["q"])
        if c < 5:
            dbg_dump(st, f"n{c}_0", SP_N[c][0:128, :], node_sp[c])
    n5_at = lambda cc: ar.b(O_B + cc * 4096, 128, T)
    n5n_at = lambda cc: ar.b(O_A + cc * 4096, 128, T)
    k.waits("scalar", node_sp[5])
    nf_toks = emit_norm(st, n5_at, n5n_at, "ln", 1e-6,
                        tr_off=O_CS + 4096, bc_off=O_C,
                        gain_at=lambda cc: sv(st, "L1_ffn_ln_g", col=cc),
                        bias_at=lambda cc: sv(st, "L1_ffn_ln_b", col=cc),
                        src_toks=list(node_sp[5]) + [st.in_read] +
                        [t for t in st.stg_free if t])
    dbg_dump(st, "n5n_0", n5n_at(0), [nf_toks[0]])
    for cc in range(KC):
        k.waits("vector", [st.r_tok[cc], nf_toks[cc]] + st.r_readers[cc])
        st.r_readers[cc] = []
        st.r_tok[cc] = k.sig("vector", lambda e, cc=cc: e.tensor_tensor(
            out=st.R[:, cc * T:(cc + 1) * T],
            in0=st.R[:, cc * T:(cc + 1) * T],
            in1=n5n_at(cc), op=ALU.add))


def load_small(st):
    k = st.k
    dr = st.dr
    names = [
        ("L0_attn_norm_in_w", 128, KC),
        ("L0_attn_conv_w", 128, 4 * KC),
        ("L0_attn_conv_b", 128, KC),
        ("L0_attn_bau", 64, 1),
        ("L0_attn_bad", 128, 4),
        ("L0_attn_alpha_spike", 128, 4),
        ("L0_attn_bbeta", 8, 1),
        ("L0_attn_beta_spike", 128, 4),
        ("L0_attn_head_norm_w", 128, 4),
        ("L0_attn_bu1", 64, 1),
        ("L0_attn_bu2", 128, KC),
        ("L0_attn_ff_norm_w", 128, KC),
        ("L0_ffn_clip_x", 1, 1),
        ("L0_ffn_clip_down", 1, 1),
        ("L1_attn_b_qkv", 128, 24),
        ("L1_attn_b_fc", 128, KC),
        ("L1_ffn_ew", 18, 1),
        ("L1_ffn_act_w", 6, 1),
        ("L1_ffn_be", 128, 20 * KC),
        ("L1_ffn_ln_g", 128, KC),
        ("L1_ffn_ln_b", 128, KC),
    ]
    off = 0
    st.small = {}
    st.small_tok = {}
    for name, rows, cols in names:
        assert off + cols <= 4096
        ap = st.ROWS[0:rows, off:off + cols]
        st.small[name] = ap
        off += cols
        src = dr[name]
        if cols == 1:
            tok = k.dma("sync", ap, src[0:rows, 0:1])
        else:
            inner = src.shape[1]
            tok = k.dma("sync", ap.rearrange("p (c w) -> p c w", w=inner),
                        src.rearrange("(c p) w -> p c w", p=rows), slow=True)
        st.small_tok[name] = tok
    st.nr_sum = st.ROWS[0:1, off:off + TS]
    st.nr_ssq = st.ROWS[0:1, off + TS:off + 2 * TS]
    st.nr_a = st.ROWS[0:1, off + 2 * TS:off + 3 * TS]
    st.nr_t = st.ROWS[0:1, off + 3 * TS:off + 4 * TS]
    off += 4 * TS
    st.misc = st.ROWS[:, off:off + 512]
    off += 512
    st.ccrow = st.ROWS[0:1, off:off + 128]
    off += 128
    assert off <= 4096, off
    st.params_tok = list(st.small_tok.values())


def load_consts(st):
    k = st.k
    dr = st.dr
    C = st.con
    st.CONID = C.b(0, 128, 128)
    st.CONER = C.b(256, 1, 128)
    st.CONEC = C.b(512, 128, 1)
    st.CONBO8 = C.b(516, 128, 32)
    st.CONSEL = C.b(580, 8, 512)
    st.CONIS = C.b(1604, 128, 64)
    st.CONLT = C.f(1732, 128, 64)
    st.CONUT = C.f(1988, 128, 64)
    st.CONUI = C.f(2244, 128, 64)
    st.ZERO64 = C.f(2500, 128, 64)
    st.EPS5 = C.f(2756, 128, 1)
    st.EPS6 = C.f(2760, 128, 1)
    st.CONROPE = C.b(2764, 128, 1024)
    st.CONERF = C.f(4812, 1, 128)
    st.CONIDF = C.f(5324, 128, 128)
    st.CONER64 = C.bf[64:65, 5836 // 2:5836 // 2 + 128]
    toks = [
        k.dma("sync", st.CONID, dr["c_ident_bf"]),
        k.dma("sync", st.CONER, dr["c_onesrow_bf"]),
        k.dma("sync", st.CONEC, dr["c_onescol_bf"]),
        k.dma("sync", st.CONBO8, dr["c_bo8_bf"]),
        k.dma("sync", st.CONSEL, dr["c_headsel_bf"]),
        k.dma("sync", st.CONIS, dr["c_istack_bf"]),
        k.dma("sync", st.CONLT, dr["c_ltmask"]),
        k.dma("sync", st.CONUT, dr["c_utmask"]),
        k.dma("sync", st.CONUI, dr["c_uimask"]),
        k.dma("sync", st.CONROPE.rearrange("p (c o) -> p c o", o=128),
              dr["c_rope_bd_bf"].rearrange("(c p) o -> p c o", p=128)),
        k.dma("sync", st.CONERF, dr["c_onesrow_f"]),
        k.dma("sync", st.CONIDF, dr["c_ident_f"]),
        k.dma("sync", st.CONER64, dr["c_onesrow_bf"]),
    ]
    zt = k.sig("vector", lambda e: e.memset(st.ZERO64, 0.0))
    k.sig("vector", lambda e: e.memset(st.EPS5, 1e-5))
    zt = k.sig("vector", lambda e: e.memset(st.EPS6, 1e-6))
    st.const_tok = toks + [zt]
    for eng in ("tensor", "vector", "scalar"):
        k.waits(eng, toks)
        k.wait(eng, zt)


def emit_all(st):
    k = st.k
    load_consts(st)
    load_small(st)
    for eng in ("tensor", "vector", "scalar"):
        k.waits(eng, st.params_tok)
    for c in range(KC):
        st.r_tok[c] = k.dma("sync", st.R[:, c * T:(c + 1) * T],
                            st.dr["xT"][c * 128:(c + 1) * 128, :])
    emit_l0_attn(st)
    if st.debug_stage != 1:
        emit_l0_ffn(st)
        if st.debug_stage != 2:
            emit_l1_attn(st)
            if st.debug_stage != 3:
                emit_l1_ffn(st)
    toks = []
    for c in range(KC):
        k.waits("sync", [st.r_tok[c]] + st.r_readers[c])
        toks.append(k.dma("sync", st.out[c * 128:(c + 1) * 128, :],
                          st.R[:, c * T:(c + 1) * T]))
    k.waits("sync", toks)


def build_program(debug_stage=0):
    nc = bass.Bass("TRN2", target_bir_lowering=False, debug=False)
    dr = {}

    def din(name, shape, dtype=F32):
        dr[name] = nc.dram_tensor(name, list(shape), dtype,
                                  kind="ExternalInput").ap()

    din("xT", (D, T))
    din("L0_attn_norm_in_w", (D, 1))
    din("L0_attn_conv_w", (D, 4))
    din("L0_attn_conv_b", (D, 1))
    for w in ("Wq", "Wk", "Wv", "Wspike"):
        din("L0_attn_" + w, (D, 512))
    din("L0_attn_Wo", (512, D))
    din("L0_attn_Wau", (D, 64))
    din("L0_attn_bau", (64, 1))
    din("L0_attn_Wad", (64, 512))
    din("L0_attn_bad", (512, 1))
    din("L0_attn_alpha_spike", (512, 1))
    din("L0_attn_Wbeta", (D, 8))
    din("L0_attn_bbeta", (8, 1))
    din("L0_attn_beta_spike", (512, 1))
    din("L0_attn_head_norm_w", (512, 1))
    din("L0_attn_Wu1", (D, 64))
    din("L0_attn_bu1", (64, 1))
    din("L0_attn_Wu2", (64, D))
    din("L0_attn_bu2", (D, 1))
    din("L0_attn_ff_norm_w", (D, 1))
    din("L0_attn_Wff1", (D, 4096))
    din("L0_attn_Wff3", (D, 4096))
    din("L0_attn_Wff2", (4096, D))
    din("L0_ffn_clip_x", (1, 1))
    din("L0_ffn_clip_down", (1, 1))
    din("L0_ffn_w_gate", (2816, D))
    din("L0_ffn_w_up", (2816, D))
    din("L0_ffn_w_down", (D, 2816))
    din("L1_attn_w_qkv", (D, 3 * D))
    din("L1_attn_b_qkv", (3 * D, 1))
    din("L1_attn_w_fc", (D, D))
    din("L1_attn_b_fc", (D, 1))
    din("L1_ffn_ew", (18, 1))
    din("L1_ffn_act_w", (6, 1))
    din("L1_ffn_ew_row", (1, 18))
    din("L1_ffn_act_w_row", (1, 6))
    din("L1_ffn_be", (20 * D, 1))
    for e_ in range(20):
        din(f"L1_ffn_We{e_}", (D, D))
    for c_ in range(6):
        for w_ in ("Wq", "Wk", "Wv", "Wo"):
            din(f"L1_ffn_{w_}{c_}", (D, D))
    din("L1_ffn_ln_g", (D, 1))
    din("L1_ffn_ln_b", (D, 1))
    din("c_ident_bf", (128, 128), BF16)
    din("c_onesrow_bf", (1, 128), BF16)
    din("c_onescol_bf", (128, 1), BF16)
    din("c_headsel_bf", (8, 4 * 128), BF16)
    din("c_bo8_bf", (128, 32), BF16)
    din("c_ltmask", (128, CH))
    din("c_utmask", (128, CH))
    din("c_uimask", (128, CH))
    din("c_istack_bf", (128, CH), BF16)
    din("c_rope_bd_bf", (8 * 128, 128), BF16)
    din("c_onesrow_f", (1, 128))
    din("c_ident_f", (128, 128))

    st = St()
    st.nc = nc
    st.dr = dr
    st.out = nc.dram_tensor("out", [D, T], F32, kind="ExternalOutput").ap()
    st.spill_ln0 = nc.dram_tensor("spill_ln0", [D, T], BF16, kind="Internal").ap()
    st.spill_gate = nc.dram_tensor("spill_gate", [D, T], BF16, kind="Internal").ap()
    st.spill_r = nc.dram_tensor("spill_r", [D, T], F32, kind="Internal").ap()
    st.spill_y1 = nc.dram_tensor("spill_y1", [D, T], BF16, kind="Internal").ap()
    st.spill_h2 = nc.dram_tensor("spill_h2", [22 * 128, T], BF16,
                                 kind="Internal").ap()
    st.spill_qkv = nc.dram_tensor("spill_qkv", [3 * D, T], BF16,
                                  kind="Internal").ap()
    st.spill_ln3 = nc.dram_tensor("spill_ln3", [D, T], BF16,
                                  kind="Internal").ap()
    st.spill_node = [nc.dram_tensor(f"spill_node{i}", [D, T], BF16,
                                    kind="Internal").ap() for i in range(6)]
    st.spill_e = [nc.dram_tensor(f"spill_e{i}", [D, T], BF16,
                                 kind="Internal").ap() for i in range(3)]
    st.cc_in = nc.dram_tensor("cc_in", [1, 128], F32, kind="Internal")
    st.cc_out = nc.dram_tensor("cc_out", [1, 128], F32, kind="Internal")
    st.bscr = nc.dram_tensor("bscr", [1, 2], F32, kind="Internal").ap()
    st.rnd_i32 = nc.dram_tensor("rnd_i32", [128, T], I32, kind="Internal").ap()
    st.debug_stage = debug_stage

    es = contextlib.ExitStack()
    with es:
        R_t = es.enter_context(nc.sbuf_tensor("Rbuf", [128, KC * T], F32))
        AR_t = es.enter_context(nc.sbuf_tensor("Arena", [128, AR_BYTES // 2],
                                               BF16))
        ROWS_t = es.enter_context(nc.sbuf_tensor("Rows", [128, 4096], F32))
        CON_t = es.enter_context(nc.sbuf_tensor("Consts", [128, 4096], BF16))
        st.R = R_t.ap()
        st.ar = Arena(AR_t.ap(), AR_BYTES)
        st.ROWS = ROWS_t.ap()
        st.con = Arena(CON_t.ap(), 8192)
        pb = [es.enter_context(nc.psum_tensor(f"pb{i}", [128, 512], F32))
              for i in range(6)]
        pbf = [es.enter_context(nc.psum_tensor(f"pbf{i}", [128, 1024], BF16))
               for i in range(2)]
        st.pb = [Bank(p.ap()) for p in pb]
        st.pbf = [Bank(p.ap()) for p in pbf]
        sems = {n: es.enter_context(nc.semaphore(n))
                for n in ["dq", "dg", "da", "t", "v", "s", "g", "cc"]}
        st.sems = sems
        st.k = KB(nc, sems)
        st.r_tok = [None] * KC
        st.r_readers = [[] for _ in range(KC)]
        st.dbg_toks = []

        emit_all(st)

        k = st.k
        with nc.Block() as block:
            @block.sync
            def _(e):
                for f in k.ops["sync"]:
                    f(e)

            @block.gpsimd
            def _(e):
                for f in k.ops["gpsimd"]:
                    f(e)

            @block.scalar
            def _(e):
                for f in k.ops["scalar"]:
                    f(e)

            @block.vector
            def _(e):
                for f in k.ops["vector"]:
                    f(e)

            @block.tensor
            def _(e):
                for f in k.ops["tensor"]:
                    f(e)
    return nc


_CONSTS = None


def _prepare_weights(base):
    m = dict(base)

    def col(name):
        m[name] = np.ascontiguousarray(np.asarray(m[name],
                                                  dtype=np.float32).reshape(-1, 1))
    for name in ["L0_attn_norm_in_w", "L0_attn_conv_b", "L0_attn_bau",
                 "L0_attn_bad", "L0_attn_bbeta", "L0_attn_bu1", "L0_attn_bu2",
                 "L0_attn_ff_norm_w", "L1_attn_b_qkv", "L1_attn_b_fc",
                 "L1_ffn_ln_g", "L1_ffn_ln_b", "L1_ffn_act_w",
                 "L0_ffn_clip_x", "L0_ffn_clip_down", "L1_ffn_ew",
                 "L0_attn_alpha_spike", "L0_attn_beta_spike",
                 "L0_attn_head_norm_w", "L1_ffn_be"]:
        col(name)
    m["L0_attn_conv_w"] = np.ascontiguousarray(
        np.asarray(m["L0_attn_conv_w"], dtype=np.float32).reshape(D, 4))
    m["L1_ffn_ew_row"] = np.asarray(base["L1_ffn_ew"],
                                    np.float32).reshape(1, 18)
    m["L1_ffn_act_w_row"] = np.asarray(base["L1_ffn_act_w"],
                                       np.float32).reshape(1, 6)
    We3 = np.asarray(m.pop("L1_ffn_We"), np.float32).reshape(20, D, D)
    for e_ in range(20):
        m[f"L1_ffn_We{e_}"] = We3[e_]
    for w in ("Wq", "Wk", "Wv", "Wo"):
        W3 = np.asarray(m.pop("L1_ffn_" + w), np.float32).reshape(6, D, D)
        for c_ in range(6):
            m[f"L1_ffn_{w}{c_}"] = W3[c_]
    for name in list(m):
        m[name] = np.ascontiguousarray(np.asarray(m[name], dtype=np.float32))
    return m


def kernel(**inputs):
    global _CONSTS
    if _CONSTS is None:
        _CONSTS = _host_constants()
    x = np.asarray(inputs["L0_attn_x"])
    base = {n: v for n, v in inputs.items() if n != "L0_attn_x"}
    m = _prepare_weights(base)
    m.update(_CONSTS)
    in_maps = []
    for b in range(NCORES):
        d = dict(m)
        d["xT"] = np.ascontiguousarray(x[b].T)
        in_maps.append(d)
    nc = build_program(DEBUG_STAGE)
    res = run_bass_kernel_spmd(nc, in_maps, core_ids=list(range(NCORES)))
    outs = [np.asarray(res.results[b]["out"]).T for b in range(NCORES)]
    return np.ascontiguousarray(np.stack(outs, axis=0))


# version 04
# speedup vs baseline: 1.0906x; 1.0906x over previous
import contextlib
import numpy as np
import ml_dtypes
import concourse.bass as bass
import concourse.mybir as mybir
from concourse.bass_utils import run_bass_kernel_spmd

F32 = mybir.dt.float32
BF16 = mybir.dt.bfloat16
I32 = mybir.dt.int32
AF = mybir.ActivationFunctionType
ALU = mybir.AluOpType
AX = mybir.AxisListType
BF = ml_dtypes.bfloat16

T = 2048
D = 1024
KC = 8
TS = 512
NSL = 4
NCORES = 8
CH = 64
NCH = T // CH

DEBUG_STAGE = 0
DBG_TAGS = set()
CC_RANKS = None

ESEM = {"tensor": "t", "vector": "v", "scalar": "s", "gpsimd": "g"}


class KB:
    ENGINES = ("sync", "gpsimd", "scalar", "vector", "tensor")

    def __init__(self, nc, sems):
        self.nc = nc
        self.sems = sems
        self.cnt = {n: 0 for n in sems}
        self.ops = {e: [] for e in self.ENGINES}
        self.waited = {e: {} for e in self.ENGINES}

    def op(self, eng, fn, inc=None):
        if inc is not None:
            sem, n = inc
            self.cnt[sem] += n
            val = self.cnt[sem]
            h = self.sems[sem]

            def wrapped(e, fn=fn, h=h, n=n):
                fn(e).then_inc(h, n)
            self.ops[eng].append(wrapped)
            return (sem, val)
        self.ops[eng].append(fn)
        return None

    def sig(self, eng, fn):
        return self.op(eng, fn, inc=(ESEM[eng], 1))

    def wait(self, eng, tok):
        if tok is None:
            return
        sem, val = tok
        if self.waited[eng].get(sem, 0) >= val:
            return
        self.waited[eng][sem] = val
        h = self.sems[sem]
        self.ops[eng].append(lambda e, h=h, val=val: e.wait_ge(h, val))

    def waits(self, eng, toks):
        for t in toks:
            self.wait(eng, t)

    def dma(self, queue, out, in_, waits=(), slow=False):
        sem = {"sync": "dq", "gpsimd": "dg", "scalar": "da"}[queue]
        self.waits(queue, waits)
        return self.op(queue,
                       lambda e, o=out, i=in_: e.dma_start(
                           out=o, in_=i, allow_slow_non_contiguous=slow),
                       inc=(sem, 16))

    def tcnt(self):
        return ("t", self.cnt["t"])

    def vcnt(self):
        return ("v", self.cnt["v"])

    def scnt(self):
        return ("s", self.cnt["s"])


class Bank:
    __slots__ = ("ap", "free")

    def __init__(self, ap):
        self.ap = ap
        self.free = None


class Arena:
    def __init__(self, bf_ap, nbytes):
        self.bf = bf_ap
        self.f32 = bf_ap.bitcast(F32)
        self.nbytes = nbytes

    def b(self, off, rows, cols):
        assert off % 2 == 0 and off + 2 * cols <= self.nbytes, (off, cols)
        return self.bf[0:rows, off // 2: off // 2 + cols]

    def b_at(self, off, r0, rows, cols):
        assert off % 2 == 0 and off + 2 * cols <= self.nbytes, (off, cols)
        return self.bf[r0:r0 + rows, off // 2: off // 2 + cols]

    def f(self, off, rows, cols):
        assert off % 4 == 0 and off + 4 * cols <= self.nbytes, (off, cols)
        return self.f32[0:rows, off // 4: off // 4 + cols]


def _host_constants():
    c = {}
    c["c_ident_bf"] = np.eye(128, dtype=np.float32).astype(BF)
    c["c_onesrow_bf"] = np.ones((1, 128), dtype=np.float32).astype(BF)
    c["c_onescol_bf"] = np.ones((128, 1), dtype=np.float32).astype(BF)
    sel = np.zeros((8, 4 * 128), dtype=np.float32)
    for cc in range(4):
        for p in range(128):
            sel[cc * 2 + (p // 64), cc * 128 + p] = 1.0
    c["c_headsel_bf"] = np.ascontiguousarray(sel.astype(BF))
    bo8 = np.zeros((128, 32), dtype=np.float32)
    for cc in range(4):
        for p in range(128):
            bo8[p, cc * 8 + 2 * cc + (p // 64)] = 1.0
    c["c_bo8_bf"] = np.ascontiguousarray(bo8.astype(BF))
    lt = np.tril(np.ones((CH, CH), dtype=np.float32), -1)
    ut = np.triu(np.ones((CH, CH), dtype=np.float32), 1)
    ui = np.triu(np.ones((CH, CH), dtype=np.float32), 0)
    c["c_ltmask"] = np.ascontiguousarray(np.concatenate([lt, lt], axis=0))
    c["c_utmask"] = np.ascontiguousarray(np.concatenate([ut, ut], axis=0))
    c["c_uimask"] = np.ascontiguousarray(np.concatenate([ui, ui], axis=0))
    c["c_istack_bf"] = np.ascontiguousarray(
        np.concatenate([np.eye(CH, dtype=np.float32)] * 2, axis=0)).astype(BF)
    DL1 = 64
    inv_freq = 1.0 / (10000.0 ** (np.arange(0, DL1, 2, dtype=np.float32) / DL1))
    bd = np.zeros((8, 128, 128), dtype=np.float32)
    for ck in range(8):
        for hh in range(2):
            ang = (ck * 2 + hh) * inv_freq
            co, si = np.cos(ang), np.sin(ang)
            for j in range(32):
                b = hh * 64 + 2 * j
                bd[ck, b, b] = co[j]
                bd[ck, b + 1, b] = -si[j]
                bd[ck, b, b + 1] = si[j]
                bd[ck, b + 1, b + 1] = co[j]
    c["c_rope_bd_bf"] = np.ascontiguousarray(bd.reshape(8 * 128, 128)).astype(BF)
    c["c_onesrow_f"] = np.ones((1, 128), dtype=np.float32)
    c["c_ident_f"] = np.eye(128, dtype=np.float32)
    return c


class St:
    pass


def dbg_dump(st, tag, ap, toks=()):
    if tag not in DBG_TAGS:
        return
    k = st.k
    name = "dbg_" + tag
    dt = ap.dtype
    sh = list(ap.shape)
    dram = st.nc.dram_tensor(name, sh, dt, kind="ExternalOutput").ap()
    k.waits("sync", list(toks))
    for eng in ("tensor", "vector", "scalar", "gpsimd"):
        sem = ESEM[eng]
        k.wait("sync", (sem, k.cnt[sem]))
    tok = k.dma("sync", dram, ap)
    for eng in ("tensor", "vector", "scalar", "gpsimd"):
        k.wait(eng, tok)
    st.dbg_toks.append(tok)


def sv(st, name, col=None, rows=None, ncols=1):
    ap = st.small[name]
    if col is not None:
        ap = ap[:, col:col + ncols]
    if rows is not None:
        ap = ap[0:rows, :]
    return ap


def crow(st, r, ncols=T, rows=1):
    return st.scr[r:r + rows, 0:ncols]


def mm_raw(st, bank, mms, pre_waits=()):
    k = st.k
    k.wait("tensor", bank.free)
    k.waits("tensor", pre_waits)
    n = len(mms)
    tok = None
    for i, fn in enumerate(mms):
        last = i == n - 1
        tok = k.op("tensor",
                   lambda e, fn=fn, s=(i == 0), p=last: fn(e, s, p),
                   inc=(("t", 1) if last else None))
    return tok


def mm_group(st, bank, mms, consumer_eng, consumer_fn, pre_waits=(),
             consumer_waits=()):
    k = st.k
    tok = mm_raw(st, bank, mms, pre_waits)
    k.wait(consumer_eng, tok)
    k.waits(consumer_eng, consumer_waits)
    ctok = k.sig(consumer_eng, consumer_fn)
    bank.free = ctok
    return ctok


def stack_evac(st, bank, dst, mmtok):
    k = st.k
    k.wait("scalar", mmtok)
    k.sig("scalar", lambda e: e.copy(dst[:, 0:64], bank.ap[0:64, 0:64]))
    t2 = k.sig("scalar", lambda e: e.copy(dst[:, 64:128], bank.ap[64:128, 64:128]))
    bank.free = t2
    return t2


def wload_bf(st, dram, col0, ncols, arena_off, waits=(), kc_rows=None):
    nrows = dram.shape[0]
    if nrows < 128:
        dst = st.ar.b_at(arena_off, 0, nrows, ncols)
        return st.k.dma("gpsimd", dst, dram[:, col0:col0 + ncols], waits=waits)
    if kc_rows is None:
        kc_rows = nrows // 128
    dst = st.ar.b(arena_off, 128, kc_rows * ncols)
    src = dram.rearrange("(c p) o -> p c o", p=128)[:, :, col0:col0 + ncols]
    return st.k.dma("gpsimd", dst.rearrange("p (c o) -> p c o", o=ncols), src,
                    waits=waits)


def emit_proj(st, wdram, in_at, n_in, out_cols, evac, wslot, wslot_sz,
              in_toks=(), mcg=4, bank_ids=(4, 5), w_pre=None, nsl=NSL,
              tslice=TS, tok0=0, w_waits=()):
    k = st.k
    if not hasattr(st, "wslot_readers"):
        st.wslot_readers = {}
    ngroups = (out_cols + mcg * 128 - 1) // (mcg * 128)
    gsz = n_in * mcg * 128
    assert 2 * 2 * gsz <= wslot_sz, (gsz, wslot_sz)
    wread = [st.wslot_readers.get((wslot, 0)), st.wslot_readers.get((wslot, 1))]
    for g in range(ngroups):
        col0 = g * mcg * 128
        ncols = min(mcg * 128, out_cols - col0)
        slot = g % 2
        aoff = wslot + slot * 2 * gsz
        wt = wload_bf(st, wdram, col0, ncols, aoff,
                      waits=([wread[slot]] if wread[slot] else []) +
                      list(w_waits))
        wap = st.ar.b(aoff, 128, n_in * ncols)
        if w_pre is not None:
            wt = w_pre(col0, ncols, wap, wt)
        nmc = (ncols + 127) // 128
        for mc in range(nmc):
            mco = col0 // 128 + mc
            for s in range(nsl):
                bank = st.pb[bank_ids[(mc * nsl + s) % len(bank_ids)]]
                mms = []
                for kc in range(n_in):
                    def mk(kc=kc, wap=wap, bank=bank, ncols=ncols, mc=mc, s=s):
                        def f(e, st_, sp):
                            return e.matmul(
                                bank.ap[:, 0:tslice],
                                wap[:, kc * ncols + mc * 128:
                                    kc * ncols + mc * 128 + 128],
                                in_at(kc)[:, tok0 + s * tslice:
                                          tok0 + (s + 1) * tslice],
                                start=st_, stop=sp)
                        return f
                    mms.append(mk())
                mmtok = mm_raw(st, bank, mms,
                               pre_waits=[wt] + list(in_toks))
                evac(mco, s, bank, mmtok)
        wread[slot] = k.tcnt()
        st.wslot_readers[(wslot, slot)] = wread[slot]
    return k.tcnt()


def emit_norm(st, src_at, out_bf_at, kind, eps, tr_off, bc_off,
              gain_at=None, bias_at=None, src_toks=(), also_f32_at=None):
    k = st.k
    ar = st.ar
    eps_ap = st.EPS5 if eps == 1e-5 else st.EPS6
    tr = ar.b(tr_off, 128, TS)
    rowb = ar.b(tr_off + 2 * TS, 1, 2 * TS)
    a_bc = ar.f(bc_off, 128, T)
    need_mean = kind == "ln"
    if need_mean:
        b_bc = ar.f(bc_off + 4 * T, 128, T)
        tmp = ar.f(bc_off + 8 * T, 128, T)
    else:
        b_bc = None
        tmp = ar.f(bc_off + 4 * T, 128, T)
    sum_r = st.nr_sum
    ssq_r = st.nr_ssq
    a_r = st.nr_a
    t_r = st.nr_t
    k.waits("scalar", list(src_toks))
    bc_evs = []
    for s in range(NSL):
        cs = slice(s * TS, (s + 1) * TS)
        passes = ((("sum", AF.Copy, st.pb[0], sum_r),) if need_mean else ()) + \
                 (("ssq", AF.Square, st.pb[1], ssq_r),)
        for nm, fn, bank, row in passes:
            for c in range(KC):
                k.wait("scalar", k.tcnt())
                ct = k.sig("scalar", lambda e, c=c, fn=fn, cs=cs: e.activation(
                    tr, src_at(c)[:, cs], fn))
                if c == 0:
                    k.wait("tensor", bank.free)
                k.wait("tensor", ct)
                k.op("tensor", lambda e, b=bank, c=c: e.matmul(
                    b.ap[0:1, 0:TS], st.CONEC, tr,
                    start=(c == 0), stop=(c == KC - 1)),
                    inc=("t", 1))
            mmtok = k.tcnt()
            k.wait("scalar", mmtok)
            ev = k.sig("scalar", lambda e, b=bank, row=row: e.copy(
                row, b.ap[0:1, 0:TS]))
            bank.free = ev
            k.wait("vector", ev)
        if need_mean:
            k.sig("vector", lambda e: e.tensor_scalar(
                out=sum_r, in0=sum_r, scalar1=1.0 / D, scalar2=None,
                op0=ALU.mult))
            k.sig("vector", lambda e: e.tensor_tensor(
                out=t_r, in0=sum_r, in1=sum_r, op=ALU.mult))
            vt = k.sig("vector", lambda e: e.scalar_tensor_tensor(
                out=t_r, in0=ssq_r, scalar=1.0 / D, in1=t_r,
                op0=ALU.mult, op1=ALU.subtract))
        else:
            vt = k.sig("vector", lambda e: e.tensor_scalar(
                out=t_r, in0=ssq_r, scalar1=1.0 / D, scalar2=None,
                op0=ALU.mult))
        k.wait("scalar", vt)
        k.sig("scalar", lambda e: e.activation(
            t_r, t_r, AF.Ln, bias=eps_ap[0:1, :], scale=1.0))
        sq = k.sig("scalar", lambda e: e.activation(
            a_r, t_r, AF.Exp, scale=-0.5))
        k.wait("vector", sq)
        k.wait("vector", k.tcnt())
        rva = k.sig("vector", lambda e: e.tensor_copy(rowb[0:1, 0:TS], a_r))
        rvb = None
        if need_mean:
            k.sig("vector", lambda e: e.scalar_tensor_tensor(
                out=t_r, in0=sum_r, scalar=-1.0, in1=a_r,
                op0=ALU.mult, op1=ALU.mult))
            rvb = k.sig("vector", lambda e: e.tensor_copy(
                rowb[0:1, TS:2 * TS], t_r))
        srcs = [(0, a_bc, rva)] + ([(TS, b_bc, rvb)] if need_mean else [])
        for (coff, dstbc, rtok) in srcs:
            bank = st.pb[2]
            k.wait("tensor", bank.free)
            k.wait("tensor", rtok)
            mt = k.op("tensor", lambda e, b=bank, coff=coff: e.matmul(
                b.ap[:, 0:TS], st.CONER, rowb[0:1, coff:coff + TS],
                start=True, stop=True), inc=("t", 1))
            k.wait("scalar", mt)
            ev = k.sig("scalar", lambda e, b=bank, cs=cs, dstbc=dstbc: e.copy(
                dstbc[:, cs], b.ap[:, 0:TS]))
            bank.free = ev
            bc_evs.append(ev)
    out_toks = []
    k.waits("vector", bc_evs)
    for c in range(KC):
        if need_mean:
            k.sig("vector", lambda e, c=c: e.tensor_tensor(
                out=tmp, in0=src_at(c), in1=a_bc, op=ALU.mult))
            if gain_at is None:
                tk = k.sig("vector", lambda e, c=c: e.tensor_tensor(
                    out=out_bf_at(c), in0=tmp, in1=b_bc, op=ALU.add))
            else:
                k.sig("vector", lambda e, c=c: e.tensor_tensor(
                    out=tmp, in0=tmp, in1=b_bc, op=ALU.add))
                tk = k.sig("vector", lambda e, c=c: e.tensor_scalar(
                    out=out_bf_at(c), in0=tmp, scalar1=gain_at(c),
                    scalar2=(bias_at(c) if bias_at else None),
                    op0=ALU.mult,
                    op1=(ALU.add if bias_at else Ellipsis)))
        else:
            if gain_at is None:
                tk = k.sig("vector", lambda e, c=c: e.tensor_tensor(
                    out=out_bf_at(c), in0=src_at(c), in1=a_bc, op=ALU.mult))
            else:
                k.sig("vector", lambda e, c=c: e.tensor_tensor(
                    out=tmp, in0=src_at(c), in1=a_bc, op=ALU.mult))
                tk = k.sig("vector", lambda e, c=c: e.tensor_scalar(
                    out=out_bf_at(c), in0=tmp, scalar1=gain_at(c),
                    scalar2=None, op0=ALU.mult))
        out_toks.append(tk)
    return out_toks


O_A = 0
O_B = 32768
O_C = 65536
O_CS = 98304
AR_BYTES = 112 * 1024

O_LN0 = O_A
O_H = O_B
O_GATE = O_C
OQ = O_A
OKK = O_A + 16384
OAB = O_B
OSPK = O_B + 16384
OV = O_C
ODR = O_C + 16384


def qp(st, name, cc):
    base = {"q": OQ, "k": OKK, "v": OV, "dr": ODR, "ab": OAB, "spk": OSPK}[name]
    return st.ar.b(base + cc * 4096, 128, T)


def a_ln0(st, c):
    return st.ar.b(O_LN0 + c * 4096, 128, T)


def a_h(st, c):
    return st.ar.b(O_H + c * 4096, 128, T)


def a_gate(st, c):
    return st.ar.b(O_GATE + c * 4096, 128, T)


def emit_l0_attn(st):
    k = st.k

    ln_toks = emit_norm(
        st, lambda c: st.R[:, c * T:(c + 1) * T], lambda c: a_ln0(st, c),
        "ln", 1e-5, tr_off=O_B, bc_off=O_C,
        src_toks=[t for t in st.r_tok if t])
    for c in range(KC):
        dbg_dump(st, f"ln0_{c}", a_ln0(st, c), [ln_toks[c]])
    emit_rms_conv_h(st, ln_toks)
    for c in range(KC):
        dbg_dump(st, f"h_{c}", a_h(st, c), [st.h_tok[c]])
    gtoks = emit_gate(st)
    for c in range(KC):
        dbg_dump(st, f"gate_{c}", a_gate(st, c), [gtoks[c]])
    dbg_dump(st, "gate_ps", a_gate(st, 0), [gtoks[0]])
    sp = []
    for c in range(KC):
        sp.append(k.dma("sync", st.spill_ln0[c * 128:(c + 1) * 128, :],
                        a_ln0(st, c), waits=[ln_toks[c]]))
        sp.append(k.dma("sync", st.spill_gate[c * 128:(c + 1) * 128, :],
                        a_gate(st, c), waits=[gtoks[c]]))
    st.l0_spill_toks = sp
    emit_l0_projections(st)
    for cc in range(4):
        dbg_dump(st, f"q_{cc}", qp(st, "q", cc), [st.qk_tok])
        dbg_dump(st, f"k_{cc}", qp(st, "k", cc), [st.qk_tok])
        dbg_dump(st, f"v_{cc}", qp(st, "v", cc), [st.v_tok])
        dbg_dump(st, f"drive_{cc}", qp(st, "dr", cc), [st.drive_tok])
        dbg_dump(st, f"ab_{cc}", qp(st, "ab", cc), [st.ab_tok])
    emit_l0_scan(st)
    for cc in range(4):
        dbg_dump(st, f"spk_{cc}", qp(st, "spk", cc), [st.scan_tok])
    emit_l0_decay(st)
    dbg_dump(st, "beta", st.ar.b_at(O_CS + 12288, 0, 8, T), [st.decay_tok])
    dbg_dump(st, "pc", st.misc[:, 8:136], [st.decay_tok])
    for cc in range(4):
        dbg_dump(st, f"P_{cc}", qp(st, "dr", cc), [st.decay_tok])
        dbg_dump(st, f"Pinv_{cc}", qp(st, "spk", cc), [st.decay_tok])
        dbg_dump(st, f"kap_{cc}", qp(st, "ab", cc), [st.decay_tok])
        dbg_dump(st, f"kt_{cc}", qp(st, "k", cc), [st.decay_tok])
        dbg_dump(st, f"qt_{cc}", qp(st, "q", cc), [st.decay_tok])
    emit_l0_deltanet(st)
    for cc in range(4):
        dbg_dump(st, f"yfm_{cc}", st.ar.b(OQ + cc * 4096, 128, T),
                 [st.y_tok])
    emit_l0_tail(st)


def emit_rms_conv_h(st, ln_toks):
    k = st.k
    ar = st.ar
    ssq_r, a_r, t_r = st.nr_ssq, st.nr_a, st.nr_t
    tr = ar.b(O_CS, 128, TS)
    rowb = ar.b(O_CS + 1024, 1, TS)
    a_bc = ar.f(O_CS + 4096, 128, T)
    k.waits("scalar", ln_toks)
    bc_evs = []
    for s in range(NSL):
        cs = slice(s * TS, (s + 1) * TS)
        bank = st.pb[1]
        for c in range(KC):
            k.wait("scalar", k.tcnt())
            ct = k.sig("scalar", lambda e, c=c, cs=cs: e.activation(
                tr, a_ln0(st, c)[:, cs], AF.Square))
            if c == 0:
                k.wait("tensor", bank.free)
            k.wait("tensor", ct)
            k.op("tensor", lambda e, b=bank, c=c: e.matmul(
                b.ap[0:1, 0:TS], st.CONEC, tr,
                start=(c == 0), stop=(c == KC - 1)),
                inc=("t", 1))
        mmtok = k.tcnt()
        k.wait("scalar", mmtok)
        ev = k.sig("scalar", lambda e, b=bank: e.copy(ssq_r, b.ap[0:1, 0:TS]))
        bank.free = ev
        k.wait("vector", ev)
        vt = k.sig("vector", lambda e: e.tensor_scalar(
            out=t_r, in0=ssq_r, scalar1=1.0 / D, scalar2=None, op0=ALU.mult))
        k.wait("scalar", vt)
        k.sig("scalar", lambda e: e.activation(
            t_r, t_r, AF.Ln, bias=st.EPS6[0:1, :], scale=1.0))
        sq = k.sig("scalar", lambda e: e.activation(
            a_r, t_r, AF.Exp, scale=-0.5))
        k.wait("vector", sq)
        k.wait("vector", k.tcnt())
        rva = k.sig("vector", lambda e: e.tensor_copy(rowb, a_r))
        bank = st.pb[2]
        k.wait("tensor", bank.free)
        k.wait("tensor", rva)
        mt = k.op("tensor", lambda e, b=bank: e.matmul(
            b.ap[:, 0:TS], st.CONER, rowb, start=True, stop=True),
            inc=("t", 1))
        k.wait("scalar", mt)
        ev = k.sig("scalar", lambda e, b=bank, cs=cs: e.copy(
            a_bc[:, cs], b.ap[:, 0:TS]))
        bank.free = ev
        bc_evs.append(ev)
    cs1 = ar.f(O_C, 128, T + 3)
    cs2 = ar.f(O_C + 4 * (T + 4), 128, T)
    k.sig("vector", lambda e: e.memset(cs1[:, 0:3], 0.0))
    k.waits("vector", bc_evs)
    dbg_dump(st, "rmsabc", a_bc, bc_evs)
    st.h_tok = []
    for c in range(KC):
        if st.h_tok:
            k.wait("vector", st.h_tok[-1])
        k.sig("vector", lambda e, c=c: e.tensor_tensor(
            out=cs2, in0=a_ln0(st, c), in1=a_bc, op=ALU.mult))
        k.sig("vector", lambda e, c=c: e.tensor_scalar(
            out=cs1[:, 3:3 + T], in0=cs2,
            scalar1=sv(st, "L0_attn_norm_in_w", col=c), scalar2=None,
            op0=ALU.mult))
        k.sig("vector", lambda e, c=c: e.tensor_scalar(
            out=cs2, in0=cs1[:, 0:T],
            scalar1=sv(st, "L0_attn_conv_w", col=4 * c), scalar2=None,
            op0=ALU.mult))
        for j in range(1, 4):
            k.sig("vector", lambda e, c=c, j=j: e.scalar_tensor_tensor(
                out=cs2, in0=cs1[:, j:j + T],
                scalar=sv(st, "L0_attn_conv_w", col=4 * c + j), in1=cs2,
                op0=ALU.mult, op1=ALU.add))
        vt2 = k.vcnt()
        if c == 0:
            dbg_dump(st, "cs1_0", cs1, [vt2])
            dbg_dump(st, "cs2_0", cs2, [vt2])
        k.wait("scalar", vt2)
        ht = k.sig("scalar", lambda e, c=c: e.activation(
            a_h(st, c), cs2, AF.Silu,
            bias=sv(st, "L0_attn_conv_b", col=c), scale=1.0))
        st.h_tok.append(ht)


def emit_gate(st):
    k = st.k
    ar = st.ar
    dr = st.dr
    wu1 = wload_bf(st, dr["L0_attn_Wu1"], 0, 64, O_CS + 2048,
                   waits=[st.h_tok[-1]])
    wu2 = wload_bf(st, dr["L0_attn_Wu2"], 0, D, O_CS + 4096, kc_rows=1,
                   waits=[st.h_tok[-1]])
    u = ar.b(O_CS + 8192, 64, T)
    for s in range(NSL):
        bank = st.pb[4]
        mms = []
        for kc in range(KC):
            def mk(kc=kc, s=s, bank=bank):
                def f(e, st_, sp):
                    return e.matmul(bank.ap[0:64, 0:TS],
                                    ar.b(O_CS + 2048 + kc * 128, 128, 64),
                                    a_ln0(st, kc)[:, s * TS:(s + 1) * TS],
                                    start=st_, stop=sp)
                return f
            mms.append(mk())
        mm_group(st, bank, mms, "scalar",
                 lambda e, s=s, bank=bank: e.activation(
                     u[:, s * TS:(s + 1) * TS], bank.ap[0:64, 0:TS], AF.Silu,
                     bias=sv(st, "L0_attn_bu1", rows=64), scale=1.0),
                 pre_waits=[wu1, st.h_tok[-1]])
    utok = k.scnt()
    gtoks = []
    for c in range(KC):
        tok = None
        for s in range(NSL):
            bank = st.pb[5]
            def mk(c=c, s=s, bank=bank):
                def f(e, st_, sp):
                    return e.matmul(
                        bank.ap[:, 0:TS],
                        ar.b(O_CS + 4096, 64, D)[:, c * 128:(c + 1) * 128],
                        u[:, s * TS:(s + 1) * TS], start=st_, stop=sp)
                return f
            tok = mm_group(st, bank, [mk()], "scalar",
                           lambda e, c=c, s=s, bank=bank: e.activation(
                               a_gate(st, c)[:, s * TS:(s + 1) * TS],
                               bank.ap[:, 0:TS], AF.Sigmoid,
                               bias=sv(st, "L0_attn_bu2", col=c), scale=1.0),
                           pre_waits=[wu2, utok])
        gtoks.append(tok)
    return gtoks


def emit_l0_projections(st):
    k = st.k
    ar = st.ar
    dr = st.dr
    h_at = lambda kc: a_h(st, kc)
    h_toks = list(st.h_tok)
    k.waits("gpsimd", st.l0_spill_toks)

    def evac_to(name):
        def evac(mco, s, bank, mmtok):
            k.wait("scalar", mmtok)
            ev = k.sig("scalar", lambda e, mco=mco, s=s, bank=bank: e.copy(
                qp(st, name, mco)[:, s * TS:(s + 1) * TS], bank.ap[:, 0:TS]))
            bank.free = ev
        return evac

    WSL = O_CS
    emit_proj(st, dr["L0_attn_Wspike"], h_at, KC, 512, evac_to("dr"),
              WSL, 16384, in_toks=h_toks)
    st.drive_tok = k.scnt()
    emit_proj(st, dr["L0_attn_Wq"], h_at, KC, 512, evac_to("q"),
              WSL, 16384, in_toks=h_toks)
    st.q_tok = k.scnt()
    emit_proj(st, dr["L0_attn_Wk"], h_at, KC, 512, evac_to("k"),
              WSL, 16384, in_toks=h_toks)
    st.k_tok = k.scnt()
    emit_proj(st, dr["L0_attn_Wv"], h_at, KC, 512, evac_to("v"),
              WSL, 16384, in_toks=h_toks)
    st.v_tok = k.scnt()

    wau = wload_bf(st, dr["L0_attn_Wau"], 0, 64, WSL, waits=[k.tcnt()])
    wad = wload_bf(st, dr["L0_attn_Wad"], 0, 512, WSL + 1024, kc_rows=1,
                   waits=[k.tcnt()])
    wbeta = wload_bf(st, dr["L0_attn_Wbeta"], 0, 8, WSL + 2048,
                     waits=[k.tcnt()])
    u2 = ar.b(WSL + 8192, 64, T)
    for s in range(NSL):
        bank = st.pb[4]
        mms = []
        for kc in range(KC):
            def mk(kc=kc, s=s, bank=bank):
                def f(e, st_, sp):
                    return e.matmul(bank.ap[0:64, 0:TS],
                                    ar.b(WSL + kc * 128, 128, 64),
                                    h_at(kc)[:, s * TS:(s + 1) * TS],
                                    start=st_, stop=sp)
                return f
            mms.append(mk())
        mm_group(st, bank, mms, "scalar",
                 lambda e, s=s, bank=bank: e.activation(
                     u2[:, s * TS:(s + 1) * TS], bank.ap[0:64, 0:TS], AF.Silu,
                     bias=sv(st, "L0_attn_bau", rows=64), scale=1.0),
                 pre_waits=[wau] + h_toks)
    u2tok = k.scnt()
    beta_rows = ar.b_at(O_CS + 12288, 0, 8, T)
    for s in range(NSL):
        bank = st.pb[5]
        mms = []
        for kc in range(KC):
            def mk(kc=kc, s=s, bank=bank):
                def f(e, st_, sp):
                    return e.matmul(bank.ap[0:8, 0:TS],
                                    ar.b(WSL + 2048 + kc * 16, 128, 8),
                                    h_at(kc)[:, s * TS:(s + 1) * TS],
                                    start=st_, stop=sp)
                return f
            mms.append(mk())
        mm_group(st, bank, mms, "scalar",
                 lambda e, s=s, bank=bank: e.activation(
                     beta_rows[:, s * TS:(s + 1) * TS], bank.ap[0:8, 0:TS],
                     AF.Identity, bias=sv(st, "L0_attn_bbeta", rows=8),
                     scale=1.0),
                 pre_waits=[wbeta] + h_toks)
    st.betab_tok = k.scnt()
    for cc in range(4):
        for s in range(NSL):
            bank = st.pb[4]
            def mk(cc=cc, s=s, bank=bank):
                def f(e, st_, sp):
                    return e.matmul(
                        bank.ap[:, 0:TS],
                        ar.b(WSL + 1024, 64, 512)[:, cc * 128:(cc + 1) * 128],
                        u2[:, s * TS:(s + 1) * TS], start=st_, stop=sp)
                return f
            def ev(e, cc=cc, s=s, bank=bank):
                return e.copy(qp(st, "ab", cc)[:, s * TS:(s + 1) * TS],
                              bank.ap[:, 0:TS],
                              ) if False else e.activation(
                    qp(st, "ab", cc)[:, s * TS:(s + 1) * TS],
                    bank.ap[:, 0:TS], AF.Identity,
                    bias=sv(st, "L0_attn_bad", col=cc), scale=1.0)
            mm_group(st, bank, [mk()], "scalar", ev,
                     pre_waits=[wad, u2tok])
    st.ab_tok = k.scnt()
    emit_l0_qknorm(st)


def emit_l0_qknorm(st):
    k = st.k
    ar = st.ar
    sqb = ar.b(O_CS + 8192, 128, TS)
    rowsA = ar.f(O_CS, 8, T)[0:8, :]
    rab = ar.b_at(O_CS + 9216, 0, 8, TS)
    for name, tok in (("q", st.q_tok), ("k", st.k_tok)):
        for s in range(NSL):
            cs = slice(s * TS, (s + 1) * TS)
            bank = st.pb[0]
            for cc in range(4):
                k.waits("scalar", [tok, st.ab_tok])
                k.wait("scalar", k.tcnt())
                sqt = k.sig("scalar", lambda e, cc=cc, name=name, cs=cs:
                            e.activation(sqb, qp(st, name, cc)[:, cs],
                                         AF.Square))
                if cc == 0:
                    k.wait("tensor", bank.free)
                k.wait("tensor", sqt)
                k.op("tensor", lambda e, b=bank, cc=cc: e.matmul(
                    b.ap[0:8, 0:TS], st.CONBO8[:, cc * 8:(cc + 1) * 8],
                    sqb, start=(cc == 0), stop=(cc == 3)),
                    inc=("t", 1))
            mmt = k.tcnt()
            k.wait("scalar", mmt)
            k.sig("scalar", lambda e, b=bank, cs=cs: e.activation(
                rowsA[:, cs], b.ap[0:8, 0:TS], AF.Ln))
            ev = k.sig("scalar", lambda e, cs=cs: e.activation(
                rowsA[:, cs], rowsA[:, cs], AF.Exp, scale=0.5))
            bank.free = ev
        k.wait("vector", k.scnt())
        k.sig("vector", lambda e: e.tensor_scalar(
            out=rowsA, in0=rowsA, scalar1=1e-6, scalar2=None, op0=ALU.add))
        rv = k.sig("vector", lambda e: e.reciprocal(rowsA, rowsA))
        for s in range(NSL):
            cs = slice(s * TS, (s + 1) * TS)
            k.wait("vector", rv)
            k.wait("vector", k.tcnt())
            rbt = k.sig("vector", lambda e, cs=cs: e.tensor_copy(
                rab, rowsA[:, cs]))
            for cc in range(4):
                bank = st.pb[4 + (cc % 2)]
                k.wait("tensor", bank.free)
                k.wait("tensor", rbt)
                mt = k.op("tensor", lambda e, b=bank, cc=cc: e.matmul(
                    b.ap[:, 0:TS], st.CONSEL[:, cc * 128:(cc + 1) * 128],
                    rab, start=True, stop=True), inc=("t", 1))
                k.wait("vector", mt)
                ct = k.sig("vector", lambda e, b=bank, cc=cc, cs=cs, name=name:
                           e.tensor_tensor(
                               out=qp(st, name, cc)[:, cs],
                               in0=qp(st, name, cc)[:, cs],
                               in1=b.ap[:, 0:TS], op=ALU.mult))
                bank.free = ct
    st.qk_tok = k.vcnt()


def emit_l0_scan(st):
    k = st.k
    ar = st.ar
    mem = st.misc[:, 0:4]
    k.wait("vector", st.drive_tok)
    k.sig("vector", lambda e: e.memset(mem, 0.0))
    dr4 = ar.bf[0:128, ODR // 2: ODR // 2 + 4 * T].rearrange(
        "p (c t) -> p c t", t=T)
    sp4 = ar.bf[0:128, OSPK // 2: OSPK // 2 + 4 * T].rearrange(
        "p (c t) -> p c t", t=T)
    for t in range(T):
        t1 = k.sig("vector", lambda e, t=t: e.scalar_tensor_tensor(
            out=mem, in0=mem, scalar=0.9, in1=dr4[:, :, t:t + 1],
            op0=ALU.mult, op1=ALU.add))
        k.wait("vector", t1)
        t2 = k.sig("vector", lambda e, t=t: e.tensor_scalar(
            out=sp4[:, :, t:t + 1], in0=mem, scalar1=0.5, scalar2=None,
            op0=ALU.is_gt))
        k.wait("vector", t2)
        t3 = k.sig("vector", lambda e, t=t: e.scalar_tensor_tensor(
            out=mem, in0=sp4[:, :, t:t + 1], scalar=-0.5, in1=mem,
            op0=ALU.mult, op1=ALU.add))
        k.wait("vector", t3)
    st.scan_tok = t3


def emit_l0_decay(st):
    k = st.k
    ar = st.ar
    beta_rows = ar.b_at(O_CS + 12288, 0, 8, T)
    actbf = ar.b_at(O_CS, 0, 8, T)
    tmp = ar.b(O_CS + 4096, 128, T)
    k.wait("vector", st.scan_tok)
    for s in range(NSL):
        cs = slice(s * TS, (s + 1) * TS)
        bank = st.pb[0]
        for cc in range(4):
            if cc == 0:
                k.wait("tensor", bank.free)
            k.wait("tensor", st.scan_tok)
            k.op("tensor", lambda e, b=bank, cc=cc, cs=cs: e.matmul(
                b.ap[0:8, 0:TS], st.CONBO8[:, cc * 8:(cc + 1) * 8],
                qp(st, "spk", cc)[:, cs], start=(cc == 0), stop=(cc == 3)),
                inc=("t", 1))
        mmt = k.tcnt()
        k.wait("vector", mmt)
        ev = k.sig("vector", lambda e, b=bank, cs=cs: e.tensor_scalar(
            out=actbf[:, cs], in0=b.ap[0:8, 0:TS],
            scalar1=0.0, scalar2=None, op0=ALU.is_gt))
        bank.free = ev
    abf = k.vcnt()
    k.wait("vector", st.betab_tok)
    for s in range(NSL):
        cs = slice(s * TS, (s + 1) * TS)
        bank = st.pb[1]
        for cc in range(4):
            k.wait("vector", k.tcnt())
            bspt = k.sig("vector", lambda e, cc=cc, cs=cs: e.tensor_scalar(
                out=tmp[:, 0:TS], in0=qp(st, "spk", cc)[:, cs],
                scalar1=sv(st, "L0_attn_beta_spike", col=cc), scalar2=None,
                op0=ALU.mult))
            if cc == 0:
                k.wait("tensor", bank.free)
            k.wait("tensor", bspt)
            k.op("tensor", lambda e, b=bank, cc=cc: e.matmul(
                b.ap[0:8, 0:TS], st.CONBO8[:, cc * 8:(cc + 1) * 8],
                tmp[:, 0:TS], start=(cc == 0), stop=(cc == 3)),
                inc=("t", 1))
        mmt = k.tcnt()
        k.wait("vector", mmt)
        bv = k.sig("vector", lambda e, b=bank, cs=cs: e.tensor_tensor(
            out=beta_rows[:, cs], in0=beta_rows[:, cs],
            in1=b.ap[0:8, 0:TS], op=ALU.add))
        bank.free = bv
    k.wait("scalar", k.vcnt())
    bt = k.sig("scalar", lambda e: e.activation(beta_rows, beta_rows,
                                                AF.Sigmoid))
    k.wait("vector", bt)
    k.wait("vector", abf)
    betabf = k.sig("vector", lambda e: e.tensor_tensor(
        out=beta_rows, in0=beta_rows, in1=actbf, op=ALU.mult))
    dbg_dump(st, "actbf", actbf, [abf])

    la = ar.f(O_CS + 4096, 128, 1024)
    cum = ar.f(O_CS + 8192, 128, 1024)
    pc = st.misc[:, 8:136]
    for cc in range(4):
        for hh in range(2):
            hsl = slice(hh * 1024, (hh + 1) * 1024)
            k.waits("vector", [st.ab_tok, betabf])
            k.wait("vector", k.scnt())
            stt_t = k.sig("vector", lambda e, cc=cc, hsl=hsl:
                          e.scalar_tensor_tensor(
                out=la, in0=qp(st, "spk", cc)[:, hsl],
                scalar=sv(st, "L0_attn_alpha_spike", col=cc),
                in1=qp(st, "ab", cc)[:, hsl], op0=ALU.mult, op1=ALU.add))
            k.wait("scalar", stt_t)
            at = k.sig("scalar", lambda e: e.activation(la, la, AF.Sigmoid))
            for s2 in range(2):
                bank = st.pb[s2]
                k.wait("tensor", bank.free)
                k.wait("tensor", abf)
                mt = k.op("tensor", lambda e, b=bank, cc=cc, hh=hh, s2=s2:
                          e.matmul(
                    b.ap[:, 0:TS], st.CONSEL[:, cc * 128:(cc + 1) * 128],
                    actbf[:, hh * 1024 + s2 * TS: hh * 1024 + (s2 + 1) * TS],
                    start=True, stop=True), inc=("t", 1))
                k.wait("vector", mt)
                k.wait("vector", at)
                ssl = slice(s2 * TS, (s2 + 1) * TS)
                k.sig("vector", lambda e, ssl=ssl: e.tensor_scalar(
                    out=la[:, ssl], in0=la[:, ssl], scalar1=-1.0,
                    scalar2=None, op0=ALU.add))
                k.sig("vector", lambda e, b=bank, ssl=ssl: e.tensor_tensor(
                    out=la[:, ssl], in0=la[:, ssl], in1=b.ap[:, 0:TS],
                    op=ALU.mult))
                ct = k.sig("vector", lambda e, ssl=ssl: e.tensor_scalar(
                    out=la[:, ssl], in0=la[:, ssl], scalar1=1.0,
                    scalar2=None, op0=ALU.add))
                bank.free = ct
            if cc == 0 and hh == 0:
                dbg_dump(st, "laAF", la, [k.vcnt()])
            k.wait("scalar", k.vcnt())
            lt = k.sig("scalar", lambda e: e.activation(la, la, AF.Ln))
            k.wait("vector", lt)
            for ch in range(16):
                k.op("vector", lambda e, ch=ch: e.tensor_tensor_scan(
                    out=cum[:, ch * CH:(ch + 1) * CH],
                    data0=la[:, ch * CH:(ch + 1) * CH],
                    data1=st.ZERO64, initial=0.0, op0=ALU.add, op1=ALU.add))
            cvt = k.sig("vector", lambda e: e.tensor_copy(
                cum[:, 0:1], cum[:, 0:1]))
            if cc == 0 and hh == 0:
                dbg_dump(st, "cum0", cum, [cvt])
            k.wait("scalar", cvt)
            k.sig("scalar", lambda e, cc=cc, hsl=hsl: e.activation(
                qp(st, "dr", cc)[:, hsl], cum, AF.Exp))
            k.sig("scalar", lambda e, cc=cc, hsl=hsl: e.activation(
                qp(st, "spk", cc)[:, hsl], cum, AF.Exp, scale=-1.0))
            k.sig("scalar", lambda e, cc=cc, hh=hh: e.activation(
                pc[:, cc * 32 + hh * 16: cc * 32 + (hh + 1) * 16],
                cum.rearrange("p (ch c) -> p ch c", c=CH)[:, :, CH - 1:CH],
                AF.Exp))
    st.p_tok = k.scnt()
    for cc in range(4):
        k.waits("vector", [st.p_tok, st.qk_tok])
        for s in range(NSL):
            bank = st.pb[s]
            k.wait("tensor", bank.free)
            k.wait("tensor", betabf)
            mt = k.op("tensor", lambda e, b=bank, cc=cc, s=s: e.matmul(
                b.ap[:, 0:TS], st.CONSEL[:, cc * 128:(cc + 1) * 128],
                beta_rows[:, s * TS:(s + 1) * TS], start=True, stop=True),
                inc=("t", 1))
            k.wait("vector", mt)
            ssl = slice(s * TS, (s + 1) * TS)
            k.sig("vector", lambda e, cc=cc, ssl=ssl: e.tensor_tensor(
                out=qp(st, "ab", cc)[:, ssl], in0=qp(st, "k", cc)[:, ssl],
                in1=qp(st, "spk", cc)[:, ssl], op=ALU.mult))
            ct = k.sig("vector", lambda e, b=bank, cc=cc, ssl=ssl:
                       e.tensor_tensor(
                           out=qp(st, "ab", cc)[:, ssl],
                           in0=qp(st, "ab", cc)[:, ssl],
                           in1=b.ap[:, 0:TS], op=ALU.mult))
            bank.free = ct
        k.sig("vector", lambda e, cc=cc: e.tensor_tensor(
            out=qp(st, "k", cc), in0=qp(st, "k", cc), in1=qp(st, "dr", cc),
            op=ALU.mult))
        k.sig("vector", lambda e, cc=cc: e.tensor_tensor(
            out=qp(st, "q", cc), in0=qp(st, "q", cc), in1=qp(st, "dr", cc),
            op=ALU.mult))
    st.decay_tok = k.vcnt()


DN_TILES = ["X0", "B0", "AT", "N", "X2", "B2", "X4", "B4", "X8", "B8",
            "X16", "B16", "X32", "RHS", "U", "KTM", "VTM", "YTM", "KPC"]


def emit_l0_deltanet(st):
    k = st.k
    ar = st.ar
    OY = OQ
    S = st.misc[:, 136:200]
    pc = st.misc[:, 8:136]
    toff = {n: O_CS + i * 128 for i, n in enumerate(DN_TILES)}

    def tb(n):
        return ar.b(toff[n], 128, 64)

    kpc = tb("KPC")
    Sbf = ar.b(O_CS + len(DN_TILES) * 128, 128, 256)
    pbT, pbB, pbA, pbP, pbR, pbU = st.pb
    pf0, pf1 = st.pbf
    k.wait("vector", st.decay_tok)
    for pp in range(4):
        k.sig("vector", lambda e, pp=pp: e.memset(
            st.misc[:, 136 + pp * 64:200 + pp * 64], 0.0))
    zt = k.sig("vector", lambda e: e.memset(Sbf, 0.0))
    s_tok = [zt] * 4

    def two(bank, mkA, mkB, out_col=64):
        return mm_raw(st, bank, [
            lambda e, s, p, mkA=mkA: mkA(e),
            lambda e, s, p, mkB=mkB: mkB(e),
        ], pre_waits=[])

    for c in range(NCH):
        cols = slice(c * CH, (c + 1) * CH)
        for pp in range(4):
            kA = qp(st, "k", pp)[0:64, cols]
            kB = qp(st, "k", pp)[64:128, cols]
            qA = qp(st, "q", pp)[0:64, cols]
            qB = qp(st, "q", pp)[64:128, cols]
            xA = qp(st, "ab", pp)[0:64, cols]
            xB = qp(st, "ab", pp)[64:128, cols]

            def pair(bank, lA, rA, lB, rB, pre):
                k.wait("tensor", bank.free)
                k.waits("tensor", pre)
                k.op("tensor", lambda e, bank=bank, lA=lA, rA=rA: e.matmul(
                    bank.ap[0:64, 0:64], lA, rA, start=True, stop=True))
                return k.op("tensor",
                            lambda e, bank=bank, lB=lB, rB=rB: e.matmul(
                                bank.ap[64:128, 0:64], lB, rB,
                                start=True, stop=True), inc=("t", 1))

            mt = pair(pbT, kA, xA, kB, xB, [st.decay_tok])
            k.wait("vector", mt)
            x0t = k.sig("vector", lambda e: e.tensor_tensor(
                out=tb("X0"), in0=pbT.ap[:, 0:64], in1=st.CONLT,
                op=ALU.mult))
            pbT.free = x0t
            mt = pair(pbB, xA, kA, xB, kB, [])
            k.wait("vector", mt)
            b0t = k.sig("vector", lambda e: e.tensor_tensor(
                out=tb("B0"), in0=pbB.ap[:, 0:64], in1=st.CONUT,
                op=ALU.mult))
            pbB.free = b0t
            mt = pair(pbA, xA, qA, xB, qB, [])
            k.wait("vector", mt)
            att = k.sig("vector", lambda e: e.tensor_tensor(
                out=tb("AT"), in0=pbA.ap[:, 0:64], in1=st.CONUI,
                op=ALU.mult))
            pbA.free = att
            k.wait("vector", b0t)
            ntok = k.sig("vector", lambda e: e.tensor_tensor(
                out=tb("N"), in0=st.CONIS, in1=tb("B0"), op=ALU.subtract))
            xtoks = {"X0": x0t, "B0": b0t}
            prevs = ("X0", "B0")
            for xn, bn in (("X2", "B2"), ("X4", "B4"), ("X8", "B8"),
                           ("X16", "B16"), ("X32", None)):
                px, pbn = prevs
                mt = pair(pbP, tb(pbn)[0:64, :], tb(px)[0:64, :],
                          tb(pbn)[64:128, :], tb(px)[64:128, :],
                          [xtoks[px], xtoks[pbn]])
                k.wait("scalar", mt)
                xt = k.sig("scalar", lambda e, xn=xn: e.copy(
                    tb(xn), pbP.ap[:, 0:64]))
                pbP.free = xt
                xtoks[xn] = xt
                if bn is not None:
                    mt2 = pair(pbR, tb(px)[0:64, :], tb(pbn)[0:64, :],
                               tb(px)[64:128, :], tb(pbn)[64:128, :],
                               [xtoks[px], xtoks[pbn]])
                    k.wait("scalar", mt2)
                    bt2 = k.sig("scalar", lambda e, bn=bn: e.copy(
                        tb(bn), pbR.ap[:, 0:64]))
                    pbR.free = bt2
                    xtoks[bn] = bt2
                    prevs = (xn, bn)
            ctok = ntok
            for xn in ("X2", "X4", "X8", "X16", "X32"):
                k.wait("tensor", pbU.free)
                k.waits("tensor", [ctok, xtoks[xn]])
                k.op("tensor", lambda e, xn=xn: e.matmul(
                    pbU.ap[0:64, 0:64], tb(xn)[0:64, :], tb("N")[0:64, :],
                    start=True, stop=False))
                k.op("tensor", lambda e: e.matmul(
                    pbU.ap[0:64, 0:64], st.CONIS[0:64, :], tb("N")[0:64, :],
                    start=False, stop=True))
                k.op("tensor", lambda e, xn=xn: e.matmul(
                    pbU.ap[64:128, 0:64], tb(xn)[64:128, :],
                    tb("N")[64:128, :], start=True, stop=False))
                mt = k.op("tensor", lambda e: e.matmul(
                    pbU.ap[64:128, 0:64], st.CONIS[64:128, :],
                    tb("N")[64:128, :], start=False, stop=True),
                    inc=("t", 1))
                k.wait("scalar", mt)
                ctok = k.sig("scalar", lambda e: e.copy(tb("N"),
                                                        pbU.ap[:, 0:64]))
                pbU.free = ctok
            ntok = ctok
            if c == 0 and pp == 0:
                dbg_dump(st, "dn_X0", tb("X0"), [x0t])
                dbg_dump(st, "dn_B0", tb("B0"), [b0t])
                dbg_dump(st, "dn_AT", tb("AT"), [att])
                dbg_dump(st, "dn_N", tb("N"), [ntok])
            k.wait("vector", k.tcnt())
            kpt = k.sig("vector", lambda e, pp=pp, c=c, cols=cols:
                        e.tensor_scalar(
                out=kpc, in0=qp(st, "ab", pp)[:, cols],
                scalar1=pc[:, pp * 32 + c: pp * 32 + c + 1],
                scalar2=None, op0=ALU.mult))
            k.wait("tensor", pf0.free)
            k.wait("tensor", kpt)
            k.op("tensor", lambda e: e.transpose(
                pf0.ap[0:64, 0:64], kpc[0:64, :], st.CONID[0:64, 0:64]))
            tt = k.op("tensor", lambda e: e.transpose(
                pf0.ap[64:128, 0:64], kpc[64:128, :],
                st.CONID[64:128, 64:128]), inc=("t", 1))
            k.wait("scalar", tt)
            ktm = k.sig("scalar", lambda e: e.copy(tb("KTM"),
                                                   pf0.ap[:, 0:64]))
            pf0.free = ktm
            k.wait("tensor", pf1.free)
            k.wait("tensor", st.v_tok)
            k.op("tensor", lambda e, pp=pp, cols=cols: e.transpose(
                pf1.ap[0:64, 0:64], qp(st, "v", pp)[0:64, cols],
                st.CONID[0:64, 0:64]))
            tt2 = k.op("tensor", lambda e, pp=pp, cols=cols: e.transpose(
                pf1.ap[64:128, 0:64], qp(st, "v", pp)[64:128, cols],
                st.CONID[64:128, 64:128]), inc=("t", 1))
            k.wait("scalar", tt2)
            vtm = k.sig("scalar", lambda e: e.copy(tb("VTM"),
                                                   pf1.ap[:, 0:64]))
            pf1.free = vtm

            if c == 0 and pp == 0:
                dbg_dump(st, "dn_KPC", kpc, [kpt])
                dbg_dump(st, "dn_KTM", tb("KTM"), [ktm])
                dbg_dump(st, "dn_VTM", tb("VTM"), [vtm])
            Sp = st.misc[:, 136 + pp * 64:200 + pp * 64]
            Sbp = Sbf[:, pp * 64:(pp + 1) * 64]
            mt = pair(pbR, kA, Sbp[0:64, :], kB, Sbp[64:128, :],
                      [s_tok[pp], x0t])
            k.wait("vector", mt)
            k.wait("vector", vtm)
            rhst = k.sig("vector", lambda e: e.tensor_tensor(
                out=tb("RHS"), in0=tb("VTM"), in1=pbR.ap[:, 0:64],
                op=ALU.subtract))
            pbR.free = rhst
            if c == 0 and pp == 0:
                dbg_dump(st, "dn_RHS", tb("RHS"), [rhst])
            mt = pair(pbU, tb("N")[0:64, :], tb("RHS")[0:64, :],
                      tb("N")[64:128, :], tb("RHS")[64:128, :],
                      [ntok, rhst])
            k.wait("scalar", mt)
            utok = k.sig("scalar", lambda e: e.copy(tb("U"),
                                                    pbU.ap[:, 0:64]))
            pbU.free = utok
            k.wait("tensor", pbT.free)
            k.waits("tensor", [att, utok, s_tok[pp]])
            k.op("tensor", lambda e, qA=qA, Sbp=Sbp: e.matmul(
                pbT.ap[0:64, 0:64], qA, Sbp[0:64, :], start=True, stop=False))
            k.op("tensor", lambda e: e.matmul(
                pbT.ap[0:64, 0:64], tb("AT")[0:64, :], tb("U")[0:64, :],
                start=False, stop=True))
            k.op("tensor", lambda e, qB=qB, Sbp=Sbp: e.matmul(
                pbT.ap[64:128, 0:64], qB, Sbp[64:128, :],
                start=True, stop=False))
            ot = k.op("tensor", lambda e: e.matmul(
                pbT.ap[64:128, 0:64], tb("AT")[64:128, :],
                tb("U")[64:128, :], start=False, stop=True), inc=("t", 1))
            mt = pair(pbB, tb("KTM")[0:64, :], tb("U")[0:64, :],
                      tb("KTM")[64:128, :], tb("U")[64:128, :],
                      [ktm, utok])
            k.wait("vector", mt)
            k.sig("vector", lambda e, Sp=Sp, pp=pp, c=c:
                  e.scalar_tensor_tensor(
                      out=Sp, in0=Sp,
                      scalar=pc[:, pp * 32 + c: pp * 32 + c + 1],
                      in1=pbB.ap[:, 0:64], op0=ALU.mult, op1=ALU.add))
            k.wait("vector", k.vcnt())
            sup = k.sig("vector", lambda e, Sp=Sp, Sbp=Sbp: e.tensor_copy(
                Sbp, Sp))
            pbB.free = sup
            s_tok[pp] = sup
            sso = st.misc[:, 392:400]
            k.wait("scalar", ot)
            k.wait("scalar", k.tcnt())
            k.sig("scalar", lambda e: e.activation(
                tb("YTM"), pbT.ap[:, 0:64], AF.Square,
                accum_out=sso[:, 0:1]))
            lnt = k.sig("scalar", lambda e: e.activation(
                sso[:, 2:3], sso[:, 0:1], AF.Ln, bias=st.EPS6,
                scale=1.0 / 64))
            k.wait("scalar", lnt)
            sq3 = k.sig("scalar", lambda e: e.activation(
                sso[:, 2:3], sso[:, 2:3], AF.Exp, scale=-0.5))
            k.wait("vector", sq3)
            rc2 = k.sig("vector", lambda e: e.reciprocal(
                sso[:, 4:5], sso[:, 2:3])) if False else None
            yv = k.sig("vector", lambda e: e.tensor_scalar(
                out=tb("YTM"), in0=pbT.ap[:, 0:64],
                scalar1=sso[:, 2:3], scalar2=None, op0=ALU.mult))
            pbT.free = yv
            k.wait("tensor", pf0.free)
            k.wait("tensor", yv)
            k.op("tensor", lambda e: e.transpose(
                pf0.ap[0:64, 0:64], tb("YTM")[0:64, :], st.CONID[0:64, 0:64]))
            ty = k.op("tensor", lambda e: e.transpose(
                pf0.ap[64:128, 0:64], tb("YTM")[64:128, :],
                st.CONID[64:128, 64:128]), inc=("t", 1))
            k.wait("scalar", ty)
            yev = k.sig("scalar", lambda e, pp=pp, cols=cols: e.copy(
                ar.b(OY + pp * 4096, 128, T)[:, cols], pf0.ap[:, 0:64]))
            pf0.free = yev
            if pp == 0 and c in (0, 1):
                dbg_dump(st, f"dn_U_{c}", tb("U"), [utok])
                dbg_dump(st, f"dn_Y_{c}", tb("YTM"), [yv])
                dbg_dump(st, f"dn_S_{c}", Sp, [s_tok[pp]])
    st.y_tok = k.scnt()


def emit_l0_tail(st):
    k = st.k
    ar = st.ar
    dr = st.dr
    OY = OQ
    k.waits("sync", [st.y_tok] + st.l0_spill_toks)
    ln_toks = []
    g_toks = []
    for c in range(KC):
        ln_toks.append(k.dma("sync", ar.b(O_B + c * 4096, 128, T),
                             st.spill_ln0[c * 128:(c + 1) * 128, :]))
        g_toks.append(k.dma("sync", ar.b(O_C + c * 4096, 128, T),
                            st.spill_gate[c * 128:(c + 1) * 128, :]))
    lnr = lambda c: ar.b(O_B + c * 4096, 128, T)
    gtr = lambda c: ar.b(O_C + c * 4096, 128, T)
    tmpf = ar.f(O_CS, 128, TS)
    y1f = ar.f(O_CS + 2048, 128, TS)

    def w_pre(col0, ncols, wap, tok):
        k.wait("vector", tok)
        vt = None
        for kc in range(4):
            vt = k.sig("vector", lambda e, kc=kc, wap=wap, ncols=ncols:
                       e.tensor_scalar(
                           out=wap[:, kc * ncols:(kc + 1) * ncols],
                           in0=wap[:, kc * ncols:(kc + 1) * ncols],
                           scalar1=sv(st, "L0_attn_head_norm_w", col=kc),
                           scalar2=None, op0=ALU.mult))
        return vt

    y1sp = []

    def evac(mco, s, bank, mmtok):
        ssl = slice(s * TS, (s + 1) * TS)
        k.wait("vector", mmtok)
        k.waits("vector", [ln_toks[mco], g_toks[mco]])
        if y1sp:
            k.wait("vector", y1sp[-1])
        k.sig("vector", lambda e, bank=bank, mco=mco, ssl=ssl: e.tensor_tensor(
            out=tmpf, in0=bank.ap[:, 0:TS], in1=gtr(mco)[:, ssl],
            op=ALU.mult))
        vt = k.sig("vector", lambda e, mco=mco, ssl=ssl: e.tensor_tensor(
            out=y1f, in0=tmpf, in1=lnr(mco)[:, ssl], op=ALU.add))
        bank.free = vt
        k.waits("vector", [st.r_tok[mco]] + st.r_readers[mco])
        st.r_readers[mco] = []
        st.r_tok[mco] = k.sig("vector", lambda e, mco=mco, ssl=ssl:
                              e.tensor_tensor(
            out=st.R[:, mco * T:(mco + 1) * T][:, ssl],
            in0=st.R[:, mco * T:(mco + 1) * T][:, ssl],
            in1=y1f, op=ALU.add))
        y1sp.append(k.dma("gpsimd",
                          st.spill_y1[mco * 128:(mco + 1) * 128, ssl],
                          y1f, waits=[st.r_tok[mco]]))

    emit_proj(st, dr["L0_attn_Wo"], lambda cc: ar.b(OY + cc * 4096, 128, T),
              4, D, evac, O_CS + 4096, 16384, in_toks=[st.y_tok],
              w_pre=w_pre, w_waits=[st.decay_tok, st.y_tok])
    y1r = []
    k.waits("sync", y1sp)
    for c in range(KC):
        y1r.append(k.dma("sync", ar.b(O_A + c * 4096, 128, T),
                         st.spill_y1[c * 128:(c + 1) * 128, :]))
    y1_at = lambda c: ar.b(O_A + c * 4096, 128, T)
    z_at = lambda c: ar.b(O_C + c * 4096, 128, T)
    z_toks = emit_norm(st, y1_at, z_at, "rms", 1e-6,
                       tr_off=O_CS + 4096, bc_off=O_B,
                       gain_at=lambda c: sv(st, "L0_attn_ff_norm_w", col=c),
                       src_toks=y1r)
    for c in range(KC):
        dbg_dump(st, f"z_{c}", z_at(c), [z_toks[c]])
    WF1 = O_CS + 4096
    WF3 = O_CS + 8192
    WF2 = O_CS + 12288
    trsl = ar.b(O_CS, 128, TS)
    pbf_f32 = [b.ap.bitcast(F32) for b in st.pbf]
    zlast = z_toks[-1]
    f1read = [zlast, zlast]
    f3read = [zlast, zlast]
    f2read = [zlast, zlast]
    last_hs_tt = [None]
    hs_at = lambda kcp: ar.b(O_A + kcp * 2048, 128, 1024)
    for half in range(2):
        tok0 = half * 1024
        htoks = []
        for kcp in range(32):
            slot = kcp % 2
            w1 = wload_bf(st, dr["L0_attn_Wff1"], kcp * 128, 128,
                          WF1 + slot * 2048, waits=[f1read[slot]])
            w3 = wload_bf(st, dr["L0_attn_Wff3"], kcp * 128, 128,
                          WF3 + slot * 2048, waits=[f3read[slot]])
            w1ap = ar.b(WF1 + slot * 2048, 128, 1024)
            w3ap = ar.b(WF3 + slot * 2048, 128, 1024)
            for sl in range(2):
                tsl = slice(tok0 + sl * TS, tok0 + (sl + 1) * TS)
                mms1 = []
                mms3 = []
                for kc in range(KC):
                    def mk1(kc=kc, tsl=tsl, w1ap=w1ap):
                        def f(e, s_, p_):
                            return e.matmul(st.pb[4].ap[:, 0:TS],
                                            w1ap[:, kc * 128:(kc + 1) * 128],
                                            z_at(kc)[:, tsl], start=s_, stop=p_)
                        return f
                    def mk3(kc=kc, tsl=tsl, w3ap=w3ap):
                        def f(e, s_, p_):
                            return e.matmul(st.pb[5].ap[:, 0:TS],
                                            w3ap[:, kc * 128:(kc + 1) * 128],
                                            z_at(kc)[:, tsl], start=s_, stop=p_)
                        return f
                    mms1.append(mk1())
                    mms3.append(mk3())
                m1 = mm_raw(st, st.pb[4], mms1, pre_waits=[w1] + z_toks)
                k.wait("scalar", m1)
                if last_hs_tt[0] is not None:
                    k.wait("scalar", last_hs_tt[0])
                e1 = k.sig("scalar", lambda e: e.activation(
                    trsl, st.pb[4].ap[:, 0:TS], AF.Silu))
                st.pb[4].free = e1
                m3 = mm_raw(st, st.pb[5], mms3, pre_waits=[w3] + z_toks)
                k.wait("vector", m3)
                k.wait("vector", e1)
                e3 = k.sig("vector", lambda e, kcp=kcp, sl=sl: e.tensor_tensor(
                    out=hs_at(kcp)[:, sl * TS:(sl + 1) * TS],
                    in0=trsl, in1=st.pb[5].ap[:, 0:TS], op=ALU.mult))
                st.pb[5].free = e3
                last_hs_tt[0] = e3
                htoks.append(e3)
            f1read[slot] = k.tcnt()
            f3read[slot] = k.tcnt()
        for mcog in range(2):
            accb = st.pb[0:6] + st.pbf[0:2]
            accs = [st.pb[i].ap[:, 0:TS] for i in range(6)] + \
                   [pbf_f32[0][:, 0:TS], pbf_f32[1][:, 0:TS]]
            for b in accb:
                k.wait("tensor", b.free)
            k.waits("tensor", htoks)
            for kcp in range(32):
                slot = kcp % 2
                w2ap = ar.b(WF2 + slot * 2048, 128, 1024)
                w2 = k.dma("gpsimd", w2ap,
                           dr["L0_attn_Wff2"][kcp * 128:(kcp + 1) * 128, :],
                           waits=[f2read[slot]])
                k.wait("tensor", w2)
                for mc4 in range(4):
                    mco = mcog * 4 + mc4
                    for sl in range(2):
                        idx = mc4 * 2 + sl
                        k.op("tensor",
                             lambda e, w2ap=w2ap, mco=mco, sl=sl, idx=idx,
                             kcp=kcp, accs=accs: e.matmul(
                                 accs[idx],
                                 w2ap[:, mco * 128:(mco + 1) * 128],
                                 hs_at(kcp)[:, sl * TS:(sl + 1) * TS],
                                 start=(kcp == 0), stop=(kcp == 31)),
                             inc=("t", 1))
                f2read[slot] = k.tcnt()
            dntok = k.tcnt()
            k.wait("vector", dntok)
            for mc4 in range(4):
                mco = mcog * 4 + mc4
                for sl in range(2):
                    idx = mc4 * 2 + sl
                    tsl = slice(tok0 + sl * TS, tok0 + (sl + 1) * TS)
                    k.waits("vector", [st.r_tok[mco]] + st.r_readers[mco])
                    st.r_readers[mco] = []
                    st.r_tok[mco] = k.sig(
                        "vector", lambda e, mco=mco, tsl=tsl, idx=idx,
                        accs=accs: e.tensor_tensor(
                            out=st.R[:, mco * T:(mco + 1) * T][:, tsl],
                            in0=st.R[:, mco * T:(mco + 1) * T][:, tsl],
                            in1=accs[idx], op=ALU.add))
                    accb[idx].free = st.r_tok[mco]
        if half == 0:
            k.wait("vector", k.tcnt())
    st.l0ffn_done = k.vcnt()


def emit_l0_ffn(st):
    k = st.k
    ar = st.ar
    dr = st.dr
    NI = 22
    ln1_at = lambda c: ar.b(O_A + c * 4096, 128, T)
    ln_toks = emit_norm(st, lambda c: st.R[:, c * T:(c + 1) * T], ln1_at,
                        "ln", 1e-5, tr_off=O_CS + 4096, bc_off=O_B,
                        src_toks=[t for t in st.r_tok if t])
    for c in range(KC):
        st.r_readers[c].append(ln_toks[c])
    mx = st.misc[:, 400:404]
    srow = st.misc[0:1, 404:412]
    sbc = st.misc[:, 412:416]
    wmx = st.misc[:, 416:420]
    H2B = [O_B, O_B + 4096]
    WST = O_B + 8192
    WQB = O_B + 12288
    WQT = [O_B + 14336, O_B + 16384]
    GSIL = O_B + 18432
    VQF = O_B + 22528
    vqf = ar.f(VQF, 128, T)
    gsil = ar.b(GSIL, 128, T)

    def absmax_acc(src_ap, first, pre=()):
        k.waits("vector", list(pre))
        if first:
            t = k.sig("vector", lambda e: e.tensor_reduce(
                out=mx[:, 0:1], in_=src_ap, axis=AX.X, op=ALU.max,
                apply_absolute_value=True))
            k.wait("vector", t)
            return t
        t = k.sig("vector", lambda e: e.tensor_reduce(
            out=mx[:, 1:2], in_=src_ap, axis=AX.X, op=ALU.max,
            apply_absolute_value=True))
        k.wait("vector", t)
        t = k.sig("vector", lambda e: e.tensor_tensor(
            out=mx[:, 0:1], in0=mx[:, 0:1], in1=mx[:, 1:2], op=ALU.max))
        k.wait("vector", t)
        return t

    def absmax_finish(t):
        d1 = k.dma("gpsimd", st.cc_in.ap()[0:1, :], mx[:, 0:1], waits=[t])
        d2 = k.dma("gpsimd", st.ccrow, st.cc_in.ap()[0:1, :], waits=[d1])
        k.wait("vector", d2)
        gv = k.sig("vector", lambda e: e.tensor_reduce(
            out=srow[:, 0:1], in_=st.ccrow, axis=AX.X, op=ALU.max))
        d3 = k.dma("gpsimd", st.cc_in.ap()[0:1, 0:1], srow[:, 0:1],
                   waits=[gv])
        ranks = CC_RANKS if CC_RANKS is not None else list(range(NCORES))
        if len(ranks) == 1:
            return k.dma("gpsimd", srow[:, 0:1], st.cc_in.ap()[0:1, 0:1],
                         waits=[d3])
        k.wait("gpsimd", d3)
        cct = k.op("gpsimd", lambda e, ranks=ranks: e.collective_compute(
            "AllReduce", ALU.max,
            replica_groups=[ranks],
            ins=[st.cc_in.ap().opt()],
            outs=[st.cc_out.ap().opt()],
        ), inc=("cc", 1))
        k.wait("gpsimd", cct)
        return k.dma("gpsimd", srow[:, 0:1], st.cc_out.ap()[0:1, 0:1])

    def mk_scales(rt, clip_name):
        cv = k.sig("vector", lambda e: e.tensor_copy(
            srow[:, 1:2], sv(st, clip_name, rows=1)))
        d1 = k.dma("gpsimd", st.bscr[0:1, :], srow[:, 0:2], waits=[rt, cv])
        d2 = k.dma("gpsimd", sbc[:, 0:2],
                   st.bscr[0:1, :].partition_broadcast(128), waits=[d1])
        k.wait("scalar", d2)
        sg = k.sig("scalar", lambda e: e.activation(
            sbc[:, 1:2], sbc[:, 1:2], AF.Sigmoid))
        k.wait("vector", sg)
        m1 = k.sig("vector", lambda e: e.tensor_tensor(
            out=sbc[:, 2:3], in0=sbc[:, 0:1], in1=sbc[:, 1:2], op=ALU.mult))
        k.wait("vector", m1)
        vt = k.sig("vector", lambda e: e.tensor_scalar(
            out=sbc[:, 0:1], in0=sbc[:, 2:3], scalar1=1.0 / 127.0,
            scalar2=None, op0=ALU.mult))
        k.wait("vector", vt)
        bt = k.sig("vector", lambda e: e.reciprocal(sbc[:, 1:2],
                                                    sbc[:, 0:1]))
        k.wait("vector", bt)
        return bt

    RM = 12582912.0

    def quant_magic(src_ap, dst_ap, s_ap, rs_ap, vq_ap, pre=()):
        k.waits("vector", list(pre))
        t = k.sig("vector", lambda e: e.tensor_scalar(
            out=vq_ap, in0=src_ap, scalar1=rs_ap, scalar2=None,
            op0=ALU.mult))
        k.wait("vector", t)
        t = k.sig("vector", lambda e: e.tensor_scalar(
            out=vq_ap, in0=vq_ap, scalar1=RM, scalar2=None, op0=ALU.add))
        k.wait("vector", t)
        t = k.sig("vector", lambda e: e.tensor_scalar(
            out=vq_ap, in0=vq_ap, scalar1=RM, scalar2=-128.0,
            op0=ALU.subtract, op1=ALU.max))
        k.wait("vector", t)
        t = k.sig("vector", lambda e: e.tensor_scalar(
            out=vq_ap, in0=vq_ap, scalar1=127.0, scalar2=None, op0=ALU.min))
        k.wait("vector", t)
        t = k.sig("vector", lambda e: e.tensor_scalar(
            out=dst_ap, in0=vq_ap, scalar1=s_ap, scalar2=None,
            op0=ALU.mult))
        k.wait("vector", t)
        return t

    dbg_dump(st, "ln1_0", ln1_at(0), [ln_toks[0]])
    am = None
    for c in range(KC):
        am = absmax_acc(ln1_at(c), c == 0, pre=[ln_toks[c]])
    rt = absmax_finish(am)
    bt = mk_scales(rt, "L0_ffn_clip_x")
    dbg_dump(st, "scales1", sbc[:, 0:3], [bt])
    xq_toks = []
    for c in range(KC):
        xq_toks.append(quant_magic(ln1_at(c), ln1_at(c), sbc[:, 0:1],
                                   sbc[:, 1:2], vqf,
                                   pre=([bt] if c == 0 else ())))
    dbg_dump(st, "xq_0", ln1_at(0), [xq_toks[0]])

    wqt_free = [None, None]
    st.wst_free = ln_toks[-1]
    st.wqb_free = None

    def wquant_T(wname, i, slot):
        wl = k.dma("sync", ar.f(WST, 128, 1024),
                   dr[wname][i * 128:(i + 1) * 128, :],
                   waits=([st.wst_free] if st.wst_free else []))
        k.wait("vector", wl)
        rm = k.sig("vector", lambda e: e.tensor_reduce(
            out=wmx[:, 0:1], in_=ar.f(WST, 128, 1024), axis=AX.X,
            op=ALU.max, apply_absolute_value=True))
        k.wait("vector", rm)
        rm2 = k.sig("vector", lambda e: e.tensor_scalar(
            out=wmx[:, 1:2], in0=wmx[:, 0:1], scalar1=1.0 / 127.0,
            scalar2=None, op0=ALU.mult))
        k.wait("vector", rm2)
        rm3 = k.sig("vector", lambda e: e.reciprocal(wmx[:, 2:3],
                                                     wmx[:, 1:2]))
        k.wait("vector", rm3)
        if st.wqb_free is not None:
            k.wait("vector", st.wqb_free)
        qt = quant_magic(ar.f(WST, 128, 1024), ar.b(WQB, 128, 1024),
                         wmx[:, 1:2], wmx[:, 2:3],
                         ar.f(WST, 128, 1024))
        st.wst_free = qt
        pfb = st.pbf[0]
        k.wait("tensor", pfb.free)
        k.waits("tensor", [qt] + ([wqt_free[slot]] if wqt_free[slot] else []))
        for kc in range(KC):
            k.op("tensor", lambda e, kc=kc: e.transpose(
                pfb.ap[:, kc * 128:(kc + 1) * 128],
                ar.b(WQB, 128, 1024)[:, kc * 128:(kc + 1) * 128],
                st.CONID), inc=("t", 1))
        tt = k.tcnt()
        st.wqb_free = tt
        k.wait("scalar", tt)
        ev = k.sig("scalar", lambda e, slot=slot: e.copy(
            ar.b(WQT[slot], 128, 1024), pfb.ap[:, 0:1024]))
        pfb.free = ev
        return ev

    h2_free = [[], []]
    sp_toks = []
    am2 = None
    for i in range(NI):
        evg = wquant_T("L0_ffn_w_gate", i, 0)
        evu = wquant_T("L0_ffn_w_up", i, 1)
        p = i % 2
        h2b = ar.b(H2B[p], 128, T)
        e3s = []
        pg, pu = st.pb[4], st.pb[5]
        for s in range(NSL):
            mmsg = []
            mmsu = []
            for kc in range(KC):
                def mkg(kc=kc, s=s):
                    def f(e, s_, p_):
                        return e.matmul(
                            pg.ap[:, 0:TS],
                            ar.b(WQT[0], 128, 1024)[:, kc * 128:(kc + 1) * 128],
                            ln1_at(kc)[:, s * TS:(s + 1) * TS],
                            start=s_, stop=p_)
                    return f
                def mku(kc=kc, s=s):
                    def f(e, s_, p_):
                        return e.matmul(
                            pu.ap[:, 0:TS],
                            ar.b(WQT[1], 128, 1024)[:, kc * 128:(kc + 1) * 128],
                            ln1_at(kc)[:, s * TS:(s + 1) * TS],
                            start=s_, stop=p_)
                    return f
                mmsg.append(mkg())
                mmsu.append(mku())
            mg = mm_raw(st, pg, mmsg, pre_waits=[evg] + xq_toks)
            k.wait("scalar", mg)
            k.wait("scalar", k.vcnt())
            e1 = k.sig("scalar", lambda e, s=s: e.activation(
                gsil[:, s * TS:(s + 1) * TS], pg.ap[:, 0:TS], AF.Silu))
            pg.free = e1
            mu = mm_raw(st, pu, mmsu, pre_waits=[evu] + xq_toks)
            k.wait("vector", mu)
            k.wait("vector", e1)
            k.waits("vector", h2_free[p])
            h2_free[p] = []
            e3 = k.sig("vector", lambda e, p=p, s=s: e.tensor_tensor(
                out=ar.b(H2B[p], 128, T)[:, s * TS:(s + 1) * TS],
                in0=gsil[:, s * TS:(s + 1) * TS],
                in1=pu.ap[:, 0:TS], op=ALU.mult))
            pu.free = e3
            e3s.append(e3)
            sp = k.dma("gpsimd",
                       st.spill_h2[i * 128:(i + 1) * 128,
                                   s * TS:(s + 1) * TS],
                       ar.b(H2B[p], 128, T)[:, s * TS:(s + 1) * TS],
                       waits=[e3])
            sp_toks.append(sp)
            h2_free[p].append(sp)
        am2 = absmax_acc(h2b, i == 0, pre=e3s)
        h2_free[p].append(am2)
        wqt_free[0] = k.tcnt()
        wqt_free[1] = k.tcnt()

    for i in (0, 5, 10, 16, 21):
        dbg_dump(st, f"h2_{i}", st.spill_h2[i * 128:(i + 1) * 128, :],
                 sp_toks)
    rt2 = absmax_finish(am2)
    bt2 = mk_scales(rt2, "L0_ffn_clip_down")
    dbg_dump(st, "scales2", sbc[:, 0:3], [bt2])

    WDS = O_A
    WDQ = O_A + 11264
    WDT = lambda m, ii: ar.b(O_B + (m * NI + ii) * 256, 128, 128)
    WDTg = lambda m, base, n: ar.b(O_B + (m * NI + base) * 256, 128, n * 128)
    gateup_tt = k.tcnt()
    wd_free = None
    wdt_toks = []
    for m in range(KC):
        wl = k.dma("sync", ar.f(WDS, 128, 2816),
                   dr["L0_ffn_w_down"][m * 128:(m + 1) * 128, :],
                   waits=([wd_free] if wd_free else [gateup_tt] + sp_toks))
        k.wait("vector", wl)
        rm = k.sig("vector", lambda e: e.tensor_reduce(
            out=wmx[:, 0:1], in_=ar.f(WDS, 128, 2816), axis=AX.X,
            op=ALU.max, apply_absolute_value=True))
        k.wait("vector", rm)
        rm2 = k.sig("vector", lambda e: e.tensor_scalar(
            out=wmx[:, 1:2], in0=wmx[:, 0:1], scalar1=1.0 / 127.0,
            scalar2=None, op0=ALU.mult))
        k.wait("vector", rm2)
        rm3 = k.sig("vector", lambda e: e.reciprocal(wmx[:, 2:3],
                                                     wmx[:, 1:2]))
        k.wait("vector", rm3)
        if m > 0:
            k.wait("vector", wdt_prev_tt)
        q = quant_magic(ar.f(WDS, 128, 2816), ar.b(WDQ, 128, 2816),
                        wmx[:, 1:2], wmx[:, 2:3], ar.f(WDS, 128, 2816))
        wd_free = q
        pfb = st.pbf[0]
        k.wait("tensor", pfb.free)
        k.wait("tensor", q)
        base = 0
        for ii in range(NI):
            grp = ii % 8
            if grp == 0 and ii > 0:
                tt = k.tcnt()
                k.wait("scalar", tt)
                ev = k.sig("scalar", lambda e, m=m, base=base: e.copy(
                    WDTg(m, base, 8), pfb.ap[:, 0:1024]))
                wdt_toks.append(ev)
                k.wait("tensor", ev)
                base = ii
            k.op("tensor", lambda e, m=m, ii=ii, grp=grp: e.transpose(
                pfb.ap[:, grp * 128:(grp + 1) * 128],
                ar.b(WDQ, 128, 2816)[:, ii * 128:(ii + 1) * 128],
                st.CONID), inc=("t", 1))
        wdt_prev_tt = k.tcnt()
        k.wait("scalar", wdt_prev_tt)
        nrem = NI % 8
        ev = k.sig("scalar", lambda e, m=m, base=base, nrem=nrem: e.copy(
            WDTg(m, base, nrem), pfb.ap[:, 0:nrem * 128]))
        wdt_toks.append(ev)
        pfb.free = ev

    HSB = [O_CS, O_CS + 1024]
    HQF = [O_CS + 2048, O_CS + 4096]
    HQB = [O_CS + 6144, O_CS + 7168]
    pbf_f32 = [b.ap.bitcast(F32) for b in st.pbf]
    accb = list(st.pb[0:6]) + list(st.pbf[0:2])
    accs = [st.pb[j].ap[:, 0:TS] for j in range(6)] +            [pbf_f32[0][:, 0:TS], pbf_f32[1][:, 0:TS]]
    k.waits("tensor", wdt_toks)
    dbg_dump(st, "wdt_m0", ar.b(O_B, 128, NI * 128), wdt_toks)
    dbg_dump(st, "wdt_m4", ar.b(O_B + 4 * NI * 256, 128, NI * 128), wdt_toks)
    hq_free = [None, None]
    for s in range(NSL):
        for b in accb:
            k.wait("tensor", b.free)
        stop_tok = None
        for i in range(NI):
            p = i % 2
            ld = k.dma("gpsimd", ar.b(HSB[p], 128, TS),
                       st.spill_h2[i * 128:(i + 1) * 128,
                                   s * TS:(s + 1) * TS],
                       waits=([hq_free[p]] if hq_free[p] else sp_toks))
            qd = quant_magic(ar.b(HSB[p], 128, TS), ar.b(HQB[p], 128, TS),
                             sbc[:, 0:1], sbc[:, 1:2],
                             ar.f(HQF[p], 128, TS),
                             pre=[ld, bt2] +
                                 ([hq_free[p]] if hq_free[p] else []))
            if s == 0 and i == 0:
                dbg_dump(st, "hsb00", ar.b(HSB[p], 128, TS), [ld])
                dbg_dump(st, "hqb00", ar.b(HQB[p], 128, TS), [qd])
            k.wait("tensor", qd)
            for m in range(KC):
                tok = k.op("tensor", lambda e, m=m, i=i, p=p: e.matmul(
                    accs[m], WDT(m, i), ar.b(HQB[p], 128, TS),
                    start=(i == 0), stop=(i == NI - 1)),
                    inc=("t", 1))
                if i == NI - 1:
                    stop_tok = tok
            hq_free[p] = k.tcnt()
        k.wait("vector", stop_tok)
        if s == 0:
            dbg_dump(st, "rpre0", st.R[:, 0:TS], [st.r_tok[0]])
        for m in range(KC):
            k.waits("vector", [st.r_tok[m]] + st.r_readers[m])
            st.r_readers[m] = []
            tsl = slice(s * TS, (s + 1) * TS)
            st.r_tok[m] = k.sig("vector", lambda e, m=m, tsl=tsl, idx=m:
                                e.tensor_tensor(
                out=st.R[:, m * T:(m + 1) * T][:, tsl],
                in0=st.R[:, m * T:(m + 1) * T][:, tsl],
                in1=accs[idx], op=ALU.add))
            accb[m].free = st.r_tok[m]
        if s == 0:
            dbg_dump(st, "rpost0", st.R[:, 0:TS], [st.r_tok[0]])
    st.l0ffn_out = k.vcnt()



def emit_l1_attn(st):
    k = st.k
    ar = st.ar
    dr = st.dr
    ln2_at = lambda c: ar.b(O_A + c * 4096, 128, T)
    ln_toks = emit_norm(st, lambda c: st.R[:, c * T:(c + 1) * T], ln2_at,
                        "ln", 1e-5, tr_off=O_CS + 4096, bc_off=O_B,
                        src_toks=[t for t in st.r_tok if t])
    for c in range(KC):
        st.r_readers[c].append(ln_toks[c])
    dbg_dump(st, "ln2_0", ln2_at(0), [ln_toks[0]])
    STG = [O_C + 16384, O_C + 17408]
    if not hasattr(st, "stg_free"):
        st.stg_free = [None, None]
    sp_toks = []

    def evac_qkv(mco, s, bank, mmtok):
        slot = (mco * NSL + s) % 2
        stg = ar.b(STG[slot], 128, TS)
        k.wait("vector", mmtok)
        if st.stg_free[slot]:
            k.wait("vector", st.stg_free[slot])
        vt = k.sig("vector", lambda e, stg=stg, bank=bank, mco=mco:
                   e.tensor_scalar(
                       out=stg, in0=bank.ap[:, 0:TS],
                       scalar1=sv(st, "L1_attn_b_qkv", col=mco),
                       scalar2=None, op0=ALU.add))
        bank.free = vt
        d_ = k.dma("sync", st.spill_qkv[mco * 128:(mco + 1) * 128,
                                        s * TS:(s + 1) * TS], stg, waits=[vt])
        sp_toks.append(d_)
        st.stg_free[slot] = d_
    emit_proj(st, dr["L1_attn_w_qkv"], ln2_at, KC, 3 * D, evac_qkv,
              O_C, 16384, in_toks=list(ln_toks),
              w_waits=[t for t in st.r_tok if t])
    dbg_dump(st, "qkvs0", st.spill_qkv[0:128, :], sp_toks)
    OCTX = O_A
    OPT = O_C
    OQT = O_C + 16384
    OKT = O_C + 20480
    OVT = O_C + 24576
    OVN = O_C + 28672
    RDq = O_CS
    RDB = O_CS + 2048
    ctx_at = lambda h: ar.b_at(OCTX + h * 4096, 0, 64, T)
    ctx_toks = []
    qkv_free = None
    vn_read = None
    rd_read = None
    for h in range(16):
        lw = ([qkv_free] if qkv_free else sp_toks)
        tq = k.dma("gpsimd", ar.b_at(OQT, 0, 64, T),
                   st.spill_qkv[h * 64:(h + 1) * 64, :], waits=lw)
        tk = k.dma("gpsimd", ar.b_at(OKT, 0, 64, T),
                   st.spill_qkv[1024 + h * 64:1024 + (h + 1) * 64, :],
                   waits=lw)
        tv = k.dma("gpsimd", ar.b_at(OVT, 0, 64, T),
                   st.spill_qkv[2048 + h * 64:2048 + (h + 1) * 64, :],
                   waits=lw)
        if vn_read:
            k.wait("vector", vn_read)
        mz = k.sig("vector", lambda e: e.memset(ar.b(OVN, 128, 16 * 65), 1.0))
        pfb = st.pbf[0]
        ev_toks = []
        for rnd in range(2):
            k.wait("tensor", pfb.free)
            k.wait("tensor", tv)
            if qkv_free:
                k.wait("tensor", qkv_free)
            for j in range(8):
                kt = rnd * 8 + j
                k.op("tensor", lambda e, kt=kt, j=j: e.transpose(
                    pfb.ap[:, j * 64:(j + 1) * 64],
                    ar.b_at(OVT, 0, 64, T)[:, kt * 128:(kt + 1) * 128],
                    st.CONID[0:64, 0:64]), inc=("t", 1))
            tt = k.tcnt()
            k.wait("scalar", tt)
            k.wait("scalar", mz)
            for j in range(8):
                kt = rnd * 8 + j
                et = k.sig("scalar", lambda e, kt=kt, j=j: e.copy(
                    ar.b(OVN + kt * 130, 128, 64),
                    pfb.ap[:, j * 64:(j + 1) * 64]))
                ev_toks.append(et)
            pfb.free = ev_toks[-1]
        pt_read = None
        for qb in range(4):
            pv = st.pb[2]
            k.wait("tensor", pv.free)
            k.waits("tensor", [ev_toks[-1], mz])
            exp_toks = []
            sbanks = [st.pb[0], st.pb[1], st.pb[5]]

            def emit_s(kt):
                bank = sbanks[kt % 3]
                k.wait("tensor", bank.free)
                k.waits("tensor", [tq, tk])
                mt = k.op("tensor", lambda e, kt=kt, qb=qb, bank=bank:
                          e.matmul(
                    bank.ap[:, 0:TS],
                    ar.b_at(OKT, 0, 64, T)[:, kt * 128:(kt + 1) * 128],
                    ar.b_at(OQT, 0, 64, T)[:, qb * TS:(qb + 1) * TS],
                    start=True, stop=True), inc=("t", 1))
                k.wait("scalar", mt)
                if pt_read:
                    k.wait("scalar", pt_read)
                et = k.sig("scalar", lambda e, kt=kt, bank=bank: e.activation(
                    ar.b(OPT + kt * 1024, 128, TS), bank.ap[:, 0:TS],
                    AF.Exp, scale=0.125))
                bank.free = et
                exp_toks.append(et)

            def emit_pv(kt):
                k.wait("tensor", exp_toks[kt])
                k.op("tensor", lambda e, kt=kt, pv=pv: e.matmul(
                    pv.ap[0:65, 0:TS],
                    ar.b(OVN + kt * 130, 128, 65),
                    ar.b(OPT + kt * 1024, 128, TS),
                    start=(kt == 0), stop=(kt == 15)), inc=("t", 1))
            emit_s(0)
            emit_s(1)
            for kt in range(16):
                if kt + 2 < 16:
                    emit_s(kt + 2)
                emit_pv(kt)
            pvt = k.tcnt()
            pt_read = pvt
            k.wait("vector", pvt)
            if rd_read:
                k.wait("vector", rd_read)
            rd_f = ar.f32[64:65, RDq // 4:RDq // 4 + TS]
            rd_b = ar.bf[64:65, RDB // 2:RDB // 2 + TS]
            r1 = k.sig("vector", lambda e, pv=pv: e.reciprocal(
                rd_f, pv.ap[64:65, 0:TS]))
            k.wait("vector", r1)
            r2 = k.sig("vector", lambda e: e.tensor_copy(rd_b, rd_f))
            bcb = st.pb[3]
            k.wait("tensor", bcb.free)
            k.wait("tensor", r2)
            bt_ = k.op("tensor", lambda e, bcb=bcb: e.matmul(
                bcb.ap[:, 0:TS], st.CONER64, rd_b, start=True, stop=True),
                inc=("t", 1))
            rd_read = bt_
            bcba = ar.b(O_CS + 8192, 128, TS)
            k.wait("scalar", bt_)
            if ctx_toks:
                k.wait("scalar", ctx_toks[-1])
            bev = k.sig("scalar", lambda e, bcb=bcb: e.copy(
                bcba, bcb.ap[:, 0:TS]))
            bcb.free = bev
            k.wait("vector", bev)
            ct = k.sig("vector", lambda e, h=h, qb=qb, pv=pv:
                       e.tensor_tensor(
                out=ctx_at(h)[:, qb * TS:(qb + 1) * TS],
                in0=pv.ap[0:64, 0:TS], in1=bcba[0:64, 0:TS],
                op=ALU.mult))
            pv.free = ct
            ctx_toks.append(ct)
        qkv_free = k.tcnt()
        vn_read = ("t", k.cnt["t"])
    dbg_dump(st, "ctxh0", ctx_at(0), [ctx_toks[3]])
    OWF = O_C
    wf_toks = []
    for hh in range(16):
        wf_toks.append(k.dma("gpsimd",
                             ar.b_at(OWF + hh * 2048, 0, 64, 1024),
                             dr["L1_attn_w_fc"][hh * 64:(hh + 1) * 64, :],
                             waits=[qkv_free]))
    for mco in range(KC):
        for s in range(NSL):
            bank = st.pb[4 + (mco * NSL + s) % 2]
            k.wait("tensor", bank.free)
            k.waits("tensor", wf_toks + [ctx_toks[-1]])
            tok = None
            for hh in range(16):
                tok = k.op("tensor", lambda e, hh=hh, mco=mco, s=s, bank=bank:
                           e.matmul(
                    bank.ap[:, 0:TS],
                    ar.b_at(OWF + hh * 2048, 0, 64, 1024)[
                        :, mco * 128:(mco + 1) * 128],
                    ctx_at(hh)[:, s * TS:(s + 1) * TS],
                    start=(hh == 0), stop=(hh == 15)),
                    inc=(("t", 1) if hh == 15 else None))
            k.wait("vector", tok)
            k.waits("vector", [st.r_tok[mco]] + st.r_readers[mco])
            st.r_readers[mco] = []
            tsl = slice(s * TS, (s + 1) * TS)
            st.r_tok[mco] = k.sig("vector", lambda e, mco=mco, tsl=tsl,
                                  bank=bank: e.scalar_tensor_tensor(
                out=st.R[:, mco * T:(mco + 1) * T][:, tsl],
                in0=bank.ap[:, 0:TS],
                scalar=sv(st, "L1_attn_b_fc", col=mco),
                in1=st.R[:, mco * T:(mco + 1) * T][:, tsl],
                op0=ALU.add, op1=ALU.add))
            bank.free = st.r_tok[mco]
    st.l1attn_done = k.vcnt()
    dbg_dump(st, "r3_0", st.R[:, 0:T], [st.r_tok[0]])


def emit_l1_ffn(st):
    k = st.k
    ar = st.ar
    dr = st.dr
    SP_EQ, SP_EK, SP_EV = st.spill_e[0], st.spill_e[1], st.spill_e[2]
    SP_QT, SP_KT, SP_VT = st.spill_ln0, st.spill_gate, st.spill_y1
    SP_N = st.spill_node
    SP_LN3 = st.spill_ln3
    in_at = lambda c: ar.b(O_A + c * 4096, 128, T)
    STG = [O_C + 16384, O_C + 17408]
    if not hasattr(st, "stg_free"):
        st.stg_free = [None, None]
    ewb = st.misc[:, 420:438]
    actb = st.misc[:, 438:444]
    sA = lambda c: st.misc[:, 448 + c:449 + c]
    sB = lambda c: st.misc[:, 456 + c:457 + c]
    sC = lambda c: st.misc[:, 420 + 3 * c:421 + 3 * c]
    d1 = k.dma("gpsimd", ewb, dr["L1_ffn_ew_row"][0:1, :]
               .partition_broadcast(128))
    d2 = k.dma("gpsimd", actb, dr["L1_ffn_act_w_row"][0:1, :]
               .partition_broadcast(128))
    k.waits("vector", [d1, d2])
    for c in range(6):
        k.sig("vector", lambda e, c=c: e.tensor_scalar(
            out=sA(c), in0=ewb[:, 3 * c:3 * c + 1],
            scalar1=ewb[:, 3 * c + 1:3 * c + 2], op0=ALU.mult,
            scalar2=float(1.0 / np.sqrt(128.0)), op1=ALU.mult))
        k.sig("vector", lambda e, c=c: e.tensor_scalar(
            out=sB(c), in0=ewb[:, 3 * c + 2:3 * c + 3],
            scalar1=actb[:, c:c + 1], scalar2=None, op0=ALU.mult))
    scal_tok = k.vcnt()
    ln_toks = emit_norm(st, lambda c: st.R[:, c * T:(c + 1) * T], in_at,
                        "ln", 1e-5, tr_off=O_CS + 4096, bc_off=O_B,
                        src_toks=[t for t in st.r_tok if t])
    for c in range(KC):
        st.r_readers[c].append(ln_toks[c])
    ln3_sp = []
    for c in range(KC):
        ln3_sp.append(k.dma("sync", SP_LN3[c * 128:(c + 1) * 128, :],
                            in_at(c), waits=[ln_toks[c]]))
    dbg_dump(st, "ln3_0", in_at(0), [ln_toks[0]])
    st.in_toks = list(ln_toks)
    st.in_read = None
    st.comb_read = None
    node_sp = {}

    def get(i):
        if i == -1:
            return SP_LN3, ln3_sp
        return SP_N[i], node_sp[i]

    b_at = lambda c: ar.b(O_B + c * 4096, 128, T)
    st.b_toks = []
    st.b_read = None
    st.b_dma = []

    def load_input(src_sp, src_toks):
        toks = []
        for c in range(KC):
            toks.append(k.dma("gpsimd", in_at(c),
                              src_sp[c * 128:(c + 1) * 128, :],
                              waits=([st.in_read] if st.in_read else []) +
                              ([st.comb_read] if st.comb_read else []) +
                              list(src_toks)))
        st.in_toks = toks

    def act_proj(wname, src, dst, bias_e=None, dst_sp=None, w_waits=()):
        sp_toks2 = []
        b_toks2 = []
        src_at = in_at if src == "a" else b_at
        src_toks = st.in_toks if src == "a" else st.b_toks

        def evac(mco, s, bank, mmtok):
            k.wait("vector", mmtok)
            if dst == "b":
                if st.b_read:
                    k.wait("vector", st.b_read)
                k.waits("vector", st.b_dma)
                out_ap = b_at(mco)[:, s * TS:(s + 1) * TS]
                if bias_e is not None:
                    vt = k.sig("vector", lambda e, out_ap=out_ap, bank=bank,
                               mco=mco: e.tensor_scalar(
                        out=out_ap, in0=bank.ap[:, 0:TS],
                        scalar1=sv(st, "L1_ffn_be", col=bias_e * 8 + mco),
                        scalar2=None, op0=ALU.add))
                else:
                    vt = k.sig("vector", lambda e, out_ap=out_ap, bank=bank:
                               e.tensor_copy(out_ap, bank.ap[:, 0:TS]))
                bank.free = vt
                b_toks2.append(vt)
                if dst_sp is not None:
                    d_ = k.dma("sync", dst_sp[mco * 128:(mco + 1) * 128,
                                              s * TS:(s + 1) * TS],
                               out_ap, waits=[vt])
                    sp_toks2.append(d_)
                return
            slot = (mco * NSL + s) % 2
            stg = ar.b(STG[slot], 128, TS)
            if st.stg_free[slot]:
                k.wait("vector", st.stg_free[slot])
            if bias_e is not None:
                vt = k.sig("vector", lambda e, stg=stg, bank=bank, mco=mco:
                           e.tensor_scalar(
                    out=stg, in0=bank.ap[:, 0:TS],
                    scalar1=sv(st, "L1_ffn_be", col=bias_e * 8 + mco),
                    scalar2=None, op0=ALU.add))
            else:
                vt = k.sig("vector", lambda e, stg=stg, bank=bank:
                           e.tensor_copy(stg, bank.ap[:, 0:TS]))
            bank.free = vt
            d_ = k.dma("sync", dst_sp[mco * 128:(mco + 1) * 128,
                                      s * TS:(s + 1) * TS], stg, waits=[vt])
            sp_toks2.append(d_)
            st.stg_free[slot] = d_
        emit_proj(st, dr[wname], src_at, KC, D, evac, O_C, 16384,
                  in_toks=list(src_toks), w_waits=list(w_waits))
        if src == "a":
            st.in_read = k.tcnt()
        else:
            st.b_read = k.tcnt()
        if dst == "b":
            st.b_toks = b_toks2
            st.b_dma = sp_toks2
            return sp_toks2 if dst_sp is not None else b_toks2
        return sp_toks2

    OPT = O_B
    OQT = O_C + 16384
    OKT = O_C + 20480
    OVT = O_C + 24576
    OVN = O_C + 28672
    RDq = O_CS
    RDB = O_CS + 2048

    def node_attn(c, qt_sp, kt_sp, vt_sp):
        ctx_toks = []
        qkv_free = None
        rd_read = None
        for h in range(8):
            lw = ([qkv_free] if qkv_free else
                  ([st.in_read] if st.in_read else [])) + \
                 ([st.comb_read] if st.comb_read else []) + \
                 ([] if qkv_free else [t for t in st.stg_free if t])
            tq = k.dma("gpsimd", ar.b(OQT, 128, T),
                       SP_QT[h * 128:(h + 1) * 128, :], waits=lw + qt_sp)
            tk = k.dma("gpsimd", ar.b(OKT, 128, T),
                       SP_KT[h * 128:(h + 1) * 128, :], waits=lw + kt_sp)
            tv = k.dma("gpsimd", ar.b(OVT, 128, T),
                       SP_VT[h * 128:(h + 1) * 128, :], waits=lw + vt_sp)
            pfb = st.pbf[0]
            ev_toks = []
            for rnd in range(2):
                k.wait("tensor", pfb.free)
                k.wait("tensor", tv)
                if qkv_free:
                    k.wait("tensor", qkv_free)
                for j in range(8):
                    kt = rnd * 8 + j
                    k.op("tensor", lambda e, kt=kt, j=j: e.transpose(
                        pfb.ap[:, j * 128:(j + 1) * 128],
                        ar.b(OVT, 128, T)[:, kt * 128:(kt + 1) * 128],
                        st.CONID), inc=("t", 1))
                tt = k.tcnt()
                k.wait("scalar", tt)
                et = k.sig("scalar", lambda e, rnd=rnd: e.copy(
                    ar.b(OVN + rnd * 2048, 128, 1024), pfb.ap[:, 0:1024]))
                ev_toks.append(et)
                pfb.free = et
            pt_read = None
            k.waits("scalar", st.b_dma)
            for qb in range(4):
                pv = st.pb[2]
                dnb = st.pb[3]
                k.wait("tensor", pv.free)
                k.wait("tensor", dnb.free)
                k.wait("tensor", ev_toks[-1])
                exp_toks = []
                sbanks = [st.pb[0], st.pb[1], st.pb[5]]

                def emit_s(kt):
                    bank = sbanks[kt % 3]
                    k.wait("tensor", bank.free)
                    k.waits("tensor", [tq, tk])
                    mt = k.op("tensor", lambda e, kt=kt, qb=qb, bank=bank:
                              e.matmul(
                        bank.ap[:, 0:TS],
                        ar.b(OKT, 128, T)[:, kt * 128:(kt + 1) * 128],
                        ar.b(OQT, 128, T)[:, qb * TS:(qb + 1) * TS],
                        start=True, stop=True), inc=("t", 1))
                    k.wait("scalar", mt)
                    k.wait("scalar", scal_tok)
                    if pt_read:
                        k.wait("scalar", pt_read)
                    et = k.sig("scalar", lambda e, kt=kt, bank=bank, c=c:
                               e.activation(
                        ar.b(OPT + kt * 1024, 128, TS), bank.ap[:, 0:TS],
                        AF.Exp, scale=sA(c)))
                    bank.free = et
                    exp_toks.append(et)

                def emit_pv(kt):
                    k.wait("tensor", exp_toks[kt])
                    k.op("tensor", lambda e, kt=kt, pv=pv: e.matmul(
                        pv.ap[:, 0:TS],
                        ar.b(OVN + kt * 256, 128, 128),
                        ar.b(OPT + kt * 1024, 128, TS),
                        start=(kt == 0), stop=(kt == 15)), inc=("t", 1))
                    k.op("tensor", lambda e, kt=kt, dnb=dnb: e.matmul(
                        dnb.ap[0:1, 0:TS], st.CONEC,
                        ar.b(OPT + kt * 1024, 128, TS),
                        start=(kt == 0), stop=(kt == 15)), inc=("t", 1))
                emit_s(0)
                emit_s(1)
                for kt in range(16):
                    if kt + 2 < 16:
                        emit_s(kt + 2)
                    emit_pv(kt)
                pvt = k.tcnt()
                pt_read = pvt
                k.wait("vector", pvt)
                if rd_read:
                    k.wait("vector", rd_read)
                rd_f = ar.f32[0:1, RDq // 4:RDq // 4 + TS]
                rd_b = ar.bf[0:1, RDB // 2:RDB // 2 + TS]
                r1 = k.sig("vector", lambda e, dnb=dnb: e.reciprocal(
                    rd_f, dnb.ap[0:1, 0:TS]))
                k.wait("vector", r1)
                r2 = k.sig("vector", lambda e: e.tensor_copy(rd_b, rd_f))
                bcb = st.pb[4]
                k.wait("tensor", bcb.free)
                k.wait("tensor", r2)
                bt_ = k.op("tensor", lambda e, bcb=bcb: e.matmul(
                    bcb.ap[:, 0:TS], st.CONER, rd_b, start=True, stop=True),
                    inc=("t", 1))
                rd_read = bt_
                bcba = ar.b(O_CS + 8192, 128, TS)
                k.wait("scalar", bt_)
                if ctx_toks:
                    k.wait("scalar", ctx_toks[-1])
                bev = k.sig("scalar", lambda e, bcb=bcb: e.copy(
                    bcba, bcb.ap[:, 0:TS]))
                bcb.free = bev
                k.wait("vector", bev)
                k.wait("vector", st.in_read)
                ct = k.sig("vector", lambda e, h=h, qb=qb, pv=pv:
                           e.tensor_tensor(
                    out=in_at(h)[:, qb * TS:(qb + 1) * TS],
                    in0=pv.ap[:, 0:TS], in1=bcba[:, 0:TS], op=ALU.mult))
                pv.free = ct
                dnb.free = ct
                ctx_toks.append(ct)
            qkv_free = k.tcnt()
        st.in_toks = ctx_toks
        st.in_read = qkv_free
        return ctx_toks

    def wo_combine(c, eq_sp_toks):
        EQR = [O_C + 18432, O_C + 22528]
        eq_free = [None, None]
        eq_toks = {}
        tmpf = ar.f(O_CS + 4096, 128, TS)
        sp_toks2 = []

        def evac(mco, s, bank, mmtok):
            slot = mco % 2
            if s == 0:
                eq_toks[mco] = k.dma(
                    "gpsimd", ar.b(EQR[slot], 128, T),
                    SP_EQ[mco * 128:(mco + 1) * 128, :],
                    waits=([eq_free[slot]] if eq_free[slot] else
                           [st.in_read]) + eq_sp_toks)
            k.wait("vector", mmtok)
            k.wait("vector", eq_toks[mco])
            t1 = k.sig("vector", lambda e, bank=bank, c=c: e.tensor_scalar(
                out=tmpf, in0=bank.ap[:, 0:TS], scalar1=sB(c),
                scalar2=None, op0=ALU.mult))
            bank.free = t1
            k.wait("vector", t1)
            if c == 5:
                dst = ar.b(O_B + mco * 4096, 128, T)[:, s * TS:(s + 1) * TS]
                t2 = k.sig("vector", lambda e, dst=dst, slot=slot, s=s, c=c:
                           e.scalar_tensor_tensor(
                    out=dst, in0=ar.b(EQR[slot], 128, T)[:, s * TS:(s + 1) * TS],
                    scalar=sC(c), in1=tmpf, op0=ALU.mult, op1=ALU.add))
                sp_toks2.append(t2)
            else:
                slot2 = (mco * NSL + s) % 2
                stg = ar.b(STG[slot2], 128, TS)
                if st.stg_free[slot2]:
                    k.wait("vector", st.stg_free[slot2])
                t2 = k.sig("vector", lambda e, stg=stg, slot=slot, s=s, c=c:
                           e.scalar_tensor_tensor(
                    out=stg, in0=ar.b(EQR[slot], 128, T)[:, s * TS:(s + 1) * TS],
                    scalar=sC(c), in1=tmpf, op0=ALU.mult, op1=ALU.add))
                d_ = k.dma("sync", SP_N[c][mco * 128:(mco + 1) * 128,
                                           s * TS:(s + 1) * TS], stg,
                           waits=[t2])
                sp_toks2.append(d_)
                st.stg_free[slot2] = d_
            if s == NSL - 1:
                eq_free[slot] = t2
        emit_proj(st, dr[f"L1_ffn_Wo{c}"], in_at, KC, D, evac, O_C,
                  16384, in_toks=list(st.in_toks))
        st.in_read = k.tcnt()
        st.comb_read = k.vcnt()
        return sp_toks2

    EDGE = {0: dict(eq=(-1, 0), ek=None, ev=None),
            1: dict(eq=(0, 2), ek=(-1, 1), ev=None),
            2: dict(eq=(1, 5), ek=(-1, 3), ev=(0, 4)),
            3: dict(eq=(2, 9), ek=(-1, 6), ev=(1, 8)),
            4: dict(eq=(3, 14), ek=(-1, 10), ev=(1, 12)),
            5: dict(eq=(4, 19), ek=(0, 15), ev=(2, 17))}
    st.in_read = k.tcnt()
    l1f_boundary = k.tcnt()
    for c in range(6):
        E = EDGE[c]
        if c > 0:
            load_input(*get(E["eq"][0]))
        eq_sp = act_proj(f"L1_ffn_We{E['eq'][1]}", "a", "b",
                         bias_e=E["eq"][1], dst_sp=SP_EQ,
                         w_waits=([l1f_boundary] if c == 0 else ()))
        qt_sp = act_proj(f"L1_ffn_Wq{c}", "b", "spill", dst_sp=SP_QT)
        vt_sp = None
        if E["ev"] is None:
            vt_sp = act_proj(f"L1_ffn_Wv{c}", "b", "spill", dst_sp=SP_VT)
        if E["ek"] is None:
            kt_sp = act_proj(f"L1_ffn_Wk{c}", "b", "spill", dst_sp=SP_KT)
        else:
            load_input(*get(E["ek"][0]))
            act_proj(f"L1_ffn_We{E['ek'][1]}", "a", "b", bias_e=E["ek"][1])
            kt_sp = act_proj(f"L1_ffn_Wk{c}", "b", "spill", dst_sp=SP_KT)
        if E["ev"] is not None:
            load_input(*get(E["ev"][0]))
            act_proj(f"L1_ffn_We{E['ev'][1]}", "a", "b", bias_e=E["ev"][1])
            vt_sp = act_proj(f"L1_ffn_Wv{c}", "b", "spill", dst_sp=SP_VT)
        node_attn(c, qt_sp, kt_sp, vt_sp)
        node_sp[c] = wo_combine(c, eq_sp)
        if c < 5:
            dbg_dump(st, f"n{c}_0", SP_N[c][0:128, :], node_sp[c])
    n5_at = lambda cc: ar.b(O_B + cc * 4096, 128, T)
    n5n_at = lambda cc: ar.b(O_A + cc * 4096, 128, T)
    k.waits("scalar", node_sp[5])
    nf_toks = emit_norm(st, n5_at, n5n_at, "ln", 1e-6,
                        tr_off=O_CS + 4096, bc_off=O_C,
                        gain_at=lambda cc: sv(st, "L1_ffn_ln_g", col=cc),
                        bias_at=lambda cc: sv(st, "L1_ffn_ln_b", col=cc),
                        src_toks=list(node_sp[5]) + [st.in_read] +
                        [t for t in st.stg_free if t])
    dbg_dump(st, "n5n_0", n5n_at(0), [nf_toks[0]])
    for cc in range(KC):
        k.waits("vector", [st.r_tok[cc], nf_toks[cc]] + st.r_readers[cc])
        st.r_readers[cc] = []
        st.r_tok[cc] = k.sig("vector", lambda e, cc=cc: e.tensor_tensor(
            out=st.R[:, cc * T:(cc + 1) * T],
            in0=st.R[:, cc * T:(cc + 1) * T],
            in1=n5n_at(cc), op=ALU.add))


def load_small(st):
    k = st.k
    dr = st.dr
    names = [
        ("L0_attn_norm_in_w", 128, KC),
        ("L0_attn_conv_w", 128, 4 * KC),
        ("L0_attn_conv_b", 128, KC),
        ("L0_attn_bau", 64, 1),
        ("L0_attn_bad", 128, 4),
        ("L0_attn_alpha_spike", 128, 4),
        ("L0_attn_bbeta", 8, 1),
        ("L0_attn_beta_spike", 128, 4),
        ("L0_attn_head_norm_w", 128, 4),
        ("L0_attn_bu1", 64, 1),
        ("L0_attn_bu2", 128, KC),
        ("L0_attn_ff_norm_w", 128, KC),
        ("L0_ffn_clip_x", 1, 1),
        ("L0_ffn_clip_down", 1, 1),
        ("L1_attn_b_qkv", 128, 24),
        ("L1_attn_b_fc", 128, KC),
        ("L1_ffn_ew", 18, 1),
        ("L1_ffn_act_w", 6, 1),
        ("L1_ffn_be", 128, 20 * KC),
        ("L1_ffn_ln_g", 128, KC),
        ("L1_ffn_ln_b", 128, KC),
    ]
    off = 0
    st.small = {}
    st.small_tok = {}
    for name, rows, cols in names:
        assert off + cols <= 4096
        ap = st.ROWS[0:rows, off:off + cols]
        st.small[name] = ap
        off += cols
        src = dr[name]
        if cols == 1:
            tok = k.dma("sync", ap, src[0:rows, 0:1])
        else:
            inner = src.shape[1]
            tok = k.dma("sync", ap.rearrange("p (c w) -> p c w", w=inner),
                        src.rearrange("(c p) w -> p c w", p=rows), slow=True)
        st.small_tok[name] = tok
    st.nr_sum = st.ROWS[0:1, off:off + TS]
    st.nr_ssq = st.ROWS[0:1, off + TS:off + 2 * TS]
    st.nr_a = st.ROWS[0:1, off + 2 * TS:off + 3 * TS]
    st.nr_t = st.ROWS[0:1, off + 3 * TS:off + 4 * TS]
    off += 4 * TS
    st.misc = st.ROWS[:, off:off + 512]
    off += 512
    st.ccrow = st.ROWS[0:1, off:off + 128]
    off += 128
    assert off <= 4096, off
    st.params_tok = list(st.small_tok.values())


def load_consts(st):
    k = st.k
    dr = st.dr
    C = st.con
    st.CONID = C.b(0, 128, 128)
    st.CONER = C.b(256, 1, 128)
    st.CONEC = C.b(512, 128, 1)
    st.CONBO8 = C.b(516, 128, 32)
    st.CONSEL = C.b(580, 8, 512)
    st.CONIS = C.b(1604, 128, 64)
    st.CONLT = C.f(1732, 128, 64)
    st.CONUT = C.f(1988, 128, 64)
    st.CONUI = C.f(2244, 128, 64)
    st.ZERO64 = C.f(2500, 128, 64)
    st.EPS5 = C.f(2756, 128, 1)
    st.EPS6 = C.f(2760, 128, 1)
    st.CONROPE = C.b(2764, 128, 1024)
    st.CONERF = C.f(4812, 1, 128)
    st.CONIDF = C.f(5324, 128, 128)
    st.CONER64 = C.bf[64:65, 5836 // 2:5836 // 2 + 128]
    toks = [
        k.dma("sync", st.CONID, dr["c_ident_bf"]),
        k.dma("sync", st.CONER, dr["c_onesrow_bf"]),
        k.dma("sync", st.CONEC, dr["c_onescol_bf"]),
        k.dma("sync", st.CONBO8, dr["c_bo8_bf"]),
        k.dma("sync", st.CONSEL, dr["c_headsel_bf"]),
        k.dma("sync", st.CONIS, dr["c_istack_bf"]),
        k.dma("sync", st.CONLT, dr["c_ltmask"]),
        k.dma("sync", st.CONUT, dr["c_utmask"]),
        k.dma("sync", st.CONUI, dr["c_uimask"]),
        k.dma("sync", st.CONROPE.rearrange("p (c o) -> p c o", o=128),
              dr["c_rope_bd_bf"].rearrange("(c p) o -> p c o", p=128)),
        k.dma("sync", st.CONERF, dr["c_onesrow_f"]),
        k.dma("sync", st.CONIDF, dr["c_ident_f"]),
        k.dma("sync", st.CONER64, dr["c_onesrow_bf"]),
    ]
    zt = k.sig("vector", lambda e: e.memset(st.ZERO64, 0.0))
    k.sig("vector", lambda e: e.memset(st.EPS5, 1e-5))
    zt = k.sig("vector", lambda e: e.memset(st.EPS6, 1e-6))
    st.const_tok = toks + [zt]
    for eng in ("tensor", "vector", "scalar"):
        k.waits(eng, toks)
        k.wait(eng, zt)


def emit_all(st):
    k = st.k
    load_consts(st)
    load_small(st)
    for eng in ("tensor", "vector", "scalar"):
        k.waits(eng, st.params_tok)
    for c in range(KC):
        st.r_tok[c] = k.dma("sync", st.R[:, c * T:(c + 1) * T],
                            st.dr["xT"][c * 128:(c + 1) * 128, :])
    emit_l0_attn(st)
    if st.debug_stage != 1:
        emit_l0_ffn(st)
        if st.debug_stage != 2:
            emit_l1_attn(st)
            if st.debug_stage != 3:
                emit_l1_ffn(st)
    toks = []
    for c in range(KC):
        k.waits("sync", [st.r_tok[c]] + st.r_readers[c])
        toks.append(k.dma("sync", st.out[c * 128:(c + 1) * 128, :],
                          st.R[:, c * T:(c + 1) * T]))
    k.waits("sync", toks)


def build_program(debug_stage=0):
    nc = bass.Bass("TRN2", target_bir_lowering=False, debug=False)
    dr = {}

    def din(name, shape, dtype=F32):
        dr[name] = nc.dram_tensor(name, list(shape), dtype,
                                  kind="ExternalInput").ap()

    din("xT", (D, T))
    din("L0_attn_norm_in_w", (D, 1))
    din("L0_attn_conv_w", (D, 4))
    din("L0_attn_conv_b", (D, 1))
    for w in ("Wq", "Wk", "Wv", "Wspike"):
        din("L0_attn_" + w, (D, 512))
    din("L0_attn_Wo", (512, D))
    din("L0_attn_Wau", (D, 64))
    din("L0_attn_bau", (64, 1))
    din("L0_attn_Wad", (64, 512))
    din("L0_attn_bad", (512, 1))
    din("L0_attn_alpha_spike", (512, 1))
    din("L0_attn_Wbeta", (D, 8))
    din("L0_attn_bbeta", (8, 1))
    din("L0_attn_beta_spike", (512, 1))
    din("L0_attn_head_norm_w", (512, 1))
    din("L0_attn_Wu1", (D, 64))
    din("L0_attn_bu1", (64, 1))
    din("L0_attn_Wu2", (64, D))
    din("L0_attn_bu2", (D, 1))
    din("L0_attn_ff_norm_w", (D, 1))
    din("L0_attn_Wff1", (D, 4096))
    din("L0_attn_Wff3", (D, 4096))
    din("L0_attn_Wff2", (4096, D))
    din("L0_ffn_clip_x", (1, 1))
    din("L0_ffn_clip_down", (1, 1))
    din("L0_ffn_w_gate", (2816, D))
    din("L0_ffn_w_up", (2816, D))
    din("L0_ffn_w_down", (D, 2816))
    din("L1_attn_w_qkv", (D, 3 * D))
    din("L1_attn_b_qkv", (3 * D, 1))
    din("L1_attn_w_fc", (D, D))
    din("L1_attn_b_fc", (D, 1))
    din("L1_ffn_ew", (18, 1))
    din("L1_ffn_act_w", (6, 1))
    din("L1_ffn_ew_row", (1, 18))
    din("L1_ffn_act_w_row", (1, 6))
    din("L1_ffn_be", (20 * D, 1))
    for e_ in range(20):
        din(f"L1_ffn_We{e_}", (D, D))
    for c_ in range(6):
        for w_ in ("Wq", "Wk", "Wv", "Wo"):
            din(f"L1_ffn_{w_}{c_}", (D, D))
    din("L1_ffn_ln_g", (D, 1))
    din("L1_ffn_ln_b", (D, 1))
    din("c_ident_bf", (128, 128), BF16)
    din("c_onesrow_bf", (1, 128), BF16)
    din("c_onescol_bf", (128, 1), BF16)
    din("c_headsel_bf", (8, 4 * 128), BF16)
    din("c_bo8_bf", (128, 32), BF16)
    din("c_ltmask", (128, CH))
    din("c_utmask", (128, CH))
    din("c_uimask", (128, CH))
    din("c_istack_bf", (128, CH), BF16)
    din("c_rope_bd_bf", (8 * 128, 128), BF16)
    din("c_onesrow_f", (1, 128))
    din("c_ident_f", (128, 128))

    st = St()
    st.nc = nc
    st.dr = dr
    st.out = nc.dram_tensor("out", [D, T], F32, kind="ExternalOutput").ap()
    st.spill_ln0 = nc.dram_tensor("spill_ln0", [D, T], BF16, kind="Internal").ap()
    st.spill_gate = nc.dram_tensor("spill_gate", [D, T], BF16, kind="Internal").ap()
    st.spill_r = nc.dram_tensor("spill_r", [D, T], F32, kind="Internal").ap()
    st.spill_y1 = nc.dram_tensor("spill_y1", [D, T], BF16, kind="Internal").ap()
    st.spill_h2 = nc.dram_tensor("spill_h2", [22 * 128, T], BF16,
                                 kind="Internal").ap()
    st.spill_qkv = nc.dram_tensor("spill_qkv", [3 * D, T], BF16,
                                  kind="Internal").ap()
    st.spill_ln3 = nc.dram_tensor("spill_ln3", [D, T], BF16,
                                  kind="Internal").ap()
    st.spill_node = [nc.dram_tensor(f"spill_node{i}", [D, T], BF16,
                                    kind="Internal").ap() for i in range(6)]
    st.spill_e = [nc.dram_tensor(f"spill_e{i}", [D, T], BF16,
                                 kind="Internal").ap() for i in range(3)]
    st.cc_in = nc.dram_tensor("cc_in", [1, 128], F32, kind="Internal")
    st.cc_out = nc.dram_tensor("cc_out", [1, 128], F32, kind="Internal")
    st.bscr = nc.dram_tensor("bscr", [1, 2], F32, kind="Internal").ap()
    st.rnd_i32 = nc.dram_tensor("rnd_i32", [128, T], I32, kind="Internal").ap()
    st.debug_stage = debug_stage

    es = contextlib.ExitStack()
    with es:
        R_t = es.enter_context(nc.sbuf_tensor("Rbuf", [128, KC * T], F32))
        AR_t = es.enter_context(nc.sbuf_tensor("Arena", [128, AR_BYTES // 2],
                                               BF16))
        ROWS_t = es.enter_context(nc.sbuf_tensor("Rows", [128, 4096], F32))
        CON_t = es.enter_context(nc.sbuf_tensor("Consts", [128, 4096], BF16))
        st.R = R_t.ap()
        st.ar = Arena(AR_t.ap(), AR_BYTES)
        st.ROWS = ROWS_t.ap()
        st.con = Arena(CON_t.ap(), 8192)
        pb = [es.enter_context(nc.psum_tensor(f"pb{i}", [128, 512], F32))
              for i in range(6)]
        pbf = [es.enter_context(nc.psum_tensor(f"pbf{i}", [128, 1024], BF16))
               for i in range(2)]
        st.pb = [Bank(p.ap()) for p in pb]
        st.pbf = [Bank(p.ap()) for p in pbf]
        sems = {n: es.enter_context(nc.semaphore(n))
                for n in ["dq", "dg", "da", "t", "v", "s", "g", "cc"]}
        st.sems = sems
        st.k = KB(nc, sems)
        st.r_tok = [None] * KC
        st.r_readers = [[] for _ in range(KC)]
        st.dbg_toks = []

        emit_all(st)

        k = st.k
        with nc.Block() as block:
            @block.sync
            def _(e):
                for f in k.ops["sync"]:
                    f(e)

            @block.gpsimd
            def _(e):
                for f in k.ops["gpsimd"]:
                    f(e)

            @block.scalar
            def _(e):
                for f in k.ops["scalar"]:
                    f(e)

            @block.vector
            def _(e):
                for f in k.ops["vector"]:
                    f(e)

            @block.tensor
            def _(e):
                for f in k.ops["tensor"]:
                    f(e)
    return nc


_CONSTS = None


def _prepare_weights(base):
    m = dict(base)

    def col(name):
        m[name] = np.ascontiguousarray(np.asarray(m[name],
                                                  dtype=np.float32).reshape(-1, 1))
    for name in ["L0_attn_norm_in_w", "L0_attn_conv_b", "L0_attn_bau",
                 "L0_attn_bad", "L0_attn_bbeta", "L0_attn_bu1", "L0_attn_bu2",
                 "L0_attn_ff_norm_w", "L1_attn_b_qkv", "L1_attn_b_fc",
                 "L1_ffn_ln_g", "L1_ffn_ln_b", "L1_ffn_act_w",
                 "L0_ffn_clip_x", "L0_ffn_clip_down", "L1_ffn_ew",
                 "L0_attn_alpha_spike", "L0_attn_beta_spike",
                 "L0_attn_head_norm_w", "L1_ffn_be"]:
        col(name)
    m["L0_attn_conv_w"] = np.ascontiguousarray(
        np.asarray(m["L0_attn_conv_w"], dtype=np.float32).reshape(D, 4))
    m["L1_ffn_ew_row"] = np.asarray(base["L1_ffn_ew"],
                                    np.float32).reshape(1, 18)
    m["L1_ffn_act_w_row"] = np.asarray(base["L1_ffn_act_w"],
                                       np.float32).reshape(1, 6)
    We3 = np.asarray(m.pop("L1_ffn_We"), np.float32).reshape(20, D, D)
    for e_ in range(20):
        m[f"L1_ffn_We{e_}"] = We3[e_]
    for w in ("Wq", "Wk", "Wv", "Wo"):
        W3 = np.asarray(m.pop("L1_ffn_" + w), np.float32).reshape(6, D, D)
        for c_ in range(6):
            m[f"L1_ffn_{w}{c_}"] = W3[c_]
    for name in list(m):
        m[name] = np.ascontiguousarray(np.asarray(m[name], dtype=np.float32))
    return m


def kernel(**inputs):
    global _CONSTS
    if _CONSTS is None:
        _CONSTS = _host_constants()
    x = np.asarray(inputs["L0_attn_x"])
    base = {n: v for n, v in inputs.items() if n != "L0_attn_x"}
    m = _prepare_weights(base)
    m.update(_CONSTS)
    in_maps = []
    for b in range(NCORES):
        d = dict(m)
        d["xT"] = np.ascontiguousarray(x[b].T)
        in_maps.append(d)
    nc = build_program(DEBUG_STAGE)
    res = run_bass_kernel_spmd(nc, in_maps, core_ids=list(range(NCORES)))
    outs = [np.asarray(res.results[b]["out"]).T for b in range(NCORES)]
    return np.ascontiguousarray(np.stack(outs, axis=0))


# version 05 (final)
# speedup vs baseline: 1.1103x; 1.0181x over previous
import contextlib
import numpy as np
import ml_dtypes
import concourse.bass as bass
import concourse.mybir as mybir
from concourse.bass_utils import run_bass_kernel_spmd

F32 = mybir.dt.float32
BF16 = mybir.dt.bfloat16
I32 = mybir.dt.int32
AF = mybir.ActivationFunctionType
ALU = mybir.AluOpType
AX = mybir.AxisListType
BF = ml_dtypes.bfloat16

T = 2048
D = 1024
KC = 8
TS = 512
NSL = 4
NCORES = 8
CH = 64
NCH = T // CH

DEBUG_STAGE = 0
DBG_TAGS = set()
CC_RANKS = None

ESEM = {"tensor": "t", "vector": "v", "scalar": "s", "gpsimd": "g"}


class KB:
    ENGINES = ("sync", "gpsimd", "scalar", "vector", "tensor")

    def __init__(self, nc, sems):
        self.nc = nc
        self.sems = sems
        self.cnt = {n: 0 for n in sems}
        self.ops = {e: [] for e in self.ENGINES}
        self.waited = {e: {} for e in self.ENGINES}

    def op(self, eng, fn, inc=None):
        if inc is not None:
            sem, n = inc
            self.cnt[sem] += n
            val = self.cnt[sem]
            h = self.sems[sem]

            def wrapped(e, fn=fn, h=h, n=n):
                fn(e).then_inc(h, n)
            self.ops[eng].append(wrapped)
            return (sem, val)
        self.ops[eng].append(fn)
        return None

    def sig(self, eng, fn):
        return self.op(eng, fn, inc=(ESEM[eng], 1))

    def wait(self, eng, tok):
        if tok is None:
            return
        sem, val = tok
        if self.waited[eng].get(sem, 0) >= val:
            return
        self.waited[eng][sem] = val
        h = self.sems[sem]
        self.ops[eng].append(lambda e, h=h, val=val: e.wait_ge(h, val))

    def waits(self, eng, toks):
        for t in toks:
            self.wait(eng, t)

    def dma(self, queue, out, in_, waits=(), slow=False):
        sem = {"sync": "dq", "gpsimd": "dg", "scalar": "da"}[queue]
        self.waits(queue, waits)
        return self.op(queue,
                       lambda e, o=out, i=in_: e.dma_start(
                           out=o, in_=i, allow_slow_non_contiguous=slow),
                       inc=(sem, 16))

    def tcnt(self):
        return ("t", self.cnt["t"])

    def vcnt(self):
        return ("v", self.cnt["v"])

    def scnt(self):
        return ("s", self.cnt["s"])


class Bank:
    __slots__ = ("ap", "free")

    def __init__(self, ap):
        self.ap = ap
        self.free = None


class Arena:
    def __init__(self, bf_ap, nbytes):
        self.bf = bf_ap
        self.f32 = bf_ap.bitcast(F32)
        self.nbytes = nbytes

    def b(self, off, rows, cols):
        assert off % 2 == 0 and off + 2 * cols <= self.nbytes, (off, cols)
        return self.bf[0:rows, off // 2: off // 2 + cols]

    def b_at(self, off, r0, rows, cols):
        assert off % 2 == 0 and off + 2 * cols <= self.nbytes, (off, cols)
        return self.bf[r0:r0 + rows, off // 2: off // 2 + cols]

    def f(self, off, rows, cols):
        assert off % 4 == 0 and off + 4 * cols <= self.nbytes, (off, cols)
        return self.f32[0:rows, off // 4: off // 4 + cols]


def _host_constants():
    c = {}
    c["c_ident_bf"] = np.eye(128, dtype=np.float32).astype(BF)
    c["c_onesrow_bf"] = np.ones((1, 128), dtype=np.float32).astype(BF)
    c["c_onescol_bf"] = np.ones((128, 1), dtype=np.float32).astype(BF)
    sel = np.zeros((8, 4 * 128), dtype=np.float32)
    for cc in range(4):
        for p in range(128):
            sel[cc * 2 + (p // 64), cc * 128 + p] = 1.0
    c["c_headsel_bf"] = np.ascontiguousarray(sel.astype(BF))
    bo8 = np.zeros((128, 32), dtype=np.float32)
    for cc in range(4):
        for p in range(128):
            bo8[p, cc * 8 + 2 * cc + (p // 64)] = 1.0
    c["c_bo8_bf"] = np.ascontiguousarray(bo8.astype(BF))
    lt = np.tril(np.ones((CH, CH), dtype=np.float32), -1)
    ut = np.triu(np.ones((CH, CH), dtype=np.float32), 1)
    ui = np.triu(np.ones((CH, CH), dtype=np.float32), 0)
    c["c_ltmask"] = np.ascontiguousarray(np.concatenate([lt, lt], axis=0))
    c["c_utmask"] = np.ascontiguousarray(np.concatenate([ut, ut], axis=0))
    c["c_uimask"] = np.ascontiguousarray(np.concatenate([ui, ui], axis=0))
    c["c_istack_bf"] = np.ascontiguousarray(
        np.concatenate([np.eye(CH, dtype=np.float32)] * 2, axis=0)).astype(BF)
    DL1 = 64
    inv_freq = 1.0 / (10000.0 ** (np.arange(0, DL1, 2, dtype=np.float32) / DL1))
    bd = np.zeros((8, 128, 128), dtype=np.float32)
    for ck in range(8):
        for hh in range(2):
            ang = (ck * 2 + hh) * inv_freq
            co, si = np.cos(ang), np.sin(ang)
            for j in range(32):
                b = hh * 64 + 2 * j
                bd[ck, b, b] = co[j]
                bd[ck, b + 1, b] = -si[j]
                bd[ck, b, b + 1] = si[j]
                bd[ck, b + 1, b + 1] = co[j]
    c["c_rope_bd_bf"] = np.ascontiguousarray(bd.reshape(8 * 128, 128)).astype(BF)
    c["c_onesrow_f"] = np.ones((1, 128), dtype=np.float32)
    c["c_ident_f"] = np.eye(128, dtype=np.float32)
    return c


class St:
    pass


def dbg_dump(st, tag, ap, toks=()):
    if tag not in DBG_TAGS:
        return
    k = st.k
    name = "dbg_" + tag
    dt = ap.dtype
    sh = list(ap.shape)
    dram = st.nc.dram_tensor(name, sh, dt, kind="ExternalOutput").ap()
    k.waits("sync", list(toks))
    for eng in ("tensor", "vector", "scalar", "gpsimd"):
        sem = ESEM[eng]
        k.wait("sync", (sem, k.cnt[sem]))
    tok = k.dma("sync", dram, ap)
    for eng in ("tensor", "vector", "scalar", "gpsimd"):
        k.wait(eng, tok)
    st.dbg_toks.append(tok)


def sv(st, name, col=None, rows=None, ncols=1):
    ap = st.small[name]
    if col is not None:
        ap = ap[:, col:col + ncols]
    if rows is not None:
        ap = ap[0:rows, :]
    return ap


def crow(st, r, ncols=T, rows=1):
    return st.scr[r:r + rows, 0:ncols]


def mm_raw(st, bank, mms, pre_waits=()):
    k = st.k
    k.wait("tensor", bank.free)
    k.waits("tensor", pre_waits)
    n = len(mms)
    tok = None
    for i, fn in enumerate(mms):
        last = i == n - 1
        tok = k.op("tensor",
                   lambda e, fn=fn, s=(i == 0), p=last: fn(e, s, p),
                   inc=(("t", 1) if last else None))
    return tok


def mm_group(st, bank, mms, consumer_eng, consumer_fn, pre_waits=(),
             consumer_waits=()):
    k = st.k
    tok = mm_raw(st, bank, mms, pre_waits)
    k.wait(consumer_eng, tok)
    k.waits(consumer_eng, consumer_waits)
    ctok = k.sig(consumer_eng, consumer_fn)
    bank.free = ctok
    return ctok


def stack_evac(st, bank, dst, mmtok):
    k = st.k
    k.wait("scalar", mmtok)
    k.sig("scalar", lambda e: e.copy(dst[:, 0:64], bank.ap[0:64, 0:64]))
    t2 = k.sig("scalar", lambda e: e.copy(dst[:, 64:128], bank.ap[64:128, 64:128]))
    bank.free = t2
    return t2


def wload_bf(st, dram, col0, ncols, arena_off, waits=(), kc_rows=None):
    nrows = dram.shape[0]
    if nrows < 128:
        dst = st.ar.b_at(arena_off, 0, nrows, ncols)
        return st.k.dma("gpsimd", dst, dram[:, col0:col0 + ncols], waits=waits)
    if kc_rows is None:
        kc_rows = nrows // 128
    dst = st.ar.b(arena_off, 128, kc_rows * ncols)
    src = dram.rearrange("(c p) o -> p c o", p=128)[:, :, col0:col0 + ncols]
    return st.k.dma("gpsimd", dst.rearrange("p (c o) -> p c o", o=ncols), src,
                    waits=waits)


def emit_proj(st, wdram, in_at, n_in, out_cols, evac, wslot, wslot_sz,
              in_toks=(), mcg=4, bank_ids=(4, 5), w_pre=None, nsl=NSL,
              tslice=TS, tok0=0, w_waits=()):
    k = st.k
    if not hasattr(st, "wslot_readers"):
        st.wslot_readers = {}
    ngroups = (out_cols + mcg * 128 - 1) // (mcg * 128)
    gsz = n_in * mcg * 128
    assert 2 * 2 * gsz <= wslot_sz, (gsz, wslot_sz)
    wread = [st.wslot_readers.get((wslot, 0)), st.wslot_readers.get((wslot, 1))]
    for g in range(ngroups):
        col0 = g * mcg * 128
        ncols = min(mcg * 128, out_cols - col0)
        slot = g % 2
        aoff = wslot + slot * 2 * gsz
        wt = wload_bf(st, wdram, col0, ncols, aoff,
                      waits=([wread[slot]] if wread[slot] else []) +
                      list(w_waits))
        wap = st.ar.b(aoff, 128, n_in * ncols)
        if w_pre is not None:
            wt = w_pre(col0, ncols, wap, wt)
        nmc = (ncols + 127) // 128
        for mc in range(nmc):
            mco = col0 // 128 + mc
            for s in range(nsl):
                bank = st.pb[bank_ids[(mc * nsl + s) % len(bank_ids)]]
                mms = []
                for kc in range(n_in):
                    def mk(kc=kc, wap=wap, bank=bank, ncols=ncols, mc=mc, s=s):
                        def f(e, st_, sp):
                            return e.matmul(
                                bank.ap[:, 0:tslice],
                                wap[:, kc * ncols + mc * 128:
                                    kc * ncols + mc * 128 + 128],
                                in_at(kc)[:, tok0 + s * tslice:
                                          tok0 + (s + 1) * tslice],
                                start=st_, stop=sp)
                        return f
                    mms.append(mk())
                mmtok = mm_raw(st, bank, mms,
                               pre_waits=[wt] + list(in_toks))
                evac(mco, s, bank, mmtok)
        wread[slot] = k.tcnt()
        st.wslot_readers[(wslot, slot)] = wread[slot]
    return k.tcnt()


def emit_norm(st, src_at, out_bf_at, kind, eps, tr_off, bc_off,
              gain_at=None, bias_at=None, src_toks=(), also_f32_at=None):
    k = st.k
    ar = st.ar
    eps_ap = st.EPS5 if eps == 1e-5 else st.EPS6
    tr = ar.b(tr_off, 128, TS)
    rowb = ar.b(tr_off + 2 * TS, 1, 2 * TS)
    a_bc = ar.f(bc_off, 128, T)
    need_mean = kind == "ln"
    if need_mean:
        b_bc = ar.f(bc_off + 4 * T, 128, T)
        tmp = ar.f(bc_off + 8 * T, 128, T)
    else:
        b_bc = None
        tmp = ar.f(bc_off + 4 * T, 128, T)
    sum_r = st.nr_sum
    ssq_r = st.nr_ssq
    a_r = st.nr_a
    t_r = st.nr_t
    k.waits("scalar", list(src_toks))
    bc_evs = []
    for s in range(NSL):
        cs = slice(s * TS, (s + 1) * TS)
        passes = ((("sum", AF.Copy, st.pb[0], sum_r),) if need_mean else ()) + \
                 (("ssq", AF.Square, st.pb[1], ssq_r),)
        for nm, fn, bank, row in passes:
            for c in range(KC):
                k.wait("scalar", k.tcnt())
                ct = k.sig("scalar", lambda e, c=c, fn=fn, cs=cs: e.activation(
                    tr, src_at(c)[:, cs], fn))
                if c == 0:
                    k.wait("tensor", bank.free)
                k.wait("tensor", ct)
                k.op("tensor", lambda e, b=bank, c=c: e.matmul(
                    b.ap[0:1, 0:TS], st.CONEC, tr,
                    start=(c == 0), stop=(c == KC - 1)),
                    inc=("t", 1))
            mmtok = k.tcnt()
            k.wait("scalar", mmtok)
            ev = k.sig("scalar", lambda e, b=bank, row=row: e.copy(
                row, b.ap[0:1, 0:TS]))
            bank.free = ev
            k.wait("vector", ev)
        if need_mean:
            k.sig("vector", lambda e: e.tensor_scalar(
                out=sum_r, in0=sum_r, scalar1=1.0 / D, scalar2=None,
                op0=ALU.mult))
            k.sig("vector", lambda e: e.tensor_tensor(
                out=t_r, in0=sum_r, in1=sum_r, op=ALU.mult))
            vt = k.sig("vector", lambda e: e.scalar_tensor_tensor(
                out=t_r, in0=ssq_r, scalar=1.0 / D, in1=t_r,
                op0=ALU.mult, op1=ALU.subtract))
        else:
            vt = k.sig("vector", lambda e: e.tensor_scalar(
                out=t_r, in0=ssq_r, scalar1=1.0 / D, scalar2=None,
                op0=ALU.mult))
        k.wait("scalar", vt)
        k.sig("scalar", lambda e: e.activation(
            t_r, t_r, AF.Ln, bias=eps_ap[0:1, :], scale=1.0))
        sq = k.sig("scalar", lambda e: e.activation(
            a_r, t_r, AF.Exp, scale=-0.5))
        k.wait("vector", sq)
        k.wait("vector", k.tcnt())
        rva = k.sig("vector", lambda e: e.tensor_copy(rowb[0:1, 0:TS], a_r))
        rvb = None
        if need_mean:
            k.sig("vector", lambda e: e.scalar_tensor_tensor(
                out=t_r, in0=sum_r, scalar=-1.0, in1=a_r,
                op0=ALU.mult, op1=ALU.mult))
            rvb = k.sig("vector", lambda e: e.tensor_copy(
                rowb[0:1, TS:2 * TS], t_r))
        srcs = [(0, a_bc, rva)] + ([(TS, b_bc, rvb)] if need_mean else [])
        for (coff, dstbc, rtok) in srcs:
            bank = st.pb[2]
            k.wait("tensor", bank.free)
            k.wait("tensor", rtok)
            mt = k.op("tensor", lambda e, b=bank, coff=coff: e.matmul(
                b.ap[:, 0:TS], st.CONER, rowb[0:1, coff:coff + TS],
                start=True, stop=True), inc=("t", 1))
            k.wait("scalar", mt)
            ev = k.sig("scalar", lambda e, b=bank, cs=cs, dstbc=dstbc: e.copy(
                dstbc[:, cs], b.ap[:, 0:TS]))
            bank.free = ev
            bc_evs.append(ev)
    out_toks = []
    k.waits("vector", bc_evs)
    for c in range(KC):
        if need_mean:
            k.sig("vector", lambda e, c=c: e.tensor_tensor(
                out=tmp, in0=src_at(c), in1=a_bc, op=ALU.mult))
            if gain_at is None:
                tk = k.sig("vector", lambda e, c=c: e.tensor_tensor(
                    out=out_bf_at(c), in0=tmp, in1=b_bc, op=ALU.add))
            else:
                k.sig("vector", lambda e, c=c: e.tensor_tensor(
                    out=tmp, in0=tmp, in1=b_bc, op=ALU.add))
                tk = k.sig("vector", lambda e, c=c: e.tensor_scalar(
                    out=out_bf_at(c), in0=tmp, scalar1=gain_at(c),
                    scalar2=(bias_at(c) if bias_at else None),
                    op0=ALU.mult,
                    op1=(ALU.add if bias_at else Ellipsis)))
        else:
            if gain_at is None:
                tk = k.sig("vector", lambda e, c=c: e.tensor_tensor(
                    out=out_bf_at(c), in0=src_at(c), in1=a_bc, op=ALU.mult))
            else:
                k.sig("vector", lambda e, c=c: e.tensor_tensor(
                    out=tmp, in0=src_at(c), in1=a_bc, op=ALU.mult))
                tk = k.sig("vector", lambda e, c=c: e.tensor_scalar(
                    out=out_bf_at(c), in0=tmp, scalar1=gain_at(c),
                    scalar2=None, op0=ALU.mult))
        out_toks.append(tk)
    return out_toks


O_A = 0
O_B = 32768
O_C = 65536
O_CS = 98304
AR_BYTES = 112 * 1024

O_LN0 = O_A
O_H = O_B
O_GATE = O_C
OQ = O_A
OKK = O_A + 16384
OAB = O_B
OSPK = O_B + 16384
OV = O_C
ODR = O_C + 16384


def qp(st, name, cc):
    base = {"q": OQ, "k": OKK, "v": OV, "dr": ODR, "ab": OAB, "spk": OSPK}[name]
    return st.ar.b(base + cc * 4096, 128, T)


def a_ln0(st, c):
    return st.ar.b(O_LN0 + c * 4096, 128, T)


def a_h(st, c):
    return st.ar.b(O_H + c * 4096, 128, T)


def a_gate(st, c):
    return st.ar.b(O_GATE + c * 4096, 128, T)


def emit_l0_attn(st):
    k = st.k

    ln_toks = emit_norm(
        st, lambda c: st.R[:, c * T:(c + 1) * T], lambda c: a_ln0(st, c),
        "ln", 1e-5, tr_off=O_B, bc_off=O_C,
        src_toks=[t for t in st.r_tok if t])
    for c in range(KC):
        dbg_dump(st, f"ln0_{c}", a_ln0(st, c), [ln_toks[c]])
    emit_rms_conv_h(st, ln_toks)
    for c in range(KC):
        dbg_dump(st, f"h_{c}", a_h(st, c), [st.h_tok[c]])
    gtoks = emit_gate(st)
    for c in range(KC):
        dbg_dump(st, f"gate_{c}", a_gate(st, c), [gtoks[c]])
    dbg_dump(st, "gate_ps", a_gate(st, 0), [gtoks[0]])
    sp = []
    for c in range(KC):
        sp.append(k.dma("sync", st.spill_ln0[c * 128:(c + 1) * 128, :],
                        a_ln0(st, c), waits=[ln_toks[c]]))
        sp.append(k.dma("sync", st.spill_gate[c * 128:(c + 1) * 128, :],
                        a_gate(st, c), waits=[gtoks[c]]))
    st.l0_spill_toks = sp
    emit_l0_projections(st)
    for cc in range(4):
        dbg_dump(st, f"q_{cc}", qp(st, "q", cc), [st.qk_tok])
        dbg_dump(st, f"k_{cc}", qp(st, "k", cc), [st.qk_tok])
        dbg_dump(st, f"v_{cc}", qp(st, "v", cc), [st.v_tok])
        dbg_dump(st, f"drive_{cc}", qp(st, "dr", cc), [st.drive_tok])
        dbg_dump(st, f"ab_{cc}", qp(st, "ab", cc), [st.ab_tok])
    emit_l0_scan(st)
    for cc in range(4):
        dbg_dump(st, f"spk_{cc}", qp(st, "spk", cc), [st.scan_tok])
    emit_l0_decay(st)
    dbg_dump(st, "beta", st.ar.b_at(O_CS + 12288, 0, 8, T), [st.decay_tok])
    dbg_dump(st, "pc", st.misc[:, 8:136], [st.decay_tok])
    for cc in range(4):
        dbg_dump(st, f"P_{cc}", qp(st, "dr", cc), [st.decay_tok])
        dbg_dump(st, f"Pinv_{cc}", qp(st, "spk", cc), [st.decay_tok])
        dbg_dump(st, f"kap_{cc}", qp(st, "ab", cc), [st.decay_tok])
        dbg_dump(st, f"kt_{cc}", qp(st, "k", cc), [st.decay_tok])
        dbg_dump(st, f"qt_{cc}", qp(st, "q", cc), [st.decay_tok])
    emit_l0_deltanet(st)
    for cc in range(4):
        dbg_dump(st, f"yfm_{cc}", st.ar.b(OQ + cc * 4096, 128, T),
                 [st.y_tok])
    emit_l0_tail(st)


def emit_rms_conv_h(st, ln_toks):
    k = st.k
    ar = st.ar
    ssq_r, a_r, t_r = st.nr_ssq, st.nr_a, st.nr_t
    tr = ar.b(O_CS, 128, TS)
    rowb = ar.b(O_CS + 1024, 1, TS)
    a_bc = ar.f(O_CS + 4096, 128, T)
    k.waits("scalar", ln_toks)
    bc_evs = []
    for s in range(NSL):
        cs = slice(s * TS, (s + 1) * TS)
        bank = st.pb[1]
        for c in range(KC):
            k.wait("scalar", k.tcnt())
            ct = k.sig("scalar", lambda e, c=c, cs=cs: e.activation(
                tr, a_ln0(st, c)[:, cs], AF.Square))
            if c == 0:
                k.wait("tensor", bank.free)
            k.wait("tensor", ct)
            k.op("tensor", lambda e, b=bank, c=c: e.matmul(
                b.ap[0:1, 0:TS], st.CONEC, tr,
                start=(c == 0), stop=(c == KC - 1)),
                inc=("t", 1))
        mmtok = k.tcnt()
        k.wait("scalar", mmtok)
        ev = k.sig("scalar", lambda e, b=bank: e.copy(ssq_r, b.ap[0:1, 0:TS]))
        bank.free = ev
        k.wait("vector", ev)
        vt = k.sig("vector", lambda e: e.tensor_scalar(
            out=t_r, in0=ssq_r, scalar1=1.0 / D, scalar2=None, op0=ALU.mult))
        k.wait("scalar", vt)
        k.sig("scalar", lambda e: e.activation(
            t_r, t_r, AF.Ln, bias=st.EPS6[0:1, :], scale=1.0))
        sq = k.sig("scalar", lambda e: e.activation(
            a_r, t_r, AF.Exp, scale=-0.5))
        k.wait("vector", sq)
        k.wait("vector", k.tcnt())
        rva = k.sig("vector", lambda e: e.tensor_copy(rowb, a_r))
        bank = st.pb[2]
        k.wait("tensor", bank.free)
        k.wait("tensor", rva)
        mt = k.op("tensor", lambda e, b=bank: e.matmul(
            b.ap[:, 0:TS], st.CONER, rowb, start=True, stop=True),
            inc=("t", 1))
        k.wait("scalar", mt)
        ev = k.sig("scalar", lambda e, b=bank, cs=cs: e.copy(
            a_bc[:, cs], b.ap[:, 0:TS]))
        bank.free = ev
        bc_evs.append(ev)
    cs1 = ar.f(O_C, 128, T + 3)
    cs2 = ar.f(O_C + 4 * (T + 4), 128, T)
    k.sig("vector", lambda e: e.memset(cs1[:, 0:3], 0.0))
    k.waits("vector", bc_evs)
    dbg_dump(st, "rmsabc", a_bc, bc_evs)
    st.h_tok = []
    for c in range(KC):
        if st.h_tok:
            k.wait("vector", st.h_tok[-1])
        k.sig("vector", lambda e, c=c: e.tensor_tensor(
            out=cs2, in0=a_ln0(st, c), in1=a_bc, op=ALU.mult))
        k.sig("vector", lambda e, c=c: e.tensor_scalar(
            out=cs1[:, 3:3 + T], in0=cs2,
            scalar1=sv(st, "L0_attn_norm_in_w", col=c), scalar2=None,
            op0=ALU.mult))
        k.sig("vector", lambda e, c=c: e.tensor_scalar(
            out=cs2, in0=cs1[:, 0:T],
            scalar1=sv(st, "L0_attn_conv_w", col=4 * c), scalar2=None,
            op0=ALU.mult))
        for j in range(1, 4):
            k.sig("vector", lambda e, c=c, j=j: e.scalar_tensor_tensor(
                out=cs2, in0=cs1[:, j:j + T],
                scalar=sv(st, "L0_attn_conv_w", col=4 * c + j), in1=cs2,
                op0=ALU.mult, op1=ALU.add))
        vt2 = k.vcnt()
        if c == 0:
            dbg_dump(st, "cs1_0", cs1, [vt2])
            dbg_dump(st, "cs2_0", cs2, [vt2])
        k.wait("scalar", vt2)
        ht = k.sig("scalar", lambda e, c=c: e.activation(
            a_h(st, c), cs2, AF.Silu,
            bias=sv(st, "L0_attn_conv_b", col=c), scale=1.0))
        st.h_tok.append(ht)


def emit_gate(st):
    k = st.k
    ar = st.ar
    dr = st.dr
    wu1 = wload_bf(st, dr["L0_attn_Wu1"], 0, 64, O_CS + 2048,
                   waits=[st.h_tok[-1]])
    wu2 = wload_bf(st, dr["L0_attn_Wu2"], 0, D, O_CS + 4096, kc_rows=1,
                   waits=[st.h_tok[-1]])
    u = ar.b(O_CS + 8192, 64, T)
    for s in range(NSL):
        bank = st.pb[4]
        mms = []
        for kc in range(KC):
            def mk(kc=kc, s=s, bank=bank):
                def f(e, st_, sp):
                    return e.matmul(bank.ap[0:64, 0:TS],
                                    ar.b(O_CS + 2048 + kc * 128, 128, 64),
                                    a_ln0(st, kc)[:, s * TS:(s + 1) * TS],
                                    start=st_, stop=sp)
                return f
            mms.append(mk())
        mm_group(st, bank, mms, "scalar",
                 lambda e, s=s, bank=bank: e.activation(
                     u[:, s * TS:(s + 1) * TS], bank.ap[0:64, 0:TS], AF.Silu,
                     bias=sv(st, "L0_attn_bu1", rows=64), scale=1.0),
                 pre_waits=[wu1, st.h_tok[-1]])
    utok = k.scnt()
    gtoks = []
    for c in range(KC):
        tok = None
        for s in range(NSL):
            bank = st.pb[5]
            def mk(c=c, s=s, bank=bank):
                def f(e, st_, sp):
                    return e.matmul(
                        bank.ap[:, 0:TS],
                        ar.b(O_CS + 4096, 64, D)[:, c * 128:(c + 1) * 128],
                        u[:, s * TS:(s + 1) * TS], start=st_, stop=sp)
                return f
            tok = mm_group(st, bank, [mk()], "scalar",
                           lambda e, c=c, s=s, bank=bank: e.activation(
                               a_gate(st, c)[:, s * TS:(s + 1) * TS],
                               bank.ap[:, 0:TS], AF.Sigmoid,
                               bias=sv(st, "L0_attn_bu2", col=c), scale=1.0),
                           pre_waits=[wu2, utok])
        gtoks.append(tok)
    return gtoks


def emit_l0_projections(st):
    k = st.k
    ar = st.ar
    dr = st.dr
    h_at = lambda kc: a_h(st, kc)
    h_toks = list(st.h_tok)
    k.waits("gpsimd", st.l0_spill_toks)

    def evac_to(name):
        def evac(mco, s, bank, mmtok):
            k.wait("scalar", mmtok)
            ev = k.sig("scalar", lambda e, mco=mco, s=s, bank=bank: e.copy(
                qp(st, name, mco)[:, s * TS:(s + 1) * TS], bank.ap[:, 0:TS]))
            bank.free = ev
        return evac

    WSL = O_CS
    emit_proj(st, dr["L0_attn_Wspike"], h_at, KC, 512, evac_to("dr"),
              WSL, 16384, in_toks=h_toks)
    st.drive_tok = k.scnt()
    emit_proj(st, dr["L0_attn_Wq"], h_at, KC, 512, evac_to("q"),
              WSL, 16384, in_toks=h_toks)
    st.q_tok = k.scnt()
    emit_proj(st, dr["L0_attn_Wk"], h_at, KC, 512, evac_to("k"),
              WSL, 16384, in_toks=h_toks)
    st.k_tok = k.scnt()
    emit_proj(st, dr["L0_attn_Wv"], h_at, KC, 512, evac_to("v"),
              WSL, 16384, in_toks=h_toks)
    st.v_tok = k.scnt()

    wau = wload_bf(st, dr["L0_attn_Wau"], 0, 64, WSL, waits=[k.tcnt()])
    wad = wload_bf(st, dr["L0_attn_Wad"], 0, 512, WSL + 1024, kc_rows=1,
                   waits=[k.tcnt()])
    wbeta = wload_bf(st, dr["L0_attn_Wbeta"], 0, 8, WSL + 2048,
                     waits=[k.tcnt()])
    u2 = ar.b(WSL + 8192, 64, T)
    for s in range(NSL):
        bank = st.pb[4]
        mms = []
        for kc in range(KC):
            def mk(kc=kc, s=s, bank=bank):
                def f(e, st_, sp):
                    return e.matmul(bank.ap[0:64, 0:TS],
                                    ar.b(WSL + kc * 128, 128, 64),
                                    h_at(kc)[:, s * TS:(s + 1) * TS],
                                    start=st_, stop=sp)
                return f
            mms.append(mk())
        mm_group(st, bank, mms, "scalar",
                 lambda e, s=s, bank=bank: e.activation(
                     u2[:, s * TS:(s + 1) * TS], bank.ap[0:64, 0:TS], AF.Silu,
                     bias=sv(st, "L0_attn_bau", rows=64), scale=1.0),
                 pre_waits=[wau] + h_toks)
    u2tok = k.scnt()
    beta_rows = ar.b_at(O_CS + 12288, 0, 8, T)
    for s in range(NSL):
        bank = st.pb[5]
        mms = []
        for kc in range(KC):
            def mk(kc=kc, s=s, bank=bank):
                def f(e, st_, sp):
                    return e.matmul(bank.ap[0:8, 0:TS],
                                    ar.b(WSL + 2048 + kc * 16, 128, 8),
                                    h_at(kc)[:, s * TS:(s + 1) * TS],
                                    start=st_, stop=sp)
                return f
            mms.append(mk())
        mm_group(st, bank, mms, "scalar",
                 lambda e, s=s, bank=bank: e.activation(
                     beta_rows[:, s * TS:(s + 1) * TS], bank.ap[0:8, 0:TS],
                     AF.Identity, bias=sv(st, "L0_attn_bbeta", rows=8),
                     scale=1.0),
                 pre_waits=[wbeta] + h_toks)
    st.betab_tok = k.scnt()
    for cc in range(4):
        for s in range(NSL):
            bank = st.pb[4]
            def mk(cc=cc, s=s, bank=bank):
                def f(e, st_, sp):
                    return e.matmul(
                        bank.ap[:, 0:TS],
                        ar.b(WSL + 1024, 64, 512)[:, cc * 128:(cc + 1) * 128],
                        u2[:, s * TS:(s + 1) * TS], start=st_, stop=sp)
                return f
            def ev(e, cc=cc, s=s, bank=bank):
                return e.copy(qp(st, "ab", cc)[:, s * TS:(s + 1) * TS],
                              bank.ap[:, 0:TS],
                              ) if False else e.activation(
                    qp(st, "ab", cc)[:, s * TS:(s + 1) * TS],
                    bank.ap[:, 0:TS], AF.Identity,
                    bias=sv(st, "L0_attn_bad", col=cc), scale=1.0)
            mm_group(st, bank, [mk()], "scalar", ev,
                     pre_waits=[wad, u2tok])
    st.ab_tok = k.scnt()
    emit_l0_qknorm(st)


def emit_l0_qknorm(st):
    k = st.k
    ar = st.ar
    sqb = ar.b(O_CS + 8192, 128, TS)
    rowsA = ar.f(O_CS, 8, T)[0:8, :]
    rab = ar.b_at(O_CS + 9216, 0, 8, TS)
    for name, tok in (("q", st.q_tok), ("k", st.k_tok)):
        for s in range(NSL):
            cs = slice(s * TS, (s + 1) * TS)
            bank = st.pb[0]
            for cc in range(4):
                k.waits("scalar", [tok, st.ab_tok])
                k.wait("scalar", k.tcnt())
                sqt = k.sig("scalar", lambda e, cc=cc, name=name, cs=cs:
                            e.activation(sqb, qp(st, name, cc)[:, cs],
                                         AF.Square))
                if cc == 0:
                    k.wait("tensor", bank.free)
                k.wait("tensor", sqt)
                k.op("tensor", lambda e, b=bank, cc=cc: e.matmul(
                    b.ap[0:8, 0:TS], st.CONBO8[:, cc * 8:(cc + 1) * 8],
                    sqb, start=(cc == 0), stop=(cc == 3)),
                    inc=("t", 1))
            mmt = k.tcnt()
            k.wait("scalar", mmt)
            k.sig("scalar", lambda e, b=bank, cs=cs: e.activation(
                rowsA[:, cs], b.ap[0:8, 0:TS], AF.Ln))
            ev = k.sig("scalar", lambda e, cs=cs: e.activation(
                rowsA[:, cs], rowsA[:, cs], AF.Exp, scale=0.5))
            bank.free = ev
        k.wait("vector", k.scnt())
        k.sig("vector", lambda e: e.tensor_scalar(
            out=rowsA, in0=rowsA, scalar1=1e-6, scalar2=None, op0=ALU.add))
        rv = k.sig("vector", lambda e: e.reciprocal(rowsA, rowsA))
        for s in range(NSL):
            cs = slice(s * TS, (s + 1) * TS)
            k.wait("vector", rv)
            k.wait("vector", k.tcnt())
            rbt = k.sig("vector", lambda e, cs=cs: e.tensor_copy(
                rab, rowsA[:, cs]))
            for cc in range(4):
                bank = st.pb[4 + (cc % 2)]
                k.wait("tensor", bank.free)
                k.wait("tensor", rbt)
                mt = k.op("tensor", lambda e, b=bank, cc=cc: e.matmul(
                    b.ap[:, 0:TS], st.CONSEL[:, cc * 128:(cc + 1) * 128],
                    rab, start=True, stop=True), inc=("t", 1))
                k.wait("vector", mt)
                ct = k.sig("vector", lambda e, b=bank, cc=cc, cs=cs, name=name:
                           e.tensor_tensor(
                               out=qp(st, name, cc)[:, cs],
                               in0=qp(st, name, cc)[:, cs],
                               in1=b.ap[:, 0:TS], op=ALU.mult))
                bank.free = ct
    st.qk_tok = k.vcnt()


def emit_l0_scan(st):
    k = st.k
    ar = st.ar
    mem = st.misc[:, 0:4]
    k.wait("vector", st.drive_tok)
    k.sig("vector", lambda e: e.memset(mem, 0.0))
    dr4 = ar.bf[0:128, ODR // 2: ODR // 2 + 4 * T].rearrange(
        "p (c t) -> p c t", t=T)
    sp4 = ar.bf[0:128, OSPK // 2: OSPK // 2 + 4 * T].rearrange(
        "p (c t) -> p c t", t=T)
    for t in range(T):
        t1 = k.sig("vector", lambda e, t=t: e.scalar_tensor_tensor(
            out=mem, in0=mem, scalar=0.9, in1=dr4[:, :, t:t + 1],
            op0=ALU.mult, op1=ALU.add))
        k.wait("vector", t1)
        t2 = k.sig("vector", lambda e, t=t: e.tensor_scalar(
            out=sp4[:, :, t:t + 1], in0=mem, scalar1=0.5, scalar2=None,
            op0=ALU.is_gt))
        k.wait("vector", t2)
        t3 = k.sig("vector", lambda e, t=t: e.scalar_tensor_tensor(
            out=mem, in0=sp4[:, :, t:t + 1], scalar=-0.5, in1=mem,
            op0=ALU.mult, op1=ALU.add))
        k.wait("vector", t3)
    st.scan_tok = t3


def emit_l0_decay(st):
    k = st.k
    ar = st.ar
    beta_rows = ar.b_at(O_CS + 12288, 0, 8, T)
    actbf = ar.b_at(O_CS, 0, 8, T)
    tmp = ar.b(O_CS + 4096, 128, T)
    k.wait("vector", st.scan_tok)
    for s in range(NSL):
        cs = slice(s * TS, (s + 1) * TS)
        bank = st.pb[0]
        for cc in range(4):
            if cc == 0:
                k.wait("tensor", bank.free)
            k.wait("tensor", st.scan_tok)
            k.op("tensor", lambda e, b=bank, cc=cc, cs=cs: e.matmul(
                b.ap[0:8, 0:TS], st.CONBO8[:, cc * 8:(cc + 1) * 8],
                qp(st, "spk", cc)[:, cs], start=(cc == 0), stop=(cc == 3)),
                inc=("t", 1))
        mmt = k.tcnt()
        k.wait("vector", mmt)
        ev = k.sig("vector", lambda e, b=bank, cs=cs: e.tensor_scalar(
            out=actbf[:, cs], in0=b.ap[0:8, 0:TS],
            scalar1=0.0, scalar2=None, op0=ALU.is_gt))
        bank.free = ev
    abf = k.vcnt()
    k.wait("vector", st.betab_tok)
    for s in range(NSL):
        cs = slice(s * TS, (s + 1) * TS)
        bank = st.pb[1]
        for cc in range(4):
            k.wait("vector", k.tcnt())
            bspt = k.sig("vector", lambda e, cc=cc, cs=cs: e.tensor_scalar(
                out=tmp[:, 0:TS], in0=qp(st, "spk", cc)[:, cs],
                scalar1=sv(st, "L0_attn_beta_spike", col=cc), scalar2=None,
                op0=ALU.mult))
            if cc == 0:
                k.wait("tensor", bank.free)
            k.wait("tensor", bspt)
            k.op("tensor", lambda e, b=bank, cc=cc: e.matmul(
                b.ap[0:8, 0:TS], st.CONBO8[:, cc * 8:(cc + 1) * 8],
                tmp[:, 0:TS], start=(cc == 0), stop=(cc == 3)),
                inc=("t", 1))
        mmt = k.tcnt()
        k.wait("vector", mmt)
        bv = k.sig("vector", lambda e, b=bank, cs=cs: e.tensor_tensor(
            out=beta_rows[:, cs], in0=beta_rows[:, cs],
            in1=b.ap[0:8, 0:TS], op=ALU.add))
        bank.free = bv
    k.wait("scalar", k.vcnt())
    bt = k.sig("scalar", lambda e: e.activation(beta_rows, beta_rows,
                                                AF.Sigmoid))
    k.wait("vector", bt)
    k.wait("vector", abf)
    betabf = k.sig("vector", lambda e: e.tensor_tensor(
        out=beta_rows, in0=beta_rows, in1=actbf, op=ALU.mult))
    dbg_dump(st, "actbf", actbf, [abf])

    la = ar.f(O_CS + 4096, 128, 1024)
    cum = ar.f(O_CS + 8192, 128, 1024)
    pc = st.misc[:, 8:136]
    for cc in range(4):
        for hh in range(2):
            hsl = slice(hh * 1024, (hh + 1) * 1024)
            k.waits("vector", [st.ab_tok, betabf])
            k.wait("vector", k.scnt())
            stt_t = k.sig("vector", lambda e, cc=cc, hsl=hsl:
                          e.scalar_tensor_tensor(
                out=la, in0=qp(st, "spk", cc)[:, hsl],
                scalar=sv(st, "L0_attn_alpha_spike", col=cc),
                in1=qp(st, "ab", cc)[:, hsl], op0=ALU.mult, op1=ALU.add))
            k.wait("scalar", stt_t)
            at = k.sig("scalar", lambda e: e.activation(la, la, AF.Sigmoid))
            for s2 in range(2):
                bank = st.pb[s2]
                k.wait("tensor", bank.free)
                k.wait("tensor", abf)
                mt = k.op("tensor", lambda e, b=bank, cc=cc, hh=hh, s2=s2:
                          e.matmul(
                    b.ap[:, 0:TS], st.CONSEL[:, cc * 128:(cc + 1) * 128],
                    actbf[:, hh * 1024 + s2 * TS: hh * 1024 + (s2 + 1) * TS],
                    start=True, stop=True), inc=("t", 1))
                k.wait("vector", mt)
                k.wait("vector", at)
                ssl = slice(s2 * TS, (s2 + 1) * TS)
                k.sig("vector", lambda e, ssl=ssl: e.tensor_scalar(
                    out=la[:, ssl], in0=la[:, ssl], scalar1=-1.0,
                    scalar2=None, op0=ALU.add))
                k.sig("vector", lambda e, b=bank, ssl=ssl: e.tensor_tensor(
                    out=la[:, ssl], in0=la[:, ssl], in1=b.ap[:, 0:TS],
                    op=ALU.mult))
                ct = k.sig("vector", lambda e, ssl=ssl: e.tensor_scalar(
                    out=la[:, ssl], in0=la[:, ssl], scalar1=1.0,
                    scalar2=None, op0=ALU.add))
                bank.free = ct
            if cc == 0 and hh == 0:
                dbg_dump(st, "laAF", la, [k.vcnt()])
            k.wait("scalar", k.vcnt())
            lt = k.sig("scalar", lambda e: e.activation(la, la, AF.Ln))
            k.wait("vector", lt)
            for ch in range(16):
                k.op("vector", lambda e, ch=ch: e.tensor_tensor_scan(
                    out=cum[:, ch * CH:(ch + 1) * CH],
                    data0=la[:, ch * CH:(ch + 1) * CH],
                    data1=st.ZERO64, initial=0.0, op0=ALU.add, op1=ALU.add))
            cvt = k.sig("vector", lambda e: e.tensor_copy(
                cum[:, 0:1], cum[:, 0:1]))
            if cc == 0 and hh == 0:
                dbg_dump(st, "cum0", cum, [cvt])
            k.wait("scalar", cvt)
            k.sig("scalar", lambda e, cc=cc, hsl=hsl: e.activation(
                qp(st, "dr", cc)[:, hsl], cum, AF.Exp))
            k.sig("scalar", lambda e, cc=cc, hsl=hsl: e.activation(
                qp(st, "spk", cc)[:, hsl], cum, AF.Exp, scale=-1.0))
            k.sig("scalar", lambda e, cc=cc, hh=hh: e.activation(
                pc[:, cc * 32 + hh * 16: cc * 32 + (hh + 1) * 16],
                cum.rearrange("p (ch c) -> p ch c", c=CH)[:, :, CH - 1:CH],
                AF.Exp))
    st.p_tok = k.scnt()
    for cc in range(4):
        k.waits("vector", [st.p_tok, st.qk_tok])
        for s in range(NSL):
            bank = st.pb[s]
            k.wait("tensor", bank.free)
            k.wait("tensor", betabf)
            mt = k.op("tensor", lambda e, b=bank, cc=cc, s=s: e.matmul(
                b.ap[:, 0:TS], st.CONSEL[:, cc * 128:(cc + 1) * 128],
                beta_rows[:, s * TS:(s + 1) * TS], start=True, stop=True),
                inc=("t", 1))
            k.wait("vector", mt)
            ssl = slice(s * TS, (s + 1) * TS)
            k.sig("vector", lambda e, cc=cc, ssl=ssl: e.tensor_tensor(
                out=qp(st, "ab", cc)[:, ssl], in0=qp(st, "k", cc)[:, ssl],
                in1=qp(st, "spk", cc)[:, ssl], op=ALU.mult))
            ct = k.sig("vector", lambda e, b=bank, cc=cc, ssl=ssl:
                       e.tensor_tensor(
                           out=qp(st, "ab", cc)[:, ssl],
                           in0=qp(st, "ab", cc)[:, ssl],
                           in1=b.ap[:, 0:TS], op=ALU.mult))
            bank.free = ct
        k.sig("vector", lambda e, cc=cc: e.tensor_tensor(
            out=qp(st, "k", cc), in0=qp(st, "k", cc), in1=qp(st, "dr", cc),
            op=ALU.mult))
        k.sig("vector", lambda e, cc=cc: e.tensor_tensor(
            out=qp(st, "q", cc), in0=qp(st, "q", cc), in1=qp(st, "dr", cc),
            op=ALU.mult))
    st.decay_tok = k.vcnt()


DN_TILES = ["X0", "B0", "AT", "N", "X2", "B2", "X4", "B4", "X8", "B8",
            "X16", "B16", "X32", "RHS", "U", "KTM", "VTM", "YTM", "KPC"]


def emit_l0_deltanet(st):
    k = st.k
    ar = st.ar
    OY = OQ
    S = st.misc[:, 136:200]
    pc = st.misc[:, 8:136]
    toff = {n: O_CS + i * 128 for i, n in enumerate(DN_TILES)}

    def tb(n):
        return ar.b(toff[n], 128, 64)

    kpc = tb("KPC")
    Sbf = ar.b(O_CS + len(DN_TILES) * 128, 128, 256)
    pbT, pbB, pbA, pbP, pbR, pbU = st.pb
    pf0, pf1 = st.pbf
    k.wait("vector", st.decay_tok)
    for pp in range(4):
        k.sig("vector", lambda e, pp=pp: e.memset(
            st.misc[:, 136 + pp * 64:200 + pp * 64], 0.0))
    zt = k.sig("vector", lambda e: e.memset(Sbf, 0.0))
    s_tok = [zt] * 4

    def two(bank, mkA, mkB, out_col=64):
        return mm_raw(st, bank, [
            lambda e, s, p, mkA=mkA: mkA(e),
            lambda e, s, p, mkB=mkB: mkB(e),
        ], pre_waits=[])

    for c in range(NCH):
        cols = slice(c * CH, (c + 1) * CH)
        for pp in range(4):
            kA = qp(st, "k", pp)[0:64, cols]
            kB = qp(st, "k", pp)[64:128, cols]
            qA = qp(st, "q", pp)[0:64, cols]
            qB = qp(st, "q", pp)[64:128, cols]
            xA = qp(st, "ab", pp)[0:64, cols]
            xB = qp(st, "ab", pp)[64:128, cols]

            def pair(bank, lA, rA, lB, rB, pre):
                k.wait("tensor", bank.free)
                k.waits("tensor", pre)
                k.op("tensor", lambda e, bank=bank, lA=lA, rA=rA: e.matmul(
                    bank.ap[0:64, 0:64], lA, rA, start=True, stop=True))
                return k.op("tensor",
                            lambda e, bank=bank, lB=lB, rB=rB: e.matmul(
                                bank.ap[64:128, 0:64], lB, rB,
                                start=True, stop=True), inc=("t", 1))

            mt = pair(pbT, kA, xA, kB, xB, [st.decay_tok])
            k.wait("vector", mt)
            x0t = k.sig("vector", lambda e: e.tensor_tensor(
                out=tb("X0"), in0=pbT.ap[:, 0:64], in1=st.CONLT,
                op=ALU.mult))
            pbT.free = x0t
            mt = pair(pbB, xA, kA, xB, kB, [])
            k.wait("vector", mt)
            b0t = k.sig("vector", lambda e: e.tensor_tensor(
                out=tb("B0"), in0=pbB.ap[:, 0:64], in1=st.CONUT,
                op=ALU.mult))
            pbB.free = b0t
            mt = pair(pbA, xA, qA, xB, qB, [])
            k.wait("vector", mt)
            att = k.sig("vector", lambda e: e.tensor_tensor(
                out=tb("AT"), in0=pbA.ap[:, 0:64], in1=st.CONUI,
                op=ALU.mult))
            pbA.free = att
            k.wait("vector", b0t)
            ntok = k.sig("vector", lambda e: e.tensor_tensor(
                out=tb("N"), in0=st.CONIS, in1=tb("B0"), op=ALU.subtract))
            xtoks = {"X0": x0t, "B0": b0t}
            prevs = ("X0", "B0")
            for xn, bn in (("X2", "B2"), ("X4", "B4"), ("X8", "B8"),
                           ("X16", "B16"), ("X32", None)):
                px, pbn = prevs
                mt = pair(pbP, tb(pbn)[0:64, :], tb(px)[0:64, :],
                          tb(pbn)[64:128, :], tb(px)[64:128, :],
                          [xtoks[px], xtoks[pbn]])
                k.wait("scalar", mt)
                xt = k.sig("scalar", lambda e, xn=xn: e.copy(
                    tb(xn), pbP.ap[:, 0:64]))
                pbP.free = xt
                xtoks[xn] = xt
                if bn is not None:
                    mt2 = pair(pbR, tb(px)[0:64, :], tb(pbn)[0:64, :],
                               tb(px)[64:128, :], tb(pbn)[64:128, :],
                               [xtoks[px], xtoks[pbn]])
                    k.wait("scalar", mt2)
                    bt2 = k.sig("scalar", lambda e, bn=bn: e.copy(
                        tb(bn), pbR.ap[:, 0:64]))
                    pbR.free = bt2
                    xtoks[bn] = bt2
                    prevs = (xn, bn)
            ctok = ntok
            for xn in ("X2", "X4", "X8", "X16", "X32"):
                k.wait("tensor", pbU.free)
                k.waits("tensor", [ctok, xtoks[xn]])
                k.op("tensor", lambda e, xn=xn: e.matmul(
                    pbU.ap[0:64, 0:64], tb(xn)[0:64, :], tb("N")[0:64, :],
                    start=True, stop=False))
                k.op("tensor", lambda e: e.matmul(
                    pbU.ap[0:64, 0:64], st.CONIS[0:64, :], tb("N")[0:64, :],
                    start=False, stop=True))
                k.op("tensor", lambda e, xn=xn: e.matmul(
                    pbU.ap[64:128, 0:64], tb(xn)[64:128, :],
                    tb("N")[64:128, :], start=True, stop=False))
                mt = k.op("tensor", lambda e: e.matmul(
                    pbU.ap[64:128, 0:64], st.CONIS[64:128, :],
                    tb("N")[64:128, :], start=False, stop=True),
                    inc=("t", 1))
                k.wait("scalar", mt)
                ctok = k.sig("scalar", lambda e: e.copy(tb("N"),
                                                        pbU.ap[:, 0:64]))
                pbU.free = ctok
            ntok = ctok
            if c == 0 and pp == 0:
                dbg_dump(st, "dn_X0", tb("X0"), [x0t])
                dbg_dump(st, "dn_B0", tb("B0"), [b0t])
                dbg_dump(st, "dn_AT", tb("AT"), [att])
                dbg_dump(st, "dn_N", tb("N"), [ntok])
            k.wait("vector", k.tcnt())
            kpt = k.sig("vector", lambda e, pp=pp, c=c, cols=cols:
                        e.tensor_scalar(
                out=kpc, in0=qp(st, "ab", pp)[:, cols],
                scalar1=pc[:, pp * 32 + c: pp * 32 + c + 1],
                scalar2=None, op0=ALU.mult))
            k.wait("tensor", pf0.free)
            k.wait("tensor", kpt)
            k.op("tensor", lambda e: e.transpose(
                pf0.ap[0:64, 0:64], kpc[0:64, :], st.CONID[0:64, 0:64]))
            tt = k.op("tensor", lambda e: e.transpose(
                pf0.ap[64:128, 0:64], kpc[64:128, :],
                st.CONID[64:128, 64:128]), inc=("t", 1))
            k.wait("scalar", tt)
            ktm = k.sig("scalar", lambda e: e.copy(tb("KTM"),
                                                   pf0.ap[:, 0:64]))
            pf0.free = ktm
            k.wait("tensor", pf1.free)
            k.wait("tensor", st.v_tok)
            k.op("tensor", lambda e, pp=pp, cols=cols: e.transpose(
                pf1.ap[0:64, 0:64], qp(st, "v", pp)[0:64, cols],
                st.CONID[0:64, 0:64]))
            tt2 = k.op("tensor", lambda e, pp=pp, cols=cols: e.transpose(
                pf1.ap[64:128, 0:64], qp(st, "v", pp)[64:128, cols],
                st.CONID[64:128, 64:128]), inc=("t", 1))
            k.wait("scalar", tt2)
            vtm = k.sig("scalar", lambda e: e.copy(tb("VTM"),
                                                   pf1.ap[:, 0:64]))
            pf1.free = vtm

            if c == 0 and pp == 0:
                dbg_dump(st, "dn_KPC", kpc, [kpt])
                dbg_dump(st, "dn_KTM", tb("KTM"), [ktm])
                dbg_dump(st, "dn_VTM", tb("VTM"), [vtm])
            Sp = st.misc[:, 136 + pp * 64:200 + pp * 64]
            Sbp = Sbf[:, pp * 64:(pp + 1) * 64]
            mt = pair(pbR, kA, Sbp[0:64, :], kB, Sbp[64:128, :],
                      [s_tok[pp], x0t])
            k.wait("vector", mt)
            k.wait("vector", vtm)
            rhst = k.sig("vector", lambda e: e.tensor_tensor(
                out=tb("RHS"), in0=tb("VTM"), in1=pbR.ap[:, 0:64],
                op=ALU.subtract))
            pbR.free = rhst
            if c == 0 and pp == 0:
                dbg_dump(st, "dn_RHS", tb("RHS"), [rhst])
            mt = pair(pbU, tb("N")[0:64, :], tb("RHS")[0:64, :],
                      tb("N")[64:128, :], tb("RHS")[64:128, :],
                      [ntok, rhst])
            k.wait("scalar", mt)
            utok = k.sig("scalar", lambda e: e.copy(tb("U"),
                                                    pbU.ap[:, 0:64]))
            pbU.free = utok
            k.wait("tensor", pbT.free)
            k.waits("tensor", [att, utok, s_tok[pp]])
            k.op("tensor", lambda e, qA=qA, Sbp=Sbp: e.matmul(
                pbT.ap[0:64, 0:64], qA, Sbp[0:64, :], start=True, stop=False))
            k.op("tensor", lambda e: e.matmul(
                pbT.ap[0:64, 0:64], tb("AT")[0:64, :], tb("U")[0:64, :],
                start=False, stop=True))
            k.op("tensor", lambda e, qB=qB, Sbp=Sbp: e.matmul(
                pbT.ap[64:128, 0:64], qB, Sbp[64:128, :],
                start=True, stop=False))
            ot = k.op("tensor", lambda e: e.matmul(
                pbT.ap[64:128, 0:64], tb("AT")[64:128, :],
                tb("U")[64:128, :], start=False, stop=True), inc=("t", 1))
            mt = pair(pbB, tb("KTM")[0:64, :], tb("U")[0:64, :],
                      tb("KTM")[64:128, :], tb("U")[64:128, :],
                      [ktm, utok])
            k.wait("vector", mt)
            k.sig("vector", lambda e, Sp=Sp, pp=pp, c=c:
                  e.scalar_tensor_tensor(
                      out=Sp, in0=Sp,
                      scalar=pc[:, pp * 32 + c: pp * 32 + c + 1],
                      in1=pbB.ap[:, 0:64], op0=ALU.mult, op1=ALU.add))
            k.wait("vector", k.vcnt())
            sup = k.sig("vector", lambda e, Sp=Sp, Sbp=Sbp: e.tensor_copy(
                Sbp, Sp))
            pbB.free = sup
            s_tok[pp] = sup
            sso = st.misc[:, 392:400]
            k.wait("scalar", ot)
            k.wait("scalar", k.tcnt())
            k.sig("scalar", lambda e: e.activation(
                tb("YTM"), pbT.ap[:, 0:64], AF.Square,
                accum_out=sso[:, 0:1]))
            lnt = k.sig("scalar", lambda e: e.activation(
                sso[:, 2:3], sso[:, 0:1], AF.Ln, bias=st.EPS6,
                scale=1.0 / 64))
            k.wait("scalar", lnt)
            sq3 = k.sig("scalar", lambda e: e.activation(
                sso[:, 2:3], sso[:, 2:3], AF.Exp, scale=-0.5))
            k.wait("vector", sq3)
            rc2 = k.sig("vector", lambda e: e.reciprocal(
                sso[:, 4:5], sso[:, 2:3])) if False else None
            yv = k.sig("vector", lambda e: e.tensor_scalar(
                out=tb("YTM"), in0=pbT.ap[:, 0:64],
                scalar1=sso[:, 2:3], scalar2=None, op0=ALU.mult))
            pbT.free = yv
            k.wait("tensor", pf0.free)
            k.wait("tensor", yv)
            k.op("tensor", lambda e: e.transpose(
                pf0.ap[0:64, 0:64], tb("YTM")[0:64, :], st.CONID[0:64, 0:64]))
            ty = k.op("tensor", lambda e: e.transpose(
                pf0.ap[64:128, 0:64], tb("YTM")[64:128, :],
                st.CONID[64:128, 64:128]), inc=("t", 1))
            k.wait("scalar", ty)
            yev = k.sig("scalar", lambda e, pp=pp, cols=cols: e.copy(
                ar.b(OY + pp * 4096, 128, T)[:, cols], pf0.ap[:, 0:64]))
            pf0.free = yev
            if pp == 0 and c in (0, 1):
                dbg_dump(st, f"dn_U_{c}", tb("U"), [utok])
                dbg_dump(st, f"dn_Y_{c}", tb("YTM"), [yv])
                dbg_dump(st, f"dn_S_{c}", Sp, [s_tok[pp]])
    st.y_tok = k.scnt()


def emit_l0_tail(st):
    k = st.k
    ar = st.ar
    dr = st.dr
    OY = OQ
    k.waits("sync", [st.y_tok] + st.l0_spill_toks)
    ln_toks = []
    g_toks = []
    for c in range(KC):
        ln_toks.append(k.dma("sync", ar.b(O_B + c * 4096, 128, T),
                             st.spill_ln0[c * 128:(c + 1) * 128, :]))
        g_toks.append(k.dma("sync", ar.b(O_C + c * 4096, 128, T),
                            st.spill_gate[c * 128:(c + 1) * 128, :]))
    lnr = lambda c: ar.b(O_B + c * 4096, 128, T)
    gtr = lambda c: ar.b(O_C + c * 4096, 128, T)
    tmpf = ar.f(O_CS, 128, TS)
    y1f = ar.f(O_CS + 2048, 128, TS)

    def w_pre(col0, ncols, wap, tok):
        k.wait("vector", tok)
        vt = None
        for kc in range(4):
            vt = k.sig("vector", lambda e, kc=kc, wap=wap, ncols=ncols:
                       e.tensor_scalar(
                           out=wap[:, kc * ncols:(kc + 1) * ncols],
                           in0=wap[:, kc * ncols:(kc + 1) * ncols],
                           scalar1=sv(st, "L0_attn_head_norm_w", col=kc),
                           scalar2=None, op0=ALU.mult))
        return vt

    y1sp = []

    def evac(mco, s, bank, mmtok):
        ssl = slice(s * TS, (s + 1) * TS)
        k.wait("vector", mmtok)
        k.waits("vector", [ln_toks[mco], g_toks[mco]])
        if y1sp:
            k.wait("vector", y1sp[-1])
        k.sig("vector", lambda e, bank=bank, mco=mco, ssl=ssl: e.tensor_tensor(
            out=tmpf, in0=bank.ap[:, 0:TS], in1=gtr(mco)[:, ssl],
            op=ALU.mult))
        vt = k.sig("vector", lambda e, mco=mco, ssl=ssl: e.tensor_tensor(
            out=y1f, in0=tmpf, in1=lnr(mco)[:, ssl], op=ALU.add))
        bank.free = vt
        k.waits("vector", [st.r_tok[mco]] + st.r_readers[mco])
        st.r_readers[mco] = []
        st.r_tok[mco] = k.sig("vector", lambda e, mco=mco, ssl=ssl:
                              e.tensor_tensor(
            out=st.R[:, mco * T:(mco + 1) * T][:, ssl],
            in0=st.R[:, mco * T:(mco + 1) * T][:, ssl],
            in1=y1f, op=ALU.add))
        y1sp.append(k.dma("gpsimd",
                          st.spill_y1[mco * 128:(mco + 1) * 128, ssl],
                          y1f, waits=[st.r_tok[mco]]))

    emit_proj(st, dr["L0_attn_Wo"], lambda cc: ar.b(OY + cc * 4096, 128, T),
              4, D, evac, O_CS + 4096, 16384, in_toks=[st.y_tok],
              w_pre=w_pre, w_waits=[st.decay_tok, st.y_tok])
    y1r = []
    k.waits("sync", y1sp)
    for c in range(KC):
        y1r.append(k.dma("sync", ar.b(O_A + c * 4096, 128, T),
                         st.spill_y1[c * 128:(c + 1) * 128, :]))
    y1_at = lambda c: ar.b(O_A + c * 4096, 128, T)
    z_at = lambda c: ar.b(O_C + c * 4096, 128, T)
    z_toks = emit_norm(st, y1_at, z_at, "rms", 1e-6,
                       tr_off=O_CS + 4096, bc_off=O_B,
                       gain_at=lambda c: sv(st, "L0_attn_ff_norm_w", col=c),
                       src_toks=y1r)
    for c in range(KC):
        dbg_dump(st, f"z_{c}", z_at(c), [z_toks[c]])
    WF1 = O_CS + 4096
    WF3 = O_CS + 8192
    WF2 = O_CS + 12288
    trsl = ar.b(O_CS, 128, TS)
    pbf_f32 = [b.ap.bitcast(F32) for b in st.pbf]
    zlast = z_toks[-1]
    f1read = [zlast, zlast]
    f3read = [zlast, zlast]
    f2read = [zlast, zlast]
    last_hs_tt = [None]
    hs_at = lambda kcp: ar.b(O_A + kcp * 2048, 128, 1024)
    for half in range(2):
        tok0 = half * 1024
        htoks = []
        for kcp in range(32):
            slot = kcp % 2
            w1 = wload_bf(st, dr["L0_attn_Wff1"], kcp * 128, 128,
                          WF1 + slot * 2048, waits=[f1read[slot]])
            w3 = wload_bf(st, dr["L0_attn_Wff3"], kcp * 128, 128,
                          WF3 + slot * 2048, waits=[f3read[slot]])
            w1ap = ar.b(WF1 + slot * 2048, 128, 1024)
            w3ap = ar.b(WF3 + slot * 2048, 128, 1024)
            for sl in range(2):
                tsl = slice(tok0 + sl * TS, tok0 + (sl + 1) * TS)
                mms1 = []
                mms3 = []
                for kc in range(KC):
                    def mk1(kc=kc, tsl=tsl, w1ap=w1ap):
                        def f(e, s_, p_):
                            return e.matmul(st.pb[4].ap[:, 0:TS],
                                            w1ap[:, kc * 128:(kc + 1) * 128],
                                            z_at(kc)[:, tsl], start=s_, stop=p_)
                        return f
                    def mk3(kc=kc, tsl=tsl, w3ap=w3ap):
                        def f(e, s_, p_):
                            return e.matmul(st.pb[5].ap[:, 0:TS],
                                            w3ap[:, kc * 128:(kc + 1) * 128],
                                            z_at(kc)[:, tsl], start=s_, stop=p_)
                        return f
                    mms1.append(mk1())
                    mms3.append(mk3())
                m1 = mm_raw(st, st.pb[4], mms1, pre_waits=[w1] + z_toks)
                k.wait("scalar", m1)
                if last_hs_tt[0] is not None:
                    k.wait("scalar", last_hs_tt[0])
                e1 = k.sig("scalar", lambda e: e.activation(
                    trsl, st.pb[4].ap[:, 0:TS], AF.Silu))
                st.pb[4].free = e1
                m3 = mm_raw(st, st.pb[5], mms3, pre_waits=[w3] + z_toks)
                k.wait("vector", m3)
                k.wait("vector", e1)
                e3 = k.sig("vector", lambda e, kcp=kcp, sl=sl: e.tensor_tensor(
                    out=hs_at(kcp)[:, sl * TS:(sl + 1) * TS],
                    in0=trsl, in1=st.pb[5].ap[:, 0:TS], op=ALU.mult))
                st.pb[5].free = e3
                last_hs_tt[0] = e3
                htoks.append(e3)
            f1read[slot] = k.tcnt()
            f3read[slot] = k.tcnt()
        for mcog in range(2):
            accb = st.pb[0:6] + st.pbf[0:2]
            accs = [st.pb[i].ap[:, 0:TS] for i in range(6)] + \
                   [pbf_f32[0][:, 0:TS], pbf_f32[1][:, 0:TS]]
            for b in accb:
                k.wait("tensor", b.free)
            k.waits("tensor", htoks)
            for kcp in range(32):
                slot = kcp % 2
                w2ap = ar.b(WF2 + slot * 2048, 128, 1024)
                w2 = k.dma("gpsimd", w2ap,
                           dr["L0_attn_Wff2"][kcp * 128:(kcp + 1) * 128, :],
                           waits=[f2read[slot]])
                k.wait("tensor", w2)
                for mc4 in range(4):
                    mco = mcog * 4 + mc4
                    for sl in range(2):
                        idx = mc4 * 2 + sl
                        k.op("tensor",
                             lambda e, w2ap=w2ap, mco=mco, sl=sl, idx=idx,
                             kcp=kcp, accs=accs: e.matmul(
                                 accs[idx],
                                 w2ap[:, mco * 128:(mco + 1) * 128],
                                 hs_at(kcp)[:, sl * TS:(sl + 1) * TS],
                                 start=(kcp == 0), stop=(kcp == 31)),
                             inc=("t", 1))
                f2read[slot] = k.tcnt()
            dntok = k.tcnt()
            k.wait("vector", dntok)
            for mc4 in range(4):
                mco = mcog * 4 + mc4
                for sl in range(2):
                    idx = mc4 * 2 + sl
                    tsl = slice(tok0 + sl * TS, tok0 + (sl + 1) * TS)
                    k.waits("vector", [st.r_tok[mco]] + st.r_readers[mco])
                    st.r_readers[mco] = []
                    st.r_tok[mco] = k.sig(
                        "vector", lambda e, mco=mco, tsl=tsl, idx=idx,
                        accs=accs: e.tensor_tensor(
                            out=st.R[:, mco * T:(mco + 1) * T][:, tsl],
                            in0=st.R[:, mco * T:(mco + 1) * T][:, tsl],
                            in1=accs[idx], op=ALU.add))
                    accb[idx].free = st.r_tok[mco]
        if half == 0:
            k.wait("vector", k.tcnt())
    st.l0ffn_done = k.vcnt()


def emit_l0_ffn(st):
    k = st.k
    ar = st.ar
    dr = st.dr
    NI = 22
    ln1_at = lambda c: ar.b(O_A + c * 4096, 128, T)
    ln_toks = emit_norm(st, lambda c: st.R[:, c * T:(c + 1) * T], ln1_at,
                        "ln", 1e-5, tr_off=O_CS + 4096, bc_off=O_B,
                        src_toks=[t for t in st.r_tok if t])
    for c in range(KC):
        st.r_readers[c].append(ln_toks[c])
    mx = st.misc[:, 400:404]
    srow = st.misc[0:1, 404:412]
    sbc = st.misc[:, 412:416]
    wmx = st.misc[:, 416:420]
    H2B = [O_B, O_B + 4096]
    WST = O_B + 8192
    WQB = O_B + 12288
    WQT = [O_B + 14336, O_B + 16384]
    GSIL = O_B + 18432
    VQF = O_B + 22528
    vqf = ar.f(VQF, 128, T)
    gsil = ar.b(GSIL, 128, T)

    def absmax_acc(src_ap, first, pre=()):
        k.waits("vector", list(pre))
        if first:
            t = k.sig("vector", lambda e: e.tensor_reduce(
                out=mx[:, 0:1], in_=src_ap, axis=AX.X, op=ALU.max,
                apply_absolute_value=True))
            k.wait("vector", t)
            return t
        t = k.sig("vector", lambda e: e.tensor_reduce(
            out=mx[:, 1:2], in_=src_ap, axis=AX.X, op=ALU.max,
            apply_absolute_value=True))
        k.wait("vector", t)
        t = k.sig("vector", lambda e: e.tensor_tensor(
            out=mx[:, 0:1], in0=mx[:, 0:1], in1=mx[:, 1:2], op=ALU.max))
        k.wait("vector", t)
        return t

    def absmax_finish(t):
        d1 = k.dma("gpsimd", st.cc_in.ap()[0:1, :], mx[:, 0:1], waits=[t])
        d2 = k.dma("gpsimd", st.ccrow, st.cc_in.ap()[0:1, :], waits=[d1])
        k.wait("vector", d2)
        gv = k.sig("vector", lambda e: e.tensor_reduce(
            out=srow[:, 0:1], in_=st.ccrow, axis=AX.X, op=ALU.max))
        d3 = k.dma("gpsimd", st.cc_in.ap()[0:1, 0:1], srow[:, 0:1],
                   waits=[gv])
        ranks = CC_RANKS if CC_RANKS is not None else list(range(NCORES))
        if len(ranks) == 1:
            return k.dma("gpsimd", srow[:, 0:1], st.cc_in.ap()[0:1, 0:1],
                         waits=[d3])
        k.wait("gpsimd", d3)
        cct = k.op("gpsimd", lambda e, ranks=ranks: e.collective_compute(
            "AllReduce", ALU.max,
            replica_groups=[ranks],
            ins=[st.cc_in.ap().opt()],
            outs=[st.cc_out.ap().opt()],
        ), inc=("cc", 1))
        k.wait("gpsimd", cct)
        return k.dma("gpsimd", srow[:, 0:1], st.cc_out.ap()[0:1, 0:1])

    def mk_scales(rt, clip_name):
        cv = k.sig("vector", lambda e: e.tensor_copy(
            srow[:, 1:2], sv(st, clip_name, rows=1)))
        d1 = k.dma("gpsimd", st.bscr[0:1, :], srow[:, 0:2], waits=[rt, cv])
        d2 = k.dma("gpsimd", sbc[:, 0:2],
                   st.bscr[0:1, :].partition_broadcast(128), waits=[d1])
        k.wait("scalar", d2)
        sg = k.sig("scalar", lambda e: e.activation(
            sbc[:, 1:2], sbc[:, 1:2], AF.Sigmoid))
        k.wait("vector", sg)
        m1 = k.sig("vector", lambda e: e.tensor_tensor(
            out=sbc[:, 2:3], in0=sbc[:, 0:1], in1=sbc[:, 1:2], op=ALU.mult))
        k.wait("vector", m1)
        vt = k.sig("vector", lambda e: e.tensor_scalar(
            out=sbc[:, 0:1], in0=sbc[:, 2:3], scalar1=1.0 / 127.0,
            scalar2=None, op0=ALU.mult))
        k.wait("vector", vt)
        bt = k.sig("vector", lambda e: e.reciprocal(sbc[:, 1:2],
                                                    sbc[:, 0:1]))
        k.wait("vector", bt)
        return bt

    RM = 12582912.0

    def quant_magic(src_ap, dst_ap, s_ap, rs_ap, vq_ap, pre=()):
        k.waits("vector", list(pre))
        t = k.sig("vector", lambda e: e.tensor_scalar(
            out=vq_ap, in0=src_ap, scalar1=rs_ap, scalar2=None,
            op0=ALU.mult))
        k.wait("vector", t)
        t = k.sig("vector", lambda e: e.tensor_scalar(
            out=vq_ap, in0=vq_ap, scalar1=RM, scalar2=None, op0=ALU.add))
        k.wait("vector", t)
        t = k.sig("vector", lambda e: e.tensor_scalar(
            out=vq_ap, in0=vq_ap, scalar1=RM, scalar2=-128.0,
            op0=ALU.subtract, op1=ALU.max))
        k.wait("vector", t)
        t = k.sig("vector", lambda e: e.tensor_scalar(
            out=vq_ap, in0=vq_ap, scalar1=127.0, scalar2=None, op0=ALU.min))
        k.wait("vector", t)
        t = k.sig("vector", lambda e: e.tensor_scalar(
            out=dst_ap, in0=vq_ap, scalar1=s_ap, scalar2=None,
            op0=ALU.mult))
        k.wait("vector", t)
        return t

    dbg_dump(st, "ln1_0", ln1_at(0), [ln_toks[0]])
    am = None
    for c in range(KC):
        am = absmax_acc(ln1_at(c), c == 0, pre=[ln_toks[c]])
    rt = absmax_finish(am)
    bt = mk_scales(rt, "L0_ffn_clip_x")
    dbg_dump(st, "scales1", sbc[:, 0:3], [bt])
    xq_toks = []
    for c in range(KC):
        xq_toks.append(quant_magic(ln1_at(c), ln1_at(c), sbc[:, 0:1],
                                   sbc[:, 1:2], vqf,
                                   pre=([bt] if c == 0 else ())))
    dbg_dump(st, "xq_0", ln1_at(0), [xq_toks[0]])

    wqt_free = [None, None]
    st.wst_free = ln_toks[-1]
    st.wqb_free = None

    def wquant_T(wname, i, slot):
        wl = k.dma("sync", ar.f(WST, 128, 1024),
                   dr[wname][i * 128:(i + 1) * 128, :],
                   waits=([st.wst_free] if st.wst_free else []))
        k.wait("vector", wl)
        rm = k.sig("vector", lambda e: e.tensor_reduce(
            out=wmx[:, 0:1], in_=ar.f(WST, 128, 1024), axis=AX.X,
            op=ALU.max, apply_absolute_value=True))
        k.wait("vector", rm)
        rm2 = k.sig("vector", lambda e: e.tensor_scalar(
            out=wmx[:, 1:2], in0=wmx[:, 0:1], scalar1=1.0 / 127.0,
            scalar2=None, op0=ALU.mult))
        k.wait("vector", rm2)
        rm3 = k.sig("vector", lambda e: e.reciprocal(wmx[:, 2:3],
                                                     wmx[:, 1:2]))
        k.wait("vector", rm3)
        if st.wqb_free is not None:
            k.wait("vector", st.wqb_free)
        qt = quant_magic(ar.f(WST, 128, 1024), ar.b(WQB, 128, 1024),
                         wmx[:, 1:2], wmx[:, 2:3],
                         ar.f(WST, 128, 1024))
        st.wst_free = qt
        pfb = st.pbf[0]
        k.wait("tensor", pfb.free)
        k.waits("tensor", [qt] + ([wqt_free[slot]] if wqt_free[slot] else []))
        for kc in range(KC):
            k.op("tensor", lambda e, kc=kc: e.transpose(
                pfb.ap[:, kc * 128:(kc + 1) * 128],
                ar.b(WQB, 128, 1024)[:, kc * 128:(kc + 1) * 128],
                st.CONID), inc=("t", 1))
        tt = k.tcnt()
        st.wqb_free = tt
        k.wait("scalar", tt)
        ev = k.sig("scalar", lambda e, slot=slot: e.copy(
            ar.b(WQT[slot], 128, 1024), pfb.ap[:, 0:1024]))
        pfb.free = ev
        return ev

    h2_free = [[], []]
    sp_toks = []
    am2 = None
    for i in range(NI):
        evg = wquant_T("L0_ffn_w_gate", i, 0)
        evu = wquant_T("L0_ffn_w_up", i, 1)
        p = i % 2
        h2b = ar.b(H2B[p], 128, T)
        e3s = []
        pg, pu = st.pb[4], st.pb[5]
        for s in range(NSL):
            mmsg = []
            mmsu = []
            for kc in range(KC):
                def mkg(kc=kc, s=s):
                    def f(e, s_, p_):
                        return e.matmul(
                            pg.ap[:, 0:TS],
                            ar.b(WQT[0], 128, 1024)[:, kc * 128:(kc + 1) * 128],
                            ln1_at(kc)[:, s * TS:(s + 1) * TS],
                            start=s_, stop=p_)
                    return f
                def mku(kc=kc, s=s):
                    def f(e, s_, p_):
                        return e.matmul(
                            pu.ap[:, 0:TS],
                            ar.b(WQT[1], 128, 1024)[:, kc * 128:(kc + 1) * 128],
                            ln1_at(kc)[:, s * TS:(s + 1) * TS],
                            start=s_, stop=p_)
                    return f
                mmsg.append(mkg())
                mmsu.append(mku())
            mg = mm_raw(st, pg, mmsg, pre_waits=[evg] + xq_toks)
            k.wait("scalar", mg)
            k.wait("scalar", k.vcnt())
            e1 = k.sig("scalar", lambda e, s=s: e.activation(
                gsil[:, s * TS:(s + 1) * TS], pg.ap[:, 0:TS], AF.Silu))
            pg.free = e1
            mu = mm_raw(st, pu, mmsu, pre_waits=[evu] + xq_toks)
            k.wait("vector", mu)
            k.wait("vector", e1)
            k.waits("vector", h2_free[p])
            h2_free[p] = []
            e3 = k.sig("vector", lambda e, p=p, s=s: e.tensor_tensor(
                out=ar.b(H2B[p], 128, T)[:, s * TS:(s + 1) * TS],
                in0=gsil[:, s * TS:(s + 1) * TS],
                in1=pu.ap[:, 0:TS], op=ALU.mult))
            pu.free = e3
            e3s.append(e3)
            sp = k.dma("gpsimd",
                       st.spill_h2[i * 128:(i + 1) * 128,
                                   s * TS:(s + 1) * TS],
                       ar.b(H2B[p], 128, T)[:, s * TS:(s + 1) * TS],
                       waits=[e3])
            sp_toks.append(sp)
            h2_free[p].append(sp)
        am2 = absmax_acc(h2b, i == 0, pre=e3s)
        h2_free[p].append(am2)
        wqt_free[0] = k.tcnt()
        wqt_free[1] = k.tcnt()

    for i in (0, 5, 10, 16, 21):
        dbg_dump(st, f"h2_{i}", st.spill_h2[i * 128:(i + 1) * 128, :],
                 sp_toks)
    rt2 = absmax_finish(am2)
    bt2 = mk_scales(rt2, "L0_ffn_clip_down")
    dbg_dump(st, "scales2", sbc[:, 0:3], [bt2])

    WDS = O_A
    WDQ = O_A + 11264
    WDT = lambda m, ii: ar.b(O_B + (m * NI + ii) * 256, 128, 128)
    WDTg = lambda m, base, n: ar.b(O_B + (m * NI + base) * 256, 128, n * 128)
    gateup_tt = k.tcnt()
    wd_free = None
    wdt_toks = []
    for m in range(KC):
        wl = k.dma("sync", ar.f(WDS, 128, 2816),
                   dr["L0_ffn_w_down"][m * 128:(m + 1) * 128, :],
                   waits=([wd_free] if wd_free else [gateup_tt] + sp_toks))
        k.wait("vector", wl)
        rm = k.sig("vector", lambda e: e.tensor_reduce(
            out=wmx[:, 0:1], in_=ar.f(WDS, 128, 2816), axis=AX.X,
            op=ALU.max, apply_absolute_value=True))
        k.wait("vector", rm)
        rm2 = k.sig("vector", lambda e: e.tensor_scalar(
            out=wmx[:, 1:2], in0=wmx[:, 0:1], scalar1=1.0 / 127.0,
            scalar2=None, op0=ALU.mult))
        k.wait("vector", rm2)
        rm3 = k.sig("vector", lambda e: e.reciprocal(wmx[:, 2:3],
                                                     wmx[:, 1:2]))
        k.wait("vector", rm3)
        if m > 0:
            k.wait("vector", wdt_prev_tt)
        q = quant_magic(ar.f(WDS, 128, 2816), ar.b(WDQ, 128, 2816),
                        wmx[:, 1:2], wmx[:, 2:3], ar.f(WDS, 128, 2816))
        wd_free = q
        pfb = st.pbf[0]
        k.wait("tensor", pfb.free)
        k.wait("tensor", q)
        base = 0
        for ii in range(NI):
            grp = ii % 8
            if grp == 0 and ii > 0:
                tt = k.tcnt()
                k.wait("scalar", tt)
                ev = k.sig("scalar", lambda e, m=m, base=base: e.copy(
                    WDTg(m, base, 8), pfb.ap[:, 0:1024]))
                wdt_toks.append(ev)
                k.wait("tensor", ev)
                base = ii
            k.op("tensor", lambda e, m=m, ii=ii, grp=grp: e.transpose(
                pfb.ap[:, grp * 128:(grp + 1) * 128],
                ar.b(WDQ, 128, 2816)[:, ii * 128:(ii + 1) * 128],
                st.CONID), inc=("t", 1))
        wdt_prev_tt = k.tcnt()
        k.wait("scalar", wdt_prev_tt)
        nrem = NI % 8
        ev = k.sig("scalar", lambda e, m=m, base=base, nrem=nrem: e.copy(
            WDTg(m, base, nrem), pfb.ap[:, 0:nrem * 128]))
        wdt_toks.append(ev)
        pfb.free = ev

    HSB = [O_CS, O_CS + 1024]
    HQF = [O_CS + 2048, O_CS + 4096]
    HQB = [O_CS + 6144, O_CS + 7168]
    pbf_f32 = [b.ap.bitcast(F32) for b in st.pbf]
    accb = list(st.pb[0:6]) + list(st.pbf[0:2])
    accs = [st.pb[j].ap[:, 0:TS] for j in range(6)] +            [pbf_f32[0][:, 0:TS], pbf_f32[1][:, 0:TS]]
    k.waits("tensor", wdt_toks)
    dbg_dump(st, "wdt_m0", ar.b(O_B, 128, NI * 128), wdt_toks)
    dbg_dump(st, "wdt_m4", ar.b(O_B + 4 * NI * 256, 128, NI * 128), wdt_toks)
    hq_free = [None, None]
    for s in range(NSL):
        for b in accb:
            k.wait("tensor", b.free)
        stop_tok = None
        for i in range(NI):
            p = i % 2
            ld = k.dma("gpsimd", ar.b(HSB[p], 128, TS),
                       st.spill_h2[i * 128:(i + 1) * 128,
                                   s * TS:(s + 1) * TS],
                       waits=([hq_free[p]] if hq_free[p] else sp_toks))
            qd = quant_magic(ar.b(HSB[p], 128, TS), ar.b(HQB[p], 128, TS),
                             sbc[:, 0:1], sbc[:, 1:2],
                             ar.f(HQF[p], 128, TS),
                             pre=[ld, bt2] +
                                 ([hq_free[p]] if hq_free[p] else []))
            if s == 0 and i == 0:
                dbg_dump(st, "hsb00", ar.b(HSB[p], 128, TS), [ld])
                dbg_dump(st, "hqb00", ar.b(HQB[p], 128, TS), [qd])
            k.wait("tensor", qd)
            for m in range(KC):
                tok = k.op("tensor", lambda e, m=m, i=i, p=p: e.matmul(
                    accs[m], WDT(m, i), ar.b(HQB[p], 128, TS),
                    start=(i == 0), stop=(i == NI - 1)),
                    inc=("t", 1))
                if i == NI - 1:
                    stop_tok = tok
            hq_free[p] = k.tcnt()
        k.wait("vector", stop_tok)
        if s == 0:
            dbg_dump(st, "rpre0", st.R[:, 0:TS], [st.r_tok[0]])
        for m in range(KC):
            k.waits("vector", [st.r_tok[m]] + st.r_readers[m])
            st.r_readers[m] = []
            tsl = slice(s * TS, (s + 1) * TS)
            st.r_tok[m] = k.sig("vector", lambda e, m=m, tsl=tsl, idx=m:
                                e.tensor_tensor(
                out=st.R[:, m * T:(m + 1) * T][:, tsl],
                in0=st.R[:, m * T:(m + 1) * T][:, tsl],
                in1=accs[idx], op=ALU.add))
            accb[m].free = st.r_tok[m]
        if s == 0:
            dbg_dump(st, "rpost0", st.R[:, 0:TS], [st.r_tok[0]])
    st.l0ffn_out = k.vcnt()



def emit_l1_attn(st):
    k = st.k
    ar = st.ar
    dr = st.dr
    ln2_at = lambda c: ar.b(O_A + c * 4096, 128, T)
    ln_toks = emit_norm(st, lambda c: st.R[:, c * T:(c + 1) * T], ln2_at,
                        "ln", 1e-5, tr_off=O_CS + 4096, bc_off=O_B,
                        src_toks=[t for t in st.r_tok if t])
    for c in range(KC):
        st.r_readers[c].append(ln_toks[c])
    dbg_dump(st, "ln2_0", ln2_at(0), [ln_toks[0]])
    STG = [O_C + 16384, O_C + 17408]
    if not hasattr(st, "stg_free"):
        st.stg_free = [None, None]
    sp_toks = []

    def evac_qkv(mco, s, bank, mmtok):
        slot = (mco * NSL + s) % 2
        stg = ar.b(STG[slot], 128, TS)
        k.wait("vector", mmtok)
        if st.stg_free[slot]:
            k.wait("vector", st.stg_free[slot])
        vt = k.sig("vector", lambda e, stg=stg, bank=bank, mco=mco:
                   e.tensor_scalar(
                       out=stg, in0=bank.ap[:, 0:TS],
                       scalar1=sv(st, "L1_attn_b_qkv", col=mco),
                       scalar2=None, op0=ALU.add))
        bank.free = vt
        d_ = k.dma("sync", st.spill_qkv[mco * 128:(mco + 1) * 128,
                                        s * TS:(s + 1) * TS], stg, waits=[vt])
        sp_toks.append(d_)
        st.stg_free[slot] = d_
    emit_proj(st, dr["L1_attn_w_qkv"], ln2_at, KC, 3 * D, evac_qkv,
              O_C, 16384, in_toks=list(ln_toks),
              w_waits=[t for t in st.r_tok if t])
    dbg_dump(st, "qkvs0", st.spill_qkv[0:128, :], sp_toks)
    OCTX = O_A
    OPT = O_C
    OQT = O_C + 16384
    OKT = O_C + 20480
    OVT = O_C + 24576
    OVN = O_C + 28672
    RDq = O_CS
    RDB = O_CS + 2048
    ctx_at = lambda h: ar.b_at(OCTX + h * 4096, 0, 64, T)
    ctx_toks = []
    qkv_free = None
    vn_read = None
    rd_read = None
    for h in range(16):
        lw = ([qkv_free] if qkv_free else sp_toks)
        tq = k.dma("gpsimd", ar.b_at(OQT, 0, 64, T),
                   st.spill_qkv[h * 64:(h + 1) * 64, :], waits=lw)
        tk = k.dma("gpsimd", ar.b_at(OKT, 0, 64, T),
                   st.spill_qkv[1024 + h * 64:1024 + (h + 1) * 64, :],
                   waits=lw)
        tv = k.dma("gpsimd", ar.b_at(OVT, 0, 64, T),
                   st.spill_qkv[2048 + h * 64:2048 + (h + 1) * 64, :],
                   waits=lw)
        if vn_read:
            k.wait("vector", vn_read)
        mz = k.sig("vector", lambda e: e.memset(ar.b(OVN, 128, 16 * 65), 1.0))
        pfb = st.pbf[0]
        ev_toks = []
        for rnd in range(2):
            k.wait("tensor", pfb.free)
            k.wait("tensor", tv)
            if qkv_free:
                k.wait("tensor", qkv_free)
            for j in range(8):
                kt = rnd * 8 + j
                k.op("tensor", lambda e, kt=kt, j=j: e.transpose(
                    pfb.ap[:, j * 64:(j + 1) * 64],
                    ar.b_at(OVT, 0, 64, T)[:, kt * 128:(kt + 1) * 128],
                    st.CONID[0:64, 0:64]), inc=("t", 1))
            tt = k.tcnt()
            k.wait("scalar", tt)
            k.wait("scalar", mz)
            for j in range(8):
                kt = rnd * 8 + j
                et = k.sig("scalar", lambda e, kt=kt, j=j: e.copy(
                    ar.b(OVN + kt * 130, 128, 64),
                    pfb.ap[:, j * 64:(j + 1) * 64]))
                ev_toks.append(et)
            pfb.free = ev_toks[-1]
        pt_read = None
        for qb in range(4):
            pv = st.pb[2]
            k.wait("tensor", pv.free)
            k.waits("tensor", [ev_toks[-1], mz])
            exp_toks = []
            sbanks = [st.pb[0], st.pb[1], st.pb[5]]

            def emit_s(kt):
                bank = sbanks[kt % 3]
                k.wait("tensor", bank.free)
                k.waits("tensor", [tq, tk])
                mt = k.op("tensor", lambda e, kt=kt, qb=qb, bank=bank:
                          e.matmul(
                    bank.ap[:, 0:TS],
                    ar.b_at(OKT, 0, 64, T)[:, kt * 128:(kt + 1) * 128],
                    ar.b_at(OQT, 0, 64, T)[:, qb * TS:(qb + 1) * TS],
                    start=True, stop=True), inc=("t", 1))
                k.wait("scalar", mt)
                if pt_read:
                    k.wait("scalar", pt_read)
                et = k.sig("scalar", lambda e, kt=kt, bank=bank: e.activation(
                    ar.b(OPT + kt * 1024, 128, TS), bank.ap[:, 0:TS],
                    AF.Exp, scale=0.125))
                bank.free = et
                exp_toks.append(et)

            def emit_pv(kt):
                k.wait("tensor", exp_toks[kt])
                k.op("tensor", lambda e, kt=kt, pv=pv: e.matmul(
                    pv.ap[0:65, 0:TS],
                    ar.b(OVN + kt * 130, 128, 65),
                    ar.b(OPT + kt * 1024, 128, TS),
                    start=(kt == 0), stop=(kt == 15)), inc=("t", 1))
            emit_s(0)
            emit_s(1)
            for kt in range(16):
                if kt + 2 < 16:
                    emit_s(kt + 2)
                emit_pv(kt)
            pvt = k.tcnt()
            pt_read = pvt
            k.wait("vector", pvt)
            if rd_read:
                k.wait("vector", rd_read)
            rd_f = ar.f32[64:65, RDq // 4:RDq // 4 + TS]
            rd_b = ar.bf[64:65, RDB // 2:RDB // 2 + TS]
            r1 = k.sig("vector", lambda e, pv=pv: e.reciprocal(
                rd_f, pv.ap[64:65, 0:TS]))
            k.wait("vector", r1)
            r2 = k.sig("vector", lambda e: e.tensor_copy(rd_b, rd_f))
            bcb = st.pb[3]
            k.wait("tensor", bcb.free)
            k.wait("tensor", r2)
            bt_ = k.op("tensor", lambda e, bcb=bcb: e.matmul(
                bcb.ap[:, 0:TS], st.CONER64, rd_b, start=True, stop=True),
                inc=("t", 1))
            rd_read = bt_
            bcba = ar.b(O_CS + 8192, 128, TS)
            k.wait("scalar", bt_)
            if ctx_toks:
                k.wait("scalar", ctx_toks[-1])
            bev = k.sig("scalar", lambda e, bcb=bcb: e.copy(
                bcba, bcb.ap[:, 0:TS]))
            bcb.free = bev
            k.wait("vector", bev)
            ct = k.sig("vector", lambda e, h=h, qb=qb, pv=pv:
                       e.tensor_tensor(
                out=ctx_at(h)[:, qb * TS:(qb + 1) * TS],
                in0=pv.ap[0:64, 0:TS], in1=bcba[0:64, 0:TS],
                op=ALU.mult))
            pv.free = ct
            ctx_toks.append(ct)
        qkv_free = k.tcnt()
        vn_read = ("t", k.cnt["t"])
    dbg_dump(st, "ctxh0", ctx_at(0), [ctx_toks[3]])
    OWF = O_C
    wf_toks = []
    for hh in range(16):
        wf_toks.append(k.dma("gpsimd",
                             ar.b_at(OWF + hh * 2048, 0, 64, 1024),
                             dr["L1_attn_w_fc"][hh * 64:(hh + 1) * 64, :],
                             waits=[qkv_free]))
    for mco in range(KC):
        for s in range(NSL):
            bank = st.pb[4 + (mco * NSL + s) % 2]
            k.wait("tensor", bank.free)
            k.waits("tensor", wf_toks + [ctx_toks[-1]])
            tok = None
            for hh in range(16):
                tok = k.op("tensor", lambda e, hh=hh, mco=mco, s=s, bank=bank:
                           e.matmul(
                    bank.ap[:, 0:TS],
                    ar.b_at(OWF + hh * 2048, 0, 64, 1024)[
                        :, mco * 128:(mco + 1) * 128],
                    ctx_at(hh)[:, s * TS:(s + 1) * TS],
                    start=(hh == 0), stop=(hh == 15)),
                    inc=(("t", 1) if hh == 15 else None))
            k.wait("vector", tok)
            k.waits("vector", [st.r_tok[mco]] + st.r_readers[mco])
            st.r_readers[mco] = []
            tsl = slice(s * TS, (s + 1) * TS)
            st.r_tok[mco] = k.sig("vector", lambda e, mco=mco, tsl=tsl,
                                  bank=bank: e.scalar_tensor_tensor(
                out=st.R[:, mco * T:(mco + 1) * T][:, tsl],
                in0=bank.ap[:, 0:TS],
                scalar=sv(st, "L1_attn_b_fc", col=mco),
                in1=st.R[:, mco * T:(mco + 1) * T][:, tsl],
                op0=ALU.add, op1=ALU.add))
            bank.free = st.r_tok[mco]
    st.l1attn_done = k.vcnt()
    dbg_dump(st, "r3_0", st.R[:, 0:T], [st.r_tok[0]])


def emit_l1_ffn(st):
    k = st.k
    ar = st.ar
    dr = st.dr
    SP_EQ, SP_EK, SP_EV = st.spill_e[0], st.spill_e[1], st.spill_e[2]
    SP_QT, SP_KT, SP_VT = st.spill_ln0, st.spill_gate, st.spill_y1
    SP_N = st.spill_node
    SP_LN3 = st.spill_ln3
    in_at = lambda c: ar.b(O_A + c * 4096, 128, T)
    STG = [O_C + 16384, O_C + 17408]
    if not hasattr(st, "stg_free"):
        st.stg_free = [None, None]
    ewb = st.misc[:, 420:438]
    actb = st.misc[:, 438:444]
    sA = lambda c: st.misc[:, 448 + c:449 + c]
    sB = lambda c: st.misc[:, 456 + c:457 + c]
    sC = lambda c: st.misc[:, 420 + 3 * c:421 + 3 * c]
    d1 = k.dma("gpsimd", ewb, dr["L1_ffn_ew_row"][0:1, :]
               .partition_broadcast(128))
    d2 = k.dma("gpsimd", actb, dr["L1_ffn_act_w_row"][0:1, :]
               .partition_broadcast(128))
    k.waits("vector", [d1, d2])
    for c in range(6):
        k.sig("vector", lambda e, c=c: e.tensor_scalar(
            out=sA(c), in0=ewb[:, 3 * c:3 * c + 1],
            scalar1=ewb[:, 3 * c + 1:3 * c + 2], op0=ALU.mult,
            scalar2=float(1.0 / np.sqrt(128.0)), op1=ALU.mult))
        k.sig("vector", lambda e, c=c: e.tensor_scalar(
            out=sB(c), in0=ewb[:, 3 * c + 2:3 * c + 3],
            scalar1=actb[:, c:c + 1], scalar2=None, op0=ALU.mult))
    scal_tok = k.vcnt()
    ln_toks = emit_norm(st, lambda c: st.R[:, c * T:(c + 1) * T], in_at,
                        "ln", 1e-5, tr_off=O_CS + 4096, bc_off=O_B,
                        src_toks=[t for t in st.r_tok if t])
    for c in range(KC):
        st.r_readers[c].append(ln_toks[c])
    ln3_sp = []
    for c in range(KC):
        ln3_sp.append(k.dma("sync", SP_LN3[c * 128:(c + 1) * 128, :],
                            in_at(c), waits=[ln_toks[c]]))
    dbg_dump(st, "ln3_0", in_at(0), [ln_toks[0]])
    st.in_toks = list(ln_toks)
    st.in_read = None
    st.comb_read = None
    node_sp = {}

    def get(i):
        if i == -1:
            return SP_LN3, ln3_sp
        return SP_N[i], node_sp[i]

    b_at = lambda c: ar.b(O_B + c * 4096, 128, T)
    st.b_toks = []
    st.b_read = None
    st.b_dma = []

    def load_input(src_sp, src_toks):
        toks = []
        for c in range(KC):
            toks.append(k.dma("gpsimd", in_at(c),
                              src_sp[c * 128:(c + 1) * 128, :],
                              waits=([st.in_read] if st.in_read else []) +
                              ([st.comb_read] if st.comb_read else []) +
                              list(src_toks)))
        st.in_toks = toks

    def act_proj(wname, src, dst, bias_e=None, dst_sp=None, w_waits=()):
        sp_toks2 = []
        b_toks2 = []
        src_at = in_at if src == "a" else b_at
        src_toks = st.in_toks if src == "a" else st.b_toks

        def evac(mco, s, bank, mmtok):
            k.wait("vector", mmtok)
            if dst == "b":
                if st.b_read:
                    k.wait("vector", st.b_read)
                k.waits("vector", st.b_dma)
                out_ap = b_at(mco)[:, s * TS:(s + 1) * TS]
                if bias_e is not None:
                    vt = k.sig("vector", lambda e, out_ap=out_ap, bank=bank,
                               mco=mco: e.tensor_scalar(
                        out=out_ap, in0=bank.ap[:, 0:TS],
                        scalar1=sv(st, "L1_ffn_be", col=bias_e * 8 + mco),
                        scalar2=None, op0=ALU.add))
                else:
                    vt = k.sig("vector", lambda e, out_ap=out_ap, bank=bank:
                               e.tensor_copy(out_ap, bank.ap[:, 0:TS]))
                bank.free = vt
                b_toks2.append(vt)
                if dst_sp is not None:
                    d_ = k.dma("sync", dst_sp[mco * 128:(mco + 1) * 128,
                                              s * TS:(s + 1) * TS],
                               out_ap, waits=[vt])
                    sp_toks2.append(d_)
                return
            slot = (mco * NSL + s) % 2
            stg = ar.b(STG[slot], 128, TS)
            if st.stg_free[slot]:
                k.wait("vector", st.stg_free[slot])
            if bias_e is not None:
                vt = k.sig("vector", lambda e, stg=stg, bank=bank, mco=mco:
                           e.tensor_scalar(
                    out=stg, in0=bank.ap[:, 0:TS],
                    scalar1=sv(st, "L1_ffn_be", col=bias_e * 8 + mco),
                    scalar2=None, op0=ALU.add))
            else:
                vt = k.sig("vector", lambda e, stg=stg, bank=bank:
                           e.tensor_copy(stg, bank.ap[:, 0:TS]))
            bank.free = vt
            d_ = k.dma("sync", dst_sp[mco * 128:(mco + 1) * 128,
                                      s * TS:(s + 1) * TS], stg, waits=[vt])
            sp_toks2.append(d_)
            st.stg_free[slot] = d_
        emit_proj(st, dr[wname], src_at, KC, D, evac, O_C, 16384,
                  in_toks=list(src_toks), w_waits=list(w_waits))
        if src == "a":
            st.in_read = k.tcnt()
        else:
            st.b_read = k.tcnt()
        if dst == "b":
            st.b_toks = b_toks2
            st.b_dma = sp_toks2
            return sp_toks2 if dst_sp is not None else b_toks2
        return sp_toks2

    OPT = O_B
    OQT = O_C + 16384
    OKT = O_C + 20480
    OVT = O_C + 24576
    OVN = O_C + 28672
    RDq = O_CS
    RDB = O_CS + 2048

    def node_attn(c, qt_sp, kt_sp, vt_sp):
        QKVB = [(OQT, OKT, OVT),
                (O_B + 16384, O_B + 20480, O_B + 24576)]
        qkv_read = [None, None]
        last_pv = None
        ctx_toks = []
        rd_read = None
        for h in range(8):
            hs = h % 2
            oqt, okt, ovt = QKVB[hs]
            lw = ([qkv_read[hs]] if qkv_read[hs] else
                  ([st.in_read] if st.in_read else []) +
                  ([st.comb_read] if st.comb_read else []) +
                  [t for t in st.stg_free if t])
            tq = k.dma("gpsimd", ar.b(oqt, 128, T),
                       SP_QT[h * 128:(h + 1) * 128, :], waits=lw + qt_sp)
            tk = k.dma("gpsimd", ar.b(okt, 128, T),
                       SP_KT[h * 128:(h + 1) * 128, :], waits=lw + kt_sp)
            tv = k.dma("gpsimd", ar.b(ovt, 128, T),
                       SP_VT[h * 128:(h + 1) * 128, :], waits=lw + vt_sp)
            pfb = st.pbf[0]
            ev_toks = []
            for rnd in range(2):
                k.wait("tensor", pfb.free)
                k.wait("tensor", tv)
                for j in range(8):
                    kt = rnd * 8 + j
                    k.op("tensor", lambda e, kt=kt, j=j, ovt=ovt: e.transpose(
                        pfb.ap[:, j * 128:(j + 1) * 128],
                        ar.b(ovt, 128, T)[:, kt * 128:(kt + 1) * 128],
                        st.CONID), inc=("t", 1))
                tt = k.tcnt()
                k.wait("scalar", tt)
                if last_pv:
                    k.wait("scalar", last_pv)
                et = k.sig("scalar", lambda e, rnd=rnd: e.copy(
                    ar.b(OVN + rnd * 2048, 128, 1024), pfb.ap[:, 0:1024]))
                ev_toks.append(et)
                pfb.free = et
            pt_read = None
            k.waits("scalar", st.b_dma)
            for qb in range(4):
                pv = st.pb[2]
                dnb = st.pb[3]
                k.wait("tensor", pv.free)
                k.wait("tensor", dnb.free)
                k.wait("tensor", ev_toks[-1])
                exp_toks = []
                sbanks = [st.pb[0], st.pb[1], st.pb[5]]

                def emit_s(kt):
                    bank = sbanks[kt % 3]
                    k.wait("tensor", bank.free)
                    k.waits("tensor", [tq, tk])
                    mt = k.op("tensor", lambda e, kt=kt, qb=qb, bank=bank,
                              okt=okt, oqt=oqt: e.matmul(
                        bank.ap[:, 0:TS],
                        ar.b(okt, 128, T)[:, kt * 128:(kt + 1) * 128],
                        ar.b(oqt, 128, T)[:, qb * TS:(qb + 1) * TS],
                        start=True, stop=True), inc=("t", 1))
                    k.wait("scalar", mt)
                    k.wait("scalar", scal_tok)
                    if pt_read:
                        k.wait("scalar", pt_read)
                    et = k.sig("scalar", lambda e, kt=kt, bank=bank, c=c:
                               e.activation(
                        ar.b(OPT + kt * 1024, 128, TS), bank.ap[:, 0:TS],
                        AF.Exp, scale=sA(c)))
                    bank.free = et
                    exp_toks.append(et)

                def emit_pv(kt):
                    k.wait("tensor", exp_toks[kt])
                    k.op("tensor", lambda e, kt=kt, pv=pv: e.matmul(
                        pv.ap[:, 0:TS],
                        ar.b(OVN + kt * 256, 128, 128),
                        ar.b(OPT + kt * 1024, 128, TS),
                        start=(kt == 0), stop=(kt == 15)), inc=("t", 1))
                    k.op("tensor", lambda e, kt=kt, dnb=dnb: e.matmul(
                        dnb.ap[0:1, 0:TS], st.CONEC,
                        ar.b(OPT + kt * 1024, 128, TS),
                        start=(kt == 0), stop=(kt == 15)), inc=("t", 1))
                emit_s(0)
                emit_s(1)
                for kt in range(16):
                    if kt + 2 < 16:
                        emit_s(kt + 2)
                    emit_pv(kt)
                pvt = k.tcnt()
                pt_read = pvt
                k.wait("vector", pvt)
                if rd_read:
                    k.wait("vector", rd_read)
                rd_f = ar.f32[0:1, RDq // 4:RDq // 4 + TS]
                rd_b = ar.bf[0:1, RDB // 2:RDB // 2 + TS]
                r1 = k.sig("vector", lambda e, dnb=dnb: e.reciprocal(
                    rd_f, dnb.ap[0:1, 0:TS]))
                k.wait("vector", r1)
                r2 = k.sig("vector", lambda e: e.tensor_copy(rd_b, rd_f))
                bcb = st.pb[4]
                k.wait("tensor", bcb.free)
                k.wait("tensor", r2)
                bt_ = k.op("tensor", lambda e, bcb=bcb: e.matmul(
                    bcb.ap[:, 0:TS], st.CONER, rd_b, start=True, stop=True),
                    inc=("t", 1))
                rd_read = bt_
                bcba = ar.b(O_CS + 8192, 128, TS)
                k.wait("scalar", bt_)
                if ctx_toks:
                    k.wait("scalar", ctx_toks[-1])
                bev = k.sig("scalar", lambda e, bcb=bcb: e.copy(
                    bcba, bcb.ap[:, 0:TS]))
                bcb.free = bev
                k.wait("vector", bev)
                k.wait("vector", st.in_read)
                ct = k.sig("vector", lambda e, h=h, qb=qb, pv=pv:
                           e.tensor_tensor(
                    out=in_at(h)[:, qb * TS:(qb + 1) * TS],
                    in0=pv.ap[:, 0:TS], in1=bcba[:, 0:TS], op=ALU.mult))
                pv.free = ct
                dnb.free = ct
                ctx_toks.append(ct)
            qkv_read[hs] = k.tcnt()
            last_pv = qkv_read[hs]
        st.in_toks = ctx_toks
        st.in_read = k.tcnt()
        return ctx_toks

    def wo_combine(c, eq_sp_toks):
        EQR = [O_C + 18432, O_C + 22528]
        eq_free = [None, None]
        eq_toks = {}
        tmpf = ar.f(O_CS + 4096, 128, TS)
        sp_toks2 = []

        def evac(mco, s, bank, mmtok):
            slot = mco % 2
            if s == 0:
                eq_toks[mco] = k.dma(
                    "gpsimd", ar.b(EQR[slot], 128, T),
                    SP_EQ[mco * 128:(mco + 1) * 128, :],
                    waits=([eq_free[slot]] if eq_free[slot] else
                           [st.in_read]) + eq_sp_toks)
            k.wait("vector", mmtok)
            k.wait("vector", eq_toks[mco])
            t1 = k.sig("vector", lambda e, bank=bank, c=c: e.tensor_scalar(
                out=tmpf, in0=bank.ap[:, 0:TS], scalar1=sB(c),
                scalar2=None, op0=ALU.mult))
            bank.free = t1
            k.wait("vector", t1)
            if c == 5:
                dst = ar.b(O_B + mco * 4096, 128, T)[:, s * TS:(s + 1) * TS]
                t2 = k.sig("vector", lambda e, dst=dst, slot=slot, s=s, c=c:
                           e.scalar_tensor_tensor(
                    out=dst, in0=ar.b(EQR[slot], 128, T)[:, s * TS:(s + 1) * TS],
                    scalar=sC(c), in1=tmpf, op0=ALU.mult, op1=ALU.add))
                sp_toks2.append(t2)
            else:
                slot2 = (mco * NSL + s) % 2
                stg = ar.b(STG[slot2], 128, TS)
                if st.stg_free[slot2]:
                    k.wait("vector", st.stg_free[slot2])
                t2 = k.sig("vector", lambda e, stg=stg, slot=slot, s=s, c=c:
                           e.scalar_tensor_tensor(
                    out=stg, in0=ar.b(EQR[slot], 128, T)[:, s * TS:(s + 1) * TS],
                    scalar=sC(c), in1=tmpf, op0=ALU.mult, op1=ALU.add))
                d_ = k.dma("sync", SP_N[c][mco * 128:(mco + 1) * 128,
                                           s * TS:(s + 1) * TS], stg,
                           waits=[t2])
                sp_toks2.append(d_)
                st.stg_free[slot2] = d_
            if s == NSL - 1:
                eq_free[slot] = t2
        emit_proj(st, dr[f"L1_ffn_Wo{c}"], in_at, KC, D, evac, O_C,
                  16384, in_toks=list(st.in_toks))
        st.in_read = k.tcnt()
        st.comb_read = k.vcnt()
        return sp_toks2

    EDGE = {0: dict(eq=(-1, 0), ek=None, ev=None),
            1: dict(eq=(0, 2), ek=(-1, 1), ev=None),
            2: dict(eq=(1, 5), ek=(-1, 3), ev=(0, 4)),
            3: dict(eq=(2, 9), ek=(-1, 6), ev=(1, 8)),
            4: dict(eq=(3, 14), ek=(-1, 10), ev=(1, 12)),
            5: dict(eq=(4, 19), ek=(0, 15), ev=(2, 17))}
    st.in_read = k.tcnt()
    l1f_boundary = k.tcnt()
    for c in range(6):
        E = EDGE[c]
        if c > 0:
            load_input(*get(E["eq"][0]))
        eq_sp = act_proj(f"L1_ffn_We{E['eq'][1]}", "a", "b",
                         bias_e=E["eq"][1], dst_sp=SP_EQ,
                         w_waits=([l1f_boundary] if c == 0 else ()))
        qt_sp = act_proj(f"L1_ffn_Wq{c}", "b", "spill", dst_sp=SP_QT)
        vt_sp = None
        if E["ev"] is None:
            vt_sp = act_proj(f"L1_ffn_Wv{c}", "b", "spill", dst_sp=SP_VT)
        if E["ek"] is None:
            kt_sp = act_proj(f"L1_ffn_Wk{c}", "b", "spill", dst_sp=SP_KT)
        else:
            load_input(*get(E["ek"][0]))
            act_proj(f"L1_ffn_We{E['ek'][1]}", "a", "b", bias_e=E["ek"][1])
            kt_sp = act_proj(f"L1_ffn_Wk{c}", "b", "spill", dst_sp=SP_KT)
        if E["ev"] is not None:
            load_input(*get(E["ev"][0]))
            act_proj(f"L1_ffn_We{E['ev'][1]}", "a", "b", bias_e=E["ev"][1])
            vt_sp = act_proj(f"L1_ffn_Wv{c}", "b", "spill", dst_sp=SP_VT)
        node_attn(c, qt_sp, kt_sp, vt_sp)
        node_sp[c] = wo_combine(c, eq_sp)
        if c < 5:
            dbg_dump(st, f"n{c}_0", SP_N[c][0:128, :], node_sp[c])
    n5_at = lambda cc: ar.b(O_B + cc * 4096, 128, T)
    n5n_at = lambda cc: ar.b(O_A + cc * 4096, 128, T)
    k.waits("scalar", node_sp[5])
    nf_toks = emit_norm(st, n5_at, n5n_at, "ln", 1e-6,
                        tr_off=O_CS + 4096, bc_off=O_C,
                        gain_at=lambda cc: sv(st, "L1_ffn_ln_g", col=cc),
                        bias_at=lambda cc: sv(st, "L1_ffn_ln_b", col=cc),
                        src_toks=list(node_sp[5]) + [st.in_read] +
                        [t for t in st.stg_free if t])
    dbg_dump(st, "n5n_0", n5n_at(0), [nf_toks[0]])
    for cc in range(KC):
        k.waits("vector", [st.r_tok[cc], nf_toks[cc]] + st.r_readers[cc])
        st.r_readers[cc] = []
        st.r_tok[cc] = k.sig("vector", lambda e, cc=cc: e.tensor_tensor(
            out=st.R[:, cc * T:(cc + 1) * T],
            in0=st.R[:, cc * T:(cc + 1) * T],
            in1=n5n_at(cc), op=ALU.add))


def load_small(st):
    k = st.k
    dr = st.dr
    names = [
        ("L0_attn_norm_in_w", 128, KC),
        ("L0_attn_conv_w", 128, 4 * KC),
        ("L0_attn_conv_b", 128, KC),
        ("L0_attn_bau", 64, 1),
        ("L0_attn_bad", 128, 4),
        ("L0_attn_alpha_spike", 128, 4),
        ("L0_attn_bbeta", 8, 1),
        ("L0_attn_beta_spike", 128, 4),
        ("L0_attn_head_norm_w", 128, 4),
        ("L0_attn_bu1", 64, 1),
        ("L0_attn_bu2", 128, KC),
        ("L0_attn_ff_norm_w", 128, KC),
        ("L0_ffn_clip_x", 1, 1),
        ("L0_ffn_clip_down", 1, 1),
        ("L1_attn_b_qkv", 128, 24),
        ("L1_attn_b_fc", 128, KC),
        ("L1_ffn_ew", 18, 1),
        ("L1_ffn_act_w", 6, 1),
        ("L1_ffn_be", 128, 20 * KC),
        ("L1_ffn_ln_g", 128, KC),
        ("L1_ffn_ln_b", 128, KC),
    ]
    off = 0
    st.small = {}
    st.small_tok = {}
    for name, rows, cols in names:
        assert off + cols <= 4096
        ap = st.ROWS[0:rows, off:off + cols]
        st.small[name] = ap
        off += cols
        src = dr[name]
        if cols == 1:
            tok = k.dma("sync", ap, src[0:rows, 0:1])
        else:
            inner = src.shape[1]
            tok = k.dma("sync", ap.rearrange("p (c w) -> p c w", w=inner),
                        src.rearrange("(c p) w -> p c w", p=rows), slow=True)
        st.small_tok[name] = tok
    st.nr_sum = st.ROWS[0:1, off:off + TS]
    st.nr_ssq = st.ROWS[0:1, off + TS:off + 2 * TS]
    st.nr_a = st.ROWS[0:1, off + 2 * TS:off + 3 * TS]
    st.nr_t = st.ROWS[0:1, off + 3 * TS:off + 4 * TS]
    off += 4 * TS
    st.misc = st.ROWS[:, off:off + 512]
    off += 512
    st.ccrow = st.ROWS[0:1, off:off + 128]
    off += 128
    assert off <= 4096, off
    st.params_tok = list(st.small_tok.values())


def load_consts(st):
    k = st.k
    dr = st.dr
    C = st.con
    st.CONID = C.b(0, 128, 128)
    st.CONER = C.b(256, 1, 128)
    st.CONEC = C.b(512, 128, 1)
    st.CONBO8 = C.b(516, 128, 32)
    st.CONSEL = C.b(580, 8, 512)
    st.CONIS = C.b(1604, 128, 64)
    st.CONLT = C.f(1732, 128, 64)
    st.CONUT = C.f(1988, 128, 64)
    st.CONUI = C.f(2244, 128, 64)
    st.ZERO64 = C.f(2500, 128, 64)
    st.EPS5 = C.f(2756, 128, 1)
    st.EPS6 = C.f(2760, 128, 1)
    st.CONROPE = C.b(2764, 128, 1024)
    st.CONERF = C.f(4812, 1, 128)
    st.CONIDF = C.f(5324, 128, 128)
    st.CONER64 = C.bf[64:65, 5836 // 2:5836 // 2 + 128]
    toks = [
        k.dma("sync", st.CONID, dr["c_ident_bf"]),
        k.dma("sync", st.CONER, dr["c_onesrow_bf"]),
        k.dma("sync", st.CONEC, dr["c_onescol_bf"]),
        k.dma("sync", st.CONBO8, dr["c_bo8_bf"]),
        k.dma("sync", st.CONSEL, dr["c_headsel_bf"]),
        k.dma("sync", st.CONIS, dr["c_istack_bf"]),
        k.dma("sync", st.CONLT, dr["c_ltmask"]),
        k.dma("sync", st.CONUT, dr["c_utmask"]),
        k.dma("sync", st.CONUI, dr["c_uimask"]),
        k.dma("sync", st.CONROPE.rearrange("p (c o) -> p c o", o=128),
              dr["c_rope_bd_bf"].rearrange("(c p) o -> p c o", p=128)),
        k.dma("sync", st.CONERF, dr["c_onesrow_f"]),
        k.dma("sync", st.CONIDF, dr["c_ident_f"]),
        k.dma("sync", st.CONER64, dr["c_onesrow_bf"]),
    ]
    zt = k.sig("vector", lambda e: e.memset(st.ZERO64, 0.0))
    k.sig("vector", lambda e: e.memset(st.EPS5, 1e-5))
    zt = k.sig("vector", lambda e: e.memset(st.EPS6, 1e-6))
    st.const_tok = toks + [zt]
    for eng in ("tensor", "vector", "scalar"):
        k.waits(eng, toks)
        k.wait(eng, zt)


def emit_all(st):
    k = st.k
    load_consts(st)
    load_small(st)
    for eng in ("tensor", "vector", "scalar"):
        k.waits(eng, st.params_tok)
    for c in range(KC):
        st.r_tok[c] = k.dma("sync", st.R[:, c * T:(c + 1) * T],
                            st.dr["xT"][c * 128:(c + 1) * 128, :])
    emit_l0_attn(st)
    if st.debug_stage != 1:
        emit_l0_ffn(st)
        if st.debug_stage != 2:
            emit_l1_attn(st)
            if st.debug_stage != 3:
                emit_l1_ffn(st)
    toks = []
    for c in range(KC):
        k.waits("sync", [st.r_tok[c]] + st.r_readers[c])
        toks.append(k.dma("sync", st.out[c * 128:(c + 1) * 128, :],
                          st.R[:, c * T:(c + 1) * T]))
    k.waits("sync", toks)


def build_program(debug_stage=0):
    nc = bass.Bass("TRN2", target_bir_lowering=False, debug=False)
    dr = {}

    def din(name, shape, dtype=F32):
        dr[name] = nc.dram_tensor(name, list(shape), dtype,
                                  kind="ExternalInput").ap()

    din("xT", (D, T))
    din("L0_attn_norm_in_w", (D, 1))
    din("L0_attn_conv_w", (D, 4))
    din("L0_attn_conv_b", (D, 1))
    for w in ("Wq", "Wk", "Wv", "Wspike"):
        din("L0_attn_" + w, (D, 512))
    din("L0_attn_Wo", (512, D))
    din("L0_attn_Wau", (D, 64))
    din("L0_attn_bau", (64, 1))
    din("L0_attn_Wad", (64, 512))
    din("L0_attn_bad", (512, 1))
    din("L0_attn_alpha_spike", (512, 1))
    din("L0_attn_Wbeta", (D, 8))
    din("L0_attn_bbeta", (8, 1))
    din("L0_attn_beta_spike", (512, 1))
    din("L0_attn_head_norm_w", (512, 1))
    din("L0_attn_Wu1", (D, 64))
    din("L0_attn_bu1", (64, 1))
    din("L0_attn_Wu2", (64, D))
    din("L0_attn_bu2", (D, 1))
    din("L0_attn_ff_norm_w", (D, 1))
    din("L0_attn_Wff1", (D, 4096))
    din("L0_attn_Wff3", (D, 4096))
    din("L0_attn_Wff2", (4096, D))
    din("L0_ffn_clip_x", (1, 1))
    din("L0_ffn_clip_down", (1, 1))
    din("L0_ffn_w_gate", (2816, D))
    din("L0_ffn_w_up", (2816, D))
    din("L0_ffn_w_down", (D, 2816))
    din("L1_attn_w_qkv", (D, 3 * D))
    din("L1_attn_b_qkv", (3 * D, 1))
    din("L1_attn_w_fc", (D, D))
    din("L1_attn_b_fc", (D, 1))
    din("L1_ffn_ew", (18, 1))
    din("L1_ffn_act_w", (6, 1))
    din("L1_ffn_ew_row", (1, 18))
    din("L1_ffn_act_w_row", (1, 6))
    din("L1_ffn_be", (20 * D, 1))
    for e_ in range(20):
        din(f"L1_ffn_We{e_}", (D, D))
    for c_ in range(6):
        for w_ in ("Wq", "Wk", "Wv", "Wo"):
            din(f"L1_ffn_{w_}{c_}", (D, D))
    din("L1_ffn_ln_g", (D, 1))
    din("L1_ffn_ln_b", (D, 1))
    din("c_ident_bf", (128, 128), BF16)
    din("c_onesrow_bf", (1, 128), BF16)
    din("c_onescol_bf", (128, 1), BF16)
    din("c_headsel_bf", (8, 4 * 128), BF16)
    din("c_bo8_bf", (128, 32), BF16)
    din("c_ltmask", (128, CH))
    din("c_utmask", (128, CH))
    din("c_uimask", (128, CH))
    din("c_istack_bf", (128, CH), BF16)
    din("c_rope_bd_bf", (8 * 128, 128), BF16)
    din("c_onesrow_f", (1, 128))
    din("c_ident_f", (128, 128))

    st = St()
    st.nc = nc
    st.dr = dr
    st.out = nc.dram_tensor("out", [D, T], F32, kind="ExternalOutput").ap()
    st.spill_ln0 = nc.dram_tensor("spill_ln0", [D, T], BF16, kind="Internal").ap()
    st.spill_gate = nc.dram_tensor("spill_gate", [D, T], BF16, kind="Internal").ap()
    st.spill_r = nc.dram_tensor("spill_r", [D, T], F32, kind="Internal").ap()
    st.spill_y1 = nc.dram_tensor("spill_y1", [D, T], BF16, kind="Internal").ap()
    st.spill_h2 = nc.dram_tensor("spill_h2", [22 * 128, T], BF16,
                                 kind="Internal").ap()
    st.spill_qkv = nc.dram_tensor("spill_qkv", [3 * D, T], BF16,
                                  kind="Internal").ap()
    st.spill_ln3 = nc.dram_tensor("spill_ln3", [D, T], BF16,
                                  kind="Internal").ap()
    st.spill_node = [nc.dram_tensor(f"spill_node{i}", [D, T], BF16,
                                    kind="Internal").ap() for i in range(6)]
    st.spill_e = [nc.dram_tensor(f"spill_e{i}", [D, T], BF16,
                                 kind="Internal").ap() for i in range(3)]
    st.cc_in = nc.dram_tensor("cc_in", [1, 128], F32, kind="Internal")
    st.cc_out = nc.dram_tensor("cc_out", [1, 128], F32, kind="Internal")
    st.bscr = nc.dram_tensor("bscr", [1, 2], F32, kind="Internal").ap()
    st.rnd_i32 = nc.dram_tensor("rnd_i32", [128, T], I32, kind="Internal").ap()
    st.debug_stage = debug_stage

    es = contextlib.ExitStack()
    with es:
        R_t = es.enter_context(nc.sbuf_tensor("Rbuf", [128, KC * T], F32))
        AR_t = es.enter_context(nc.sbuf_tensor("Arena", [128, AR_BYTES // 2],
                                               BF16))
        ROWS_t = es.enter_context(nc.sbuf_tensor("Rows", [128, 4096], F32))
        CON_t = es.enter_context(nc.sbuf_tensor("Consts", [128, 4096], BF16))
        st.R = R_t.ap()
        st.ar = Arena(AR_t.ap(), AR_BYTES)
        st.ROWS = ROWS_t.ap()
        st.con = Arena(CON_t.ap(), 8192)
        pb = [es.enter_context(nc.psum_tensor(f"pb{i}", [128, 512], F32))
              for i in range(6)]
        pbf = [es.enter_context(nc.psum_tensor(f"pbf{i}", [128, 1024], BF16))
               for i in range(2)]
        st.pb = [Bank(p.ap()) for p in pb]
        st.pbf = [Bank(p.ap()) for p in pbf]
        sems = {n: es.enter_context(nc.semaphore(n))
                for n in ["dq", "dg", "da", "t", "v", "s", "g", "cc"]}
        st.sems = sems
        st.k = KB(nc, sems)
        st.r_tok = [None] * KC
        st.r_readers = [[] for _ in range(KC)]
        st.dbg_toks = []

        emit_all(st)

        k = st.k
        with nc.Block() as block:
            @block.sync
            def _(e):
                for f in k.ops["sync"]:
                    f(e)

            @block.gpsimd
            def _(e):
                for f in k.ops["gpsimd"]:
                    f(e)

            @block.scalar
            def _(e):
                for f in k.ops["scalar"]:
                    f(e)

            @block.vector
            def _(e):
                for f in k.ops["vector"]:
                    f(e)

            @block.tensor
            def _(e):
                for f in k.ops["tensor"]:
                    f(e)
    return nc


_CONSTS = None


def _prepare_weights(base):
    m = dict(base)

    def col(name):
        m[name] = np.ascontiguousarray(np.asarray(m[name],
                                                  dtype=np.float32).reshape(-1, 1))
    for name in ["L0_attn_norm_in_w", "L0_attn_conv_b", "L0_attn_bau",
                 "L0_attn_bad", "L0_attn_bbeta", "L0_attn_bu1", "L0_attn_bu2",
                 "L0_attn_ff_norm_w", "L1_attn_b_qkv", "L1_attn_b_fc",
                 "L1_ffn_ln_g", "L1_ffn_ln_b", "L1_ffn_act_w",
                 "L0_ffn_clip_x", "L0_ffn_clip_down", "L1_ffn_ew",
                 "L0_attn_alpha_spike", "L0_attn_beta_spike",
                 "L0_attn_head_norm_w", "L1_ffn_be"]:
        col(name)
    m["L0_attn_conv_w"] = np.ascontiguousarray(
        np.asarray(m["L0_attn_conv_w"], dtype=np.float32).reshape(D, 4))
    m["L1_ffn_ew_row"] = np.asarray(base["L1_ffn_ew"],
                                    np.float32).reshape(1, 18)
    m["L1_ffn_act_w_row"] = np.asarray(base["L1_ffn_act_w"],
                                       np.float32).reshape(1, 6)
    We3 = np.asarray(m.pop("L1_ffn_We"), np.float32).reshape(20, D, D)
    for e_ in range(20):
        m[f"L1_ffn_We{e_}"] = We3[e_]
    for w in ("Wq", "Wk", "Wv", "Wo"):
        W3 = np.asarray(m.pop("L1_ffn_" + w), np.float32).reshape(6, D, D)
        for c_ in range(6):
            m[f"L1_ffn_{w}{c_}"] = W3[c_]
    for name in list(m):
        m[name] = np.ascontiguousarray(np.asarray(m[name], dtype=np.float32))
    return m


def kernel(**inputs):
    global _CONSTS
    if _CONSTS is None:
        _CONSTS = _host_constants()
    x = np.asarray(inputs["L0_attn_x"])
    base = {n: v for n, v in inputs.items() if n != "L0_attn_x"}
    m = _prepare_weights(base)
    m.update(_CONSTS)
    in_maps = []
    for b in range(NCORES):
        d = dict(m)
        d["xT"] = np.ascontiguousarray(x[b].T)
        in_maps.append(d)
    nc = build_program(DEBUG_STAGE)
    res = run_bass_kernel_spmd(nc, in_maps, core_ids=list(range(NCORES)))
    outs = [np.asarray(res.results[b]["out"]).T for b in range(NCORES)]
    return np.ascontiguousarray(np.stack(outs, axis=0))
